# version 09
# baseline (speedup 1.0000x reference)
import numpy as np
import ml_dtypes
import concourse.bass as bass
import concourse.mybir as mybir
from concourse.bass_utils import run_bass_kernel_spmd

F32 = mybir.dt.float32
BF16 = mybir.dt.bfloat16
AF = mybir.ActivationFunctionType
ALU = mybir.AluOpType
AX = mybir.AxisListType
BF = ml_dtypes.bfloat16

D = 1024
T = 4096
NCH = D // 128
LN_EPS = 1e-5
GEPS = float(np.finfo(np.float32).eps)

ENGS = ("sync", "gpsimd", "tensor", "vector", "scalar")


class Sched:

    def __init__(self, paranoid=False):
        self.items = {e: [] for e in ENGS}
        self.cnt = {e: 0 for e in ENGS}
        self.extra = set()
        self.paranoid = paranoid
        self.pref = ""

    def _p(self, name):
        return name if name in ENGS else self.pref + name

    def now(self, *engs):
        if not engs:
            engs = ENGS
        return {self._p(e): self.cnt.get(self._p(e), 0) for e in engs}

    def op(self, eng, fn, after=None, inc=1, n=1, sem=None):
        sem = self._p(sem) if sem else eng
        if sem not in ENGS:
            self.extra.add(sem)
            self.cnt.setdefault(sem, 0)
        waits = dict(after or {})
        if self.paranoid:
            waits = {**{k: v for k, v in self.cnt.items() if k != eng or eng in ("sync", "gpsimd")}, **waits}
        self.items[eng].append((waits, fn, inc, n, sem))
        self.cnt[sem] += inc * (n if inc == 16 else 1)
        return self.cnt[sem]

    def dma(self, eng, fn, after=None, n=1, sem=None):
        return self.op(eng, fn, after=after, inc=16, n=n, sem=sem)

    def wait_only(self, eng, after):
        self.items[eng].append((dict(after), None, 0, 0, eng))

    def barrier_all(self):
        snap = dict(self.cnt)
        for e in ENGS:
            w = {k: v for k, v in snap.items() if v > 0}
            self.items[e].append((w, None, 0, 0, e))

    def emit(self, nc):
        names = list(ENGS) + sorted(self.extra)
        sems = {e: nc.alloc_semaphore("sem_" + e) for e in names}
        blk = nc.Block()
        blk.__enter__()

        def build(eng_name):
            self_sync = eng_name in ("vector", "scalar")
            def body(E):
                waited = {k: 0 for k in names}
                own = 0
                for waits, fn, inc, ndecl, semn in self.items[eng_name]:
                    if self_sync and fn is not None and own > 0:
                        if waited[eng_name] < own:
                            E.wait_ge(sems[eng_name], own)
                            waited[eng_name] = own
                    for k in names:
                        v = waits.get(k, 0)
                        if v <= waited[k]:
                            continue
                        if k == eng_name and eng_name not in ("sync", "gpsimd") \
                                and not self_sync:
                            continue
                        if k == eng_name and self_sync and v <= own:
                            continue
                        E.wait_ge(sems[k], v)
                        waited[k] = v
                    if fn is None:
                        continue
                    out = fn(E)
                    instrs = out if isinstance(out, (list, tuple)) else [out]
                    assert len(instrs) == ndecl, (eng_name, len(instrs), ndecl)
                    if inc == 16:
                        for i in instrs:
                            i.then_inc(sems[semn], inc)
                    else:
                        instrs[-1].then_inc(sems[semn], inc)
                        if semn == eng_name:
                            own += inc
            return body

        for e in ENGS:
            if self.items[e]:
                getattr(blk, e)(build(e))
        blk.__exit__(None, None, None)
        return sems




def _mk_banks(nc):
    ps = nc.alloc_psum_tensor
    return ([ps(f"bk{i}", [128, 512], F32) for i in range(6)],
            ps("bs1", [1, 512], F32), ps("bs2", [1, 512], F32))


def _mk_in(nc, pref, kind="ExternalInput"):
    def f(name, shape, dtype):
        return nc.dram_tensor(pref + name, shape, dtype, kind=kind).ap()
    return f

def gen_phase2(nc, S, sb, din, banks, x_in, x_out, Ttok):
    E, H = 8, 2048
    QT = min(1024, Ttok)
    NQ = Ttok // QT
    NTT = QT // 512
    NJ = NQ * E

    gwT = din("gwT", [D, E], F32)
    gb = din("gb", [1, E], F32)
    rmsw = din("rmsw", [1, E], F32)
    w1d = din("w1", [E, D, H], F32)
    b1d = din("b1", [E, 1, H], F32)
    w2d = din("w2", [E, H, D], F32)
    b2d = din("b2", [E, 1, D], F32)
    eyef = din("eyef", [128, 128], BF16)
    sel4 = din("sel4", [4, 512], BF16)
    s_xq = sb("s_xq", [128, NCH * QT], F32)
    s_xh = sb("s_xh", [128, NCH * QT], BF16)
    s_h = sb("s_h", [128, 16 * 512], BF16)
    s_xf = sb("s_xf", [128, NCH * 512], F32)
    s_w1 = [sb(f"s_w1_{i}", [128, NCH * H], BF16) for i in range(2)]
    s_w2 = sb("s_w2", [128, 16 * D], BF16)
    s_b1 = [sb(f"s_b1_{i}", [1, H], BF16) for i in range(2)]
    s_b2 = sb("s_b2", [1, D], BF16)
    s_gwT = sb("s_gwT", [128, NCH * E], BF16)
    s_gbr = sb("s_gbr", [1, E], BF16)
    s_rmsb = sb("s_rmsb", [128, E], F32)
    s_rmsr = sb("s_rmsr", [1, E], F32)
    s_rmsbr = sb("s_rmsbr", [1, E], BF16)
    s_ones = sb("s_ones", [128, 512], BF16)
    s_onesf = sb("s_onesf", [1, 128], F32)
    s_eyef = sb("s_eyef", [128, 128], BF16)
    s_sel4 = sb("s_sel4", [4, 512], BF16)
    s_sq = sb("s_sq", [128, 2 * 512], BF16)
    s_st = sb("s_st", [1, 4 * 512], F32)
    s_stb = sb("s_stb", [1, 2 * 512], BF16)
    s_g = sb("s_g", [128, 48], F32)
    s_ng4 = sb("s_ng4", [128, 4], BF16)
    s_ngT = sb("s_ngT", [4, 128], BF16)
    s_ngb = sb("s_ngb", [128, 2 * 512], F32)
    s_zt = sb("s_zt", [128, 1], F32)
    s_tmp = sb("s_tmp", [128, 512], F32)
    s_eps = sb("s_eps", [1, 1], F32)
    s_geps = sb("s_geps", [128, 1], F32)

    (bk6, pS1, pS2) = banks
    pH = [bk6[0], bk6[1]]
    pY = [bk6[2], bk6[3]]
    pBC = bk6[4]
    pG = bk6[5]

    def xap(dram, q):
        return dram.rearrange("(c p) t -> p c t", p=128)[:, :, q * QT:(q + 1) * QT]

    xq3 = s_xq[:].rearrange("p (c t) -> p c t", t=QT)
    xh3 = s_xh[:].rearrange("p (c t) -> p c t", t=QT)
    xf3 = s_xf[:].rearrange("p (c t) -> p c t", t=512)
    h3 = s_h[:].rearrange("p (c t) -> p c t", t=512)
    w1v = [s_w1[i][:].rearrange("p (c h) -> p c h", h=H) for i in range(2)]
    w2v = s_w2[:].rearrange("p (c d) -> p c d", d=D)
    gw3 = s_gwT[:].rearrange("p (c e) -> p c e", e=E)
    ey = s_eyef[:]

    S.dma("sync", lambda E_: [
        E_.dma_start(s_rmsr[:], rmsw),
        E_.dma_start(s_eyef[:], eyef),
        E_.dma_start(s_sel4[:], sel4),
    ], n=3, sem="cst")
    S.dma("gpsimd", lambda G: [
        G.dma_start(s_gwT[:].rearrange("p (c e) -> p c e", e=E),
                    gwT.rearrange("(c p) e -> p c e", p=128)),
        G.dma_start(s_gbr[:], gb),
    ], n=2, sem="gwc")
    wait_gw = S.now("gwc")
    S.op("vector", lambda V: [
        V.memset(s_ones[:], 1.0),
        V.memset(s_onesf[:], 1.0),
        V.memset(s_eps[:], LN_EPS),
        V.memset(s_geps[:], GEPS),
        V.tensor_copy(s_rmsbr[:], s_rmsr[:]),
    ], after=S.now("cst"), n=5)
    S.op("tensor", lambda Tn: Tn.matmul(
        pG[:, 0:E], s_ones[:][0:1, 0:128], s_rmsbr[:], start=True, stop=True),
        after=S.now("cst", "vector"))
    S.op("vector", lambda V: V.tensor_copy(s_rmsb[:], pG[:, 0:E]),
         after=S.now("tensor"))
    rmsb_done = S.now("vector")

    pe_after_h = {}
    pe_after_y = {}
    m_w1 = {}
    m_w2 = {}

    def load_w1(j):
        q, e = divmod(j, E)
        b = e % 2
        aft = {}
        if j >= 2:
            aft["tensor"] = pe_after_h[j - 2]
        w1r = w1d[e].rearrange("(c p) h -> p c h", p=128)
        s1 = w1v[b]
        S.dma("gpsimd", lambda G: [
            G.dma_start(s1[:, :, 0:1024], w1r[:, :, 0:1024]),
            G.dma_start(s1[:, :, 1024:2048], w1r[:, :, 1024:2048]),
            G.dma_start(s_b1[b][:], b1d[e]),
        ], after=aft, n=3, sem="w1_%d" % b)
        m_w1[j] = {S._p("w1_%d" % b): S.cnt[S._p("w1_%d" % b)]}

    def load_w2(j):
        q, e = divmod(j, E)
        aft = {}
        if j >= 1:
            aft["tensor"] = pe_after_y[j - 1]
        w2r = w2d[e].rearrange("(c p) d -> p c d", p=128)
        S.dma("gpsimd", lambda G: [
            G.dma_start(w2v[:, 0:8, :], w2r[:, 0:8, :]),
            G.dma_start(s_b2[:], b2d[e]),
            G.dma_start(w2v[:, 8:16, :], w2r[:, 8:16, :]),
        ], after=aft, n=3, sem="w2s")
        m_w2[j] = {S._p("w2s"): S.cnt[S._p("w2s")]}

    load_w1(0)
    load_w2(0)
    load_w1(1)

    vec_xhat = {}
    vec_ngb = {}
    prev_vec_smalls = 0
    mm_mark = {0: 0, 1: 0}
    gelu_mark = {0: 0, 1: 0}
    yev_mark = {0: 0, 1: 0}

    for q in range(NQ):
        S.dma("sync", lambda E_, q=q: E_.dma_start(xq3, xap(x_in, q)),
              after=S.now("vector", "sxo"), sem="sx")
        m_x = S.now("sx")

        for tt in range(NTT):
            tsl = slice(tt * 512, tt * 512 + 512)
            for c in range(NCH):
                S.op("scalar", lambda Sc, c=c, tsl=tsl: [
                    Sc.activation(xh3[:, c, tsl], xq3[:, c, tsl], AF.Copy),
                    Sc.activation(s_sq[:][:, (c % 2) * 512:(c % 2) * 512 + 512],
                                  xq3[:, c, tsl], AF.Square),
                ], after={"tensor": mm_mark[c % 2], **m_x}, n=2)
                S.op("tensor", lambda Tn, c=c, tsl=tsl: [
                    Tn.matmul(pS1[:], s_ones[:][:, 0:1], xh3[:, c, tsl],
                              start=(c == 0), stop=(c == NCH - 1)),
                    Tn.matmul(pS2[:], s_ones[:][:, 0:1],
                              s_sq[:][:, (c % 2) * 512:(c % 2) * 512 + 512],
                              start=(c == 0), stop=(c == NCH - 1)),
                ], after={**S.now("scalar"), "vector": prev_vec_smalls}, n=2)
                mm_mark[c % 2] = S.cnt["tensor"]
            mu = s_st[:][:, 0:512]
            ex2 = s_st[:][:, 512:1024]
            var = s_st[:][:, 1024:1536]
            mub = s_stb[:][:, 0:512]
            rsb = s_stb[:][:, 512:1024]
            m2 = s_st[:][:, 1536:2048]
            S.op("vector", lambda V: [
                V.tensor_scalar_mul(mu, pS1[:], 1.0 / D),
                V.tensor_scalar_mul(ex2, pS2[:], 1.0 / D),
            ], after=S.now("tensor"), n=2)
            S.op("vector", lambda V: [
                V.tensor_mul(m2, mu, mu),
                V.tensor_copy(mub, mu),
            ], n=2)
            S.op("vector", lambda V: V.tensor_sub(var, ex2, m2))
            prev_vec_smalls = S.cnt["vector"]
            S.op("scalar", lambda Sc: Sc.activation(var, var, AF.Sqrt, bias=s_eps[:]),
                 after=S.now("vector"))
            S.op("vector", lambda V: V.reciprocal(var, var),
                 after=S.now("scalar"))
            S.op("vector", lambda V: V.tensor_copy(rsb, var))
            S.op("tensor", lambda Tn: [
                Tn.matmul(pBC[:], s_ones[:][0:1, 0:128], mub, start=True, stop=True),
                Tn.matmul(pG[:], s_ones[:][0:1, 0:128], rsb, start=True, stop=True),
            ], after=S.now("vector"), n=2)
            for c in range(NCH):
                S.op("vector", lambda V, c=c, tsl=tsl:
                     V.tensor_sub(s_tmp[:], xq3[:, c, tsl], pBC[:]),
                     after=S.now("tensor"))
                S.op("vector", lambda V, c=c, tsl=tsl:
                     V.tensor_tensor(xh3[:, c, tsl], s_tmp[:], pG[:], ALU.mult))
            vec_xhat[(q, tt)] = S.cnt["vector"]

            sqo = s_g[:][:, 0:8]
            ms = s_g[:][:, 8:9]
            rstg = s_g[:][:, 9:10]
            lg = s_g[:][:, 16:24]
            mx = s_g[:][:, 24:25]
            nmx = s_g[:][:, 25:26]
            zs = s_g[:][:, 26:27]
            zz = s_g[:][:, 27:28]
            exo = s_g[:][:, 32:40]
            for tg in range(4):
                gsl = slice(tt * 512 + tg * 128, tt * 512 + tg * 128 + 128)
                S.op("tensor", lambda Tn, gsl=gsl: [
                    Tn.matmul(pG[:, 0:E], xh3[:, c2, gsl], gw3[:, c2, :],
                              start=(c2 == 0), stop=False)
                    for c2 in range(NCH)] + [
                    Tn.matmul(pG[:, 0:E], s_ones[:][0:1, 0:128], s_gbr[:],
                              start=False, stop=True)],
                    after={**S.now("vector", "scalar"), **wait_gw}, n=NCH + 1)
                S.op("scalar", lambda Sc: Sc.activation(
                    sqo, pG[:, 0:E], AF.Square, accum_out=ms),
                    after=S.now("tensor"))
                S.op("scalar", lambda Sc: Sc.activation(
                    rstg, ms, AF.Sqrt, bias=s_geps[:], scale=1.0 / E))
                S.op("vector", lambda V: V.reciprocal(rstg, rstg),
                     after={**S.now("scalar"), **rmsb_done})
                S.op("vector", lambda V: V.scalar_tensor_tensor(
                    lg, pG[:, 0:E], rstg, s_rmsb[:], ALU.mult, ALU.mult))
                S.op("vector", lambda V: V.reduce_max(mx, lg, AX.X))
                S.op("vector", lambda V: V.tensor_scalar_mul(nmx, mx, -2.0))
                S.op("scalar", lambda Sc: Sc.activation(
                    exo, lg, AF.Exp, bias=nmx, scale=2.0, accum_out=zs),
                    after=S.now("vector"))
                S.op("vector", lambda V, tg=tg: [
                    V.tensor_scalar(zz, zs, 1e-6, 1.0, ALU.mult, ALU.add),
                    V.tensor_scalar_mul(s_zt[:], zs, 1e-6),
                ], after=S.now("scalar"), n=2)
                S.op("vector", lambda V: V.reciprocal(zz, zz))
                S.op("vector", lambda V, tg=tg: V.tensor_mul(
                    s_ng4[:][:, tg:tg + 1], s_zt[:], zz))
            S.op("tensor", lambda Tn: Tn.matmul(
                pBC[0:4, 0:128], s_ng4[:], ey, start=True, stop=True),
                after=S.now("vector"))
            S.op("vector", lambda V: V.tensor_copy(s_ngT[:], pBC[0:4, 0:128]),
                 after=S.now("tensor"))
            for tg in range(4):
                S.op("tensor", lambda Tn, tg=tg: Tn.matmul(
                    pBC[:, 0:128], s_sel4[:][:, tg * 128:tg * 128 + 128],
                    s_ngT[:], start=True, stop=True), after=S.now("vector"))
                S.op("vector", lambda V, tt=tt, tg=tg: V.tensor_copy(
                    s_ngb[:][:, tt * 512 + tg * 128:tt * 512 + tg * 128 + 128],
                    pBC[:, 0:128]), after=S.now("tensor"))
            vec_ngb[(q, tt)] = S.cnt["vector"]

        for e in range(E):
            j = q * E + e
            b = e % 2
            for tt in range(NTT):
                tsl = slice(tt * 512, tt * 512 + 512)
                for hc in range(16):
                    bank = pH[hc % 2]
                    aft = {"scalar": gelu_mark[hc % 2], **m_w1[j],
                           "vector": vec_xhat[(q, tt)]}
                    S.op("tensor", lambda Tn, hc=hc, bank=bank, tsl=tsl, b=b: [
                        Tn.matmul(bank[:], w1v[b][:, c2, hc * 128:hc * 128 + 128],
                                  xh3[:, c2, tsl], start=(c2 == 0), stop=False)
                        for c2 in range(NCH)] + [
                        Tn.matmul(bank[:], s_b1[b][:][:, hc * 128:hc * 128 + 128],
                                  s_ones[:][0:1, 0:512], start=False, stop=True)],
                        after=aft, n=NCH + 1)
                    S.op("scalar", lambda Sc, hc=hc, bank=bank: Sc.activation(
                        h3[:, hc, :], bank[:], AF.Gelu), after=S.now("tensor"))
                    gelu_mark[hc % 2] = S.cnt["scalar"]
                gelu_done = S.cnt["scalar"]
                pe_after_h[j] = S.cnt["tensor"]
                for dc in range(NCH):
                    bank = pY[dc % 2]
                    aft = {"vector": yev_mark[dc % 2], "scalar": gelu_done,
                           **m_w2[j]}
                    S.op("tensor", lambda Tn, dc=dc, bank=bank: [
                        Tn.matmul(bank[:], w2v[:, hc2, dc * 128:dc * 128 + 128],
                                  h3[:, hc2, :], start=(hc2 == 0), stop=False)
                        for hc2 in range(16)] + [
                        Tn.matmul(bank[:], s_b2[:][:, dc * 128:dc * 128 + 128],
                                  s_ones[:][0:1, 0:512], start=False, stop=True)],
                        after=aft, n=17)
                    if e == 0:
                        S.op("vector", lambda V, dc=dc, tsl=tsl, bank=bank:
                             V.tensor_copy(xq3[:, dc, tsl], bank[:]),
                             after=S.now("tensor"))
                    else:
                        S.op("vector", lambda V, dc=dc, tsl=tsl, bank=bank:
                             V.tensor_add(xq3[:, dc, tsl], xq3[:, dc, tsl],
                                          bank[:]),
                             after=S.now("tensor"))
                    yev_mark[dc % 2] = S.cnt["vector"]
            pe_after_y[j] = S.cnt["tensor"]
            if j + 1 < NJ:
                load_w2(j + 1)
            if j + 2 < NJ:
                load_w1(j + 2)

        for tt in range(NTT):
            tsl = slice(tt * 512, tt * 512 + 512)
            S.dma("sync", lambda E_, q=q, tsl=tsl: E_.dma_start(
                xf3, xap(x_in, q)[:, :, tsl]),
                after=S.now("vector"), sem="sxf")
            m_xf = S.now("sxf")
            for dc in range(NCH):
                S.op("vector", lambda V, dc=dc, tt=tt, tsl=tsl:
                     V.tensor_tensor(s_tmp[:], xq3[:, dc, tsl],
                                     s_ngb[:][:, tt * 512:tt * 512 + 512],
                                     ALU.mult),
                     after={**S.now("tensor"), **m_xf})
                S.op("vector", lambda V, dc=dc, tsl=tsl:
                     V.tensor_sub(s_tmp[:], xq3[:, dc, tsl], s_tmp[:]))
                S.op("vector", lambda V, dc=dc, tsl=tsl:
                     V.tensor_add(xq3[:, dc, tsl], s_tmp[:], xf3[:, dc, :]))
        S.dma("sync", lambda E_, q=q: E_.dma_start(xap(x_out, q), xq3),
              after=S.now("vector"), sem="sxo")

    S.wait_only("sync", S.now("sxo"))


def build_phase2(Ttok=T, paranoid=False):
    nc = bass.Bass("TRN2", target_bir_lowering=False, debug=False)
    S = Sched(paranoid)
    banks = _mk_banks(nc)
    x_in = nc.dram_tensor("x_in", [D, Ttok], F32, kind="ExternalInput").ap()
    x_out = nc.dram_tensor("x_out", [D, Ttok], F32, kind="ExternalOutput").ap()
    gen_phase2(nc, S, nc.alloc_sbuf_tensor, _mk_in(nc, ""), banks,
               x_in, x_out, Ttok)
    S.emit(nc)
    return nc


def prep_phase2_inputs(xcore, inp, fm=False):
    sel4 = np.zeros((4, 512), np.float32)
    for r in range(4):
        sel4[r, 128 * r:128 * r + 128] = 1.0
    return {
        "x_in": np.ascontiguousarray(np.asarray(xcore, np.float32)) if fm
        else np.ascontiguousarray(np.asarray(xcore, np.float32).T),
        "gwT": np.ascontiguousarray(np.asarray(inp["L0_ffn_gate_w"], np.float32).T),
        "gb": np.ascontiguousarray(np.asarray(inp["L0_ffn_gate_b"], np.float32)[None, :]),
        "rmsw": np.ascontiguousarray(np.asarray(inp["L0_ffn_rms_w"], np.float32)[None, :]),
        "w1": np.ascontiguousarray(np.asarray(inp["L0_ffn_w1"], np.float32)),
        "b1": np.ascontiguousarray(np.asarray(inp["L0_ffn_b1"], np.float32)[:, None, :]),
        "w2": np.ascontiguousarray(np.asarray(inp["L0_ffn_w2"], np.float32)),
        "b2": np.ascontiguousarray(np.asarray(inp["L0_ffn_b2"], np.float32)[:, None, :]),
        "eyef": np.eye(128, dtype=np.float32).astype(BF),
        "sel4": sel4.astype(BF),
    }


def gen_phase1(nc, S, sb, din, banks, x_in, x_out, Ttok):
    NT = Ttok // 512
    wkT = din("wkT", [D, D], F32)
    wvT = din("wvT", [D, D], F32)
    woT = din("woT", [D, D], F32)
    bkр = din("bk", [1, D], F32)
    bvр = din("bv", [1, D], F32)
    boр = din("bo", [1, D], F32)
    lngd = din("lng", [128, NCH], F32)
    lnbd = din("lnb", [128, NCH], F32)
    epspos = din("epspos", [1, Ttok], F32)
    s_x = sb("s_x", [128, NCH * 512], F32)
    s_xh = sb("s_xh", [128, NCH * 512], BF16)
    s_k = sb("s_k", [128, NCH * 512], BF16)
    s_v = sb("s_v", [128, NCH * 512], BF16)
    s_bnd = sb("s_bnd", [128, NCH * 512], BF16)
    s_mem = sb("s_mem", [128, NCH * 512], BF16)
    s_rs = sb("s_rs", [128, 2 * 512], F32)
    s_wk = sb("s_wk", [128, NCH * D], BF16)
    s_wv = sb("s_wv", [128, NCH * D], BF16)
    s_wo = sb("s_wo", [128, NCH * D], BF16)
    s_bk = sb("s_bk", [1, D], BF16)
    s_bv = sb("s_bv", [1, D], BF16)
    s_bo = sb("s_bo", [1, D], BF16)
    s_lng = sb("s_lng", [128, NCH], F32)
    s_lnb = sb("s_lnb", [128, NCH], F32)
    s_eps = sb("s_epsP", [1, 1], F32)
    s_epsp = sb("s_epsp", [1, Ttok], F32)
    s_ones = sb("s_ones", [128, 512], BF16)
    s_onesf = sb("s_onesf", [128, 512], F32)
    s_carry = sb("s_carry", [128, NCH], F32)
    s_sq = sb("s_sq", [128, 2 * 512], BF16)
    s_st = sb("s_st", [1, 4 * 512], F32)
    s_stb = sb("s_stb", [1, 2 * 512], BF16)
    s_t1 = sb("s_t1", [128, 512], F32)
    s_t2 = sb("s_t2", [128, 512], F32)

    (bk6, pS1, pS2) = banks
    pA, pB, pBC, pG = bk6[0], bk6[1], bk6[4], bk6[5]

    def xap(dram, t):
        return dram.rearrange("(c p) n -> p c n", p=128)[:, :, t * 512:(t + 1) * 512]

    x3 = s_x[:].rearrange("p (c n) -> p c n", n=512)
    xh3 = s_xh[:].rearrange("p (c n) -> p c n", n=512)
    k3 = s_k[:].rearrange("p (c n) -> p c n", n=512)
    v3 = s_v[:].rearrange("p (c n) -> p c n", n=512)
    b3 = s_bnd[:].rearrange("p (c n) -> p c n", n=512)
    m3 = s_mem[:].rearrange("p (c n) -> p c n", n=512)
    rs2 = s_rs[:].rearrange("p (g n) -> p g n", n=512)
    wk3 = s_wk[:].rearrange("p (c j) -> p c j", j=D)
    wv3 = s_wv[:].rearrange("p (c j) -> p c j", j=D)
    wo3 = s_wo[:].rearrange("p (c j) -> p c j", j=D)

    S.dma("sync", lambda E_: [
        E_.dma_start(s_epsp[:], epspos),
        E_.dma_start(s_lng[:], lngd),
        E_.dma_start(s_lnb[:], lnbd),
    ], n=3, sem="cst")
    S.dma("gpsimd", lambda G: [
        G.dma_start(s_wk[:].rearrange("p (c j) -> p c j", j=D),
                    wkT.rearrange("(c p) j -> p c j", p=128)),
        G.dma_start(s_wv[:].rearrange("p (c j) -> p c j", j=D),
                    wvT.rearrange("(c p) j -> p c j", p=128)),
        G.dma_start(s_wo[:].rearrange("p (c j) -> p c j", j=D),
                    woT.rearrange("(c p) j -> p c j", p=128)),
        G.dma_start(s_bk[:], bkр),
        G.dma_start(s_bv[:], bvр),
        G.dma_start(s_bo[:], boр),
    ], n=6, sem="wts")
    m_wts = S.now("wts")
    S.op("vector", lambda V: [
        V.memset(s_ones[:], 1.0),
        V.memset(s_onesf[:], 1.0),
        V.memset(s_eps[:], LN_EPS),
        V.memset(s_carry[:], 0.0),
    ], n=4)
    init_v = S.now("vector")

    mm_mark = {0: 0, 1: 0}
    evac_mark = {0: 0, 1: 0}
    oproj_mark = {0: 0, 1: 0}

    def ln_block(tsl_ap, dst3, epsrow, lng=None, lnb=None, src3=None):
        mu = s_st[:][:, 0:512]
        ex2 = s_st[:][:, 512:1024]
        var = s_st[:][:, 1024:1536]
        m2 = s_st[:][:, 1536:2048]
        mub = s_stb[:][:, 0:512]
        rsb = s_stb[:][:, 512:1024]
        v_src = S.now("vector")
        for c in range(NCH):
            S.op("scalar", lambda Sc, c=c: Sc.activation(
                s_sq[:][:, (c % 2) * 512:(c % 2) * 512 + 512],
                src3[:, c, :], AF.Square),
                after={"tensor": mm_mark[c % 2], **v_src})
            S.op("tensor", lambda Tn, c=c: [
                Tn.matmul(pS1[:], s_ones[:][:, 0:1], src3[:, c, :],
                          start=(c == 0), stop=(c == NCH - 1)),
                Tn.matmul(pS2[:], s_ones[:][:, 0:1],
                          s_sq[:][:, (c % 2) * 512:(c % 2) * 512 + 512],
                          start=(c == 0), stop=(c == NCH - 1)),
            ], after=S.now("scalar"), n=2)
            mm_mark[c % 2] = S.cnt["tensor"]
        S.op("vector", lambda V: [
            V.tensor_scalar_mul(mu, pS1[:], 1.0 / D),
            V.tensor_scalar_mul(ex2, pS2[:], 1.0 / D),
        ], after=S.now("tensor"), n=2)
        S.op("vector", lambda V: [
            V.tensor_mul(m2, mu, mu),
            V.tensor_copy(mub, mu),
        ], n=2)
        S.op("vector", lambda V: V.tensor_sub(var, ex2, m2))
        S.op("vector", lambda V: V.tensor_add(var, var, epsrow),
             after=S.now("cst"))
        S.op("scalar", lambda Sc: Sc.activation(var, var, AF.Sqrt),
             after=S.now("vector"))
        S.op("vector", lambda V: V.reciprocal(var, var), after=S.now("scalar"))
        S.op("vector", lambda V: V.tensor_copy(rsb, var))
        S.op("tensor", lambda Tn: [
            Tn.matmul(pBC[:], s_ones[:][0:1, 0:128], mub, start=True, stop=True),
            Tn.matmul(pG[:], s_ones[:][0:1, 0:128], rsb, start=True, stop=True),
        ], after=S.now("vector"), n=2)
        for c in range(NCH):
            S.op("vector", lambda V, c=c: V.tensor_tensor(
                s_t1[:], src3[:, c, :] if tsl_ap is None else tsl_ap[c],
                pBC[:], ALU.subtract), after=S.now("tensor"))
            if lng is None:
                S.op("vector", lambda V, c=c: V.tensor_tensor(
                    dst3[:, c, :], s_t1[:], pG[:], ALU.mult))
            else:
                S.op("vector", lambda V, c=c: V.tensor_tensor(
                    s_t2[:], s_t1[:], pG[:], ALU.mult))
                S.op("vector", lambda V, c=c: V.tensor_scalar(
                    dst3[:, c, :], s_t2[:], lng[:, c:c + 1], lnb[:, c:c + 1],
                    ALU.mult, ALU.add), after=S.now("cst"))

    QM_BOUND = [
        (0, [(0, 0, 1), (1, 1, -1), (2, 2, -1), (3, 3, -1)]),
        (1, [(0, 1, 1), (1, 0, 1), (2, 3, 1), (3, 2, -1)]),
        (2, [(0, 2, 1), (1, 3, -1), (2, 0, 1), (3, 1, 1)]),
        (3, [(0, 3, 1), (1, 2, 1), (2, 1, -1), (3, 0, 1)]),
    ]
    QM_RETR = [
        (0, [(0, 0, 1), (1, 1, 1), (2, 2, 1), (3, 3, 1)]),
        (1, [(0, 1, -1), (1, 0, 1), (2, 3, -1), (3, 2, 1)]),
        (2, [(0, 2, -1), (1, 3, 1), (2, 0, 1), (3, 1, -1)]),
        (3, [(0, 3, -1), (1, 2, -1), (2, 1, 1), (3, 0, 1)]),
    ]

    def quat_mul_emit(table, a3, bb3, out3, g):
        for oc_, terms in table:
            ot = out3[:, 2 * oc_ + g, :]
            terms = sorted(terms, key=lambda t_: -t_[2])
            first = True
            for (ac, bc, sgn) in terms:
                at = a3[:, 2 * ac + g, :]
                bt = bb3[:, 2 * bc + g, :]
                if first:
                    assert sgn == 1
                    S.op("vector", lambda V, ot=ot, at=at, bt=bt:
                         V.tensor_mul(ot, at, bt))
                    first = False
                else:
                    S.op("vector", lambda V, at=at, bt=bt:
                         V.tensor_mul(s_t1[:], at, bt))
                    op = ALU.add if sgn == 1 else ALU.subtract
                    S.op("vector", lambda V, ot=ot, op=op: V.tensor_tensor(
                        ot, ot, s_t1[:] if op == ALU.add else s_t1[:], op))

    for t in range(NT):
        S.dma("sync", lambda E_, t=t: E_.dma_start(x3, xap(x_in, t)),
              after=S.now("vector", "sxo"), sem="sx")
        m_x = S.now("sx")
        for c in range(NCH):
            S.op("scalar", lambda Sc, c=c: Sc.activation(
                xh3[:, c, :], x3[:, c, :], AF.Copy),
                after={**m_x, **S.now("vector")})
        cpy_done = S.now("scalar")
        S.wait_only("vector", cpy_done)
        ln_block(None, xh3, s_eps[:].broadcast_to([1, 512]), src3=xh3)
        xh_done = S.now("vector")

        for (wsrc, bsrc, dst3) in ((wk3, s_bk, k3), (wv3, s_bv, v3)):
            for oc in range(NCH):
                bank = pA if (oc % 2 == 0) else pB
                S.op("tensor", lambda Tn, oc=oc, bank=bank, wsrc=wsrc, bsrc=bsrc: [
                    Tn.matmul(bank[:], wsrc[:, c2, oc * 128:oc * 128 + 128],
                              xh3[:, c2, :], start=(c2 == 0), stop=False)
                    for c2 in range(NCH)] + [
                    Tn.matmul(bank[:], bsrc[:][:, oc * 128:oc * 128 + 128],
                              s_ones[:][0:1, 0:512], start=False, stop=True)],
                    after={**m_wts,
                           "vector": max(xh_done["vector"], oproj_mark[oc % 2]),
                           "scalar": evac_mark[oc % 2]}, n=NCH + 1)
                S.op("scalar", lambda Sc, oc=oc, bank=bank, dst3=dst3:
                     Sc.activation(dst3[:, oc, :], bank[:], AF.Copy),
                     after=S.now("tensor"))
                evac_mark[oc % 2] = S.cnt["scalar"]
        kv_done = S.now("scalar")

        for g in range(2):
            rs = rs2[:, g, :]
            S.op("scalar", lambda Sc, g=g: Sc.activation(
                s_t1[:], k3[:, 0 * 2 + g, :], AF.Square),
                after={**kv_done, **S.now("vector")})
            S.op("scalar", lambda Sc, g=g: Sc.activation(
                s_t2[:], k3[:, 1 * 2 + g, :], AF.Square))
            S.op("vector", lambda V: V.tensor_add(s_t1[:], s_t1[:], s_t2[:]),
                 after=S.now("scalar"))
            S.op("scalar", lambda Sc, g=g: Sc.activation(
                s_t2[:], k3[:, 2 * 2 + g, :], AF.Square), after=S.now("vector"))
            S.op("vector", lambda V: V.tensor_add(s_t1[:], s_t1[:], s_t2[:]),
                 after=S.now("scalar"))
            S.op("scalar", lambda Sc, g=g: Sc.activation(
                s_t2[:], k3[:, 3 * 2 + g, :], AF.Square), after=S.now("vector"))
            S.op("vector", lambda V: V.tensor_add(s_t1[:], s_t1[:], s_t2[:]),
                 after=S.now("scalar"))
            S.op("scalar", lambda Sc: Sc.activation(s_t1[:], s_t1[:], AF.Sqrt),
                 after=S.now("vector"))
            S.op("vector", lambda V: V.tensor_scalar_max(s_t1[:], s_t1[:], 1e-12),
                 after=S.now("scalar"))
            S.op("vector", lambda V, rs=rs: V.reciprocal(rs, s_t1[:]))
            for c in range(4):
                S.op("vector", lambda V, c=c, g=g, rs=rs: V.tensor_tensor(
                    k3[:, 2 * c + g, :], k3[:, 2 * c + g, :], rs, ALU.mult))
        for g in range(2):
            quat_mul_emit(QM_BOUND, v3, k3, b3, g)
        for oc in range(NCH):
            if t == 0:
                S.op("vector", lambda V, oc=oc: V.tensor_tensor_scan(
                    m3[:, oc, :], s_onesf[:], b3[:, oc, :], 0.0,
                    ALU.mult, ALU.add))
            else:
                S.op("vector", lambda V, oc=oc: V.tensor_tensor_scan(
                    m3[:, oc, :], s_onesf[:], b3[:, oc, :],
                    s_carry[:][:, oc:oc + 1], ALU.mult, ALU.add))
            S.op("vector", lambda V, oc=oc: V.tensor_copy(
                s_carry[:][:, oc:oc + 1], m3[:, oc, 511:512]))
        for g in range(2):
            quat_mul_emit(QM_RETR, m3, k3, v3, g)
        tsl = slice(t * 512, t * 512 + 512)
        ln_block(None, b3, s_epsp[:][:, tsl], lng=s_lng[:], lnb=s_lnb[:],
                 src3=v3)
        nrm_done = S.now("vector")
        for dc in range(NCH):
            bank = pA if (dc % 2 == 0) else pB
            S.op("tensor", lambda Tn, dc=dc, bank=bank: [
                Tn.matmul(bank[:], wo3[:, c2, dc * 128:dc * 128 + 128],
                          b3[:, c2, :], start=(c2 == 0), stop=False)
                for c2 in range(NCH)] + [
                Tn.matmul(bank[:], s_bo[:][:, dc * 128:dc * 128 + 128],
                          s_ones[:][0:1, 0:512], start=False, stop=True)],
                after={"vector": max(nrm_done["vector"], oproj_mark[dc % 2])},
                n=NCH + 1)
            S.op("vector", lambda V, dc=dc, bank=bank: V.tensor_add(
                x3[:, dc, :], x3[:, dc, :], bank[:]), after=S.now("tensor"))
            S.op("vector", lambda V, dc=dc: V.tensor_add(
                x3[:, dc, :], x3[:, dc, :], xh3[:, dc, :]))
            oproj_mark[dc % 2] = S.cnt["vector"]
        S.dma("sync", lambda E_, t=t: E_.dma_start(xap(x_out, t), x3),
              after=S.now("vector"), sem="sxo")

    S.wait_only("sync", S.now("sxo"))


def build_phase1(Ttok=T):
    nc = bass.Bass("TRN2", target_bir_lowering=False, debug=False)
    S = Sched()
    banks = _mk_banks(nc)
    x_in = nc.dram_tensor("x_in", [D, Ttok], F32, kind="ExternalInput").ap()
    x_out = nc.dram_tensor("x_out", [D, Ttok], F32, kind="ExternalOutput").ap()
    gen_phase1(nc, S, nc.alloc_sbuf_tensor, _mk_in(nc, ""), banks,
               x_in, x_out, Ttok)
    S.emit(nc)
    return nc


def prep_phase1_inputs(xcore, inp, fm=False):
    perm = np.array([4 * ((oc % 2) * 128 + p) + oc // 2
                     for oc in range(8) for p in range(128)])
    Wk = np.asarray(inp["L0_attn_Wk"], np.float32)
    Wv = np.asarray(inp["L0_attn_Wv"], np.float32)
    Wo = np.asarray(inp["L0_attn_Wo"], np.float32)
    Tt = xcore.shape[1] if fm else xcore.shape[0]
    return {
        "x_in": np.ascontiguousarray(np.asarray(xcore, np.float32)) if fm
        else np.ascontiguousarray(np.asarray(xcore, np.float32).T),
        "wkT": np.ascontiguousarray(Wk[perm, :].T),
        "wvT": np.ascontiguousarray(Wv[perm, :].T),
        "woT": np.ascontiguousarray(Wo[:, perm].T),
        "bk": np.ascontiguousarray(np.asarray(inp["L0_attn_bk"], np.float32)[perm][None, :]),
        "bv": np.ascontiguousarray(np.asarray(inp["L0_attn_bv"], np.float32)[perm][None, :]),
        "bo": np.ascontiguousarray(np.asarray(inp["L0_attn_bo"], np.float32)[None, :]),
        "lng": np.ascontiguousarray(np.asarray(inp["L0_attn_ln_g"], np.float32)[perm].reshape(8, 128).T),
        "lnb": np.ascontiguousarray(np.asarray(inp["L0_attn_ln_b"], np.float32)[perm].reshape(8, 128).T),
        "epspos": np.ascontiguousarray(
            (LN_EPS * np.arange(1, Tt + 1, dtype=np.float32))[None, :]),
    }


FP4_MID = [0.25, 0.75, 1.25, 1.75, 2.5, 3.5, 5.0]
FP4_STEP = [0.5, 0.5, 0.5, 0.5, 1.0, 1.0, 2.0]


def gen_phase4(nc, S, sb, din, banks, x_in, x_out, Ttok):
    NT = Ttok // 512
    I = 2816
    NI = I // 128
    wg_d = din("wg", [I, D], F32)
    wu_d = din("wu", [I, D], F32)
    wd_d = din("wd", [D, I], F32)
    eyeb = din("eyeb", [128, 128], BF16)
    eyefd = din("eyef4", [128, 128], F32)
    _int = _mk_in(nc, S.pref, kind="Internal")
    wgT = _int("wgT", [D, I], BF16)
    wuT = _int("wuT", [D, I], BF16)
    wdT = _int("wdT", [I, D], BF16)
    s_x = sb("s_x", [128, NCH * 512], F32)
    s_xh = sb("s_xh", [128, NCH * 512], BF16)
    s_x32 = sb("s_x32", [128, NCH * 512], F32)
    s_xtm = sb("s_xtm", [128, 4 * D], F32)
    s_xtmq = sb("s_xtmq", [128, 4 * D], BF16)
    s_xq = sb("s_xq", [128, NCH * 512], BF16)
    s_hq = sb("s_hq", [128, NI * 512], BF16)
    s_sg2 = [sb(f"s_sg{i}", [128, 512], BF16) for i in range(2)]
    s_hb2 = [sb(f"s_hb{i}", [128, 512], BF16) for i in range(2)]
    s_htm2 = [sb(f"s_htm{i}", [128, 512], BF16) for i in range(2)]
    s_eye = sb("s_eye", [128, 128], BF16)
    s_eyf = sb("s_eyf", [128, 128], F32)
    s_ones = sb("s_ones", [128, 512], BF16)
    s_eps = sb("s_epsP", [1, 1], F32)
    s_sq = sb("s_sq", [128, 2 * 512], BF16)
    s_st = sb("s_st", [1, 4 * 512], F32)
    s_stb = sb("s_stb", [1, 2 * 512], BF16)
    s_t1 = sb("s_t1", [128, 1024], F32)
    s_t2 = sb("s_t2", [128, 1024], F32)
    s_qmx = sb("s_qmx", [128, 64], F32)
    s_qsc = sb("s_qsc", [128, 64], F32)
    s_qrs = sb("s_qrs", [128, 64], F32)
    s_qsg = sb("s_qsg", [128, 1024], BF16)
    s_qz = sb("s_qz", [128, 1024], F32)
    s_qc = sb("s_qc", [128, 1024], BF16)
    s_qt = sb("s_qt", [128, 1024], BF16)
    s_qs = sb("s_qs", [128, 1024], F32)
    s_qmagic = sb("s_qmagic", [128, 1], F32)
    s_qmagicn = sb("s_qmagicn", [128, 1], F32)
    nc.const_aps.register(F32, 64.0, s_qmagic[:])
    nc.const_aps.register(F32, -64.0, s_qmagicn[:])
    s_wf2 = [sb(f"s_wf{i}", [128, 1024], F32) for i in range(2)]
    s_wq2 = [sb(f"s_wq{i}", [128, 1024], BF16) for i in range(2)]
    s_wt2 = [sb(f"s_wt{i}", [128, 1024], BF16) for i in range(2)]
    s_wgu = [sb(f"s_wgu{i}", [128, 2 * NCH * 128], BF16) for i in range(2)]
    s_wdn = [sb(f"s_wdn{i}", [128, 22 * 128], BF16) for i in range(2)]

    (bk6, pS1, pS2) = banks
    pA, pB, pC, pD, pBC, pG = bk6

    x3 = s_x[:].rearrange("p (c n) -> p c n", n=512)
    xh3 = s_xh[:].rearrange("p (c n) -> p c n", n=512)
    xtm = s_xtm[:].rearrange("p (g f) -> p g f", f=D)
    xq3 = s_xq[:].rearrange("p (c n) -> p c n", n=512)
    hq3 = s_hq[:].rearrange("p (c n) -> p c n", n=512)
    wgu = [s_wgu[i][:].rearrange("p (m c f) -> p m c f", m=2, f=128)
           for i in range(2)]

    def xap(dram, t):
        return dram.rearrange("(c p) n -> p c n", p=128)[:, :, t * 512:(t + 1) * 512]

    S.dma("sync", lambda E_: [
        E_.dma_start(s_eye[:], eyeb),
        E_.dma_start(s_eyf[:], eyefd),
    ], n=2, sem="cst")
    S.op("vector", lambda V: [
        V.memset(s_ones[:], 1.0),
        V.memset(s_eps[:], LN_EPS),
        V.memset(s_qmagic[:], 64.0),
        V.memset(s_qmagicn[:], -64.0),
    ], n=4)

    mm_mark = {0: 0, 1: 0}
    tr_mark = {0: 0, 1: 0}

    def emit_quant(src, dst, F, after=None):
        nb = F // 16
        sv = s_qmx[:][:, 0:nb]
        sc = s_qsc[:][:, 0:nb]
        rs = s_qrs[:][:, 0:nb]
        sg = s_qsg[:][:, 0:F]
        z = s_qz[:][:, 0:F]
        r1 = s_qt[:][:, 0:F]
        cc = s_qc[:][:, 0:F]
        ss = s_qs[:][:, 0:F]
        src3 = src.rearrange("p (b i) -> p b i", i=16)
        S.op("vector", lambda V: V.tensor_reduce(
            sv, src3, AX.X, ALU.max, apply_absolute_value=True), after=after)
        S.op("vector", lambda V: V.tensor_scalar(
            sc, sv, 1.0 / 6.0, 1e-12, ALU.mult, ALU.max))
        S.op("vector", lambda V: V.reciprocal(rs, sc))
        S.op("scalar", lambda Sc: Sc.activation(sg, src, AF.Sign),
             after=S.now("vector"))
        S.op("scalar", lambda Sc: Sc.activation(z, src, AF.Abs))
        rs_b = rs.unsqueeze(2).broadcast_to([128, nb, 16])
        sc_b = sc.unsqueeze(2).broadcast_to([128, nb, 16])
        z3 = z.rearrange("p (b i) -> p b i", i=16)
        S.op("vector", lambda V: V.tensor_tensor(z3, z3, rs_b, ALU.mult),
             after=S.now("scalar"))
        S.op("scalar", lambda Sc: Sc.activation(r1, z, AF.Copy, bias=64.0),
             after=S.now("vector"))
        S.op("scalar", lambda Sc: Sc.activation(r1, r1, AF.Copy, bias=-64.0))
        S.op("vector", lambda V: V.tensor_scalar_min(cc, r1, 2.0),
             after=S.now("scalar"))
        S.op("vector", lambda V: V.scalar_tensor_tensor(
            cc, z, 2.5, cc, ALU.is_gt, ALU.add))
        S.op("vector", lambda V: V.scalar_tensor_tensor(
            cc, z, 3.5, cc, ALU.is_gt, ALU.add))
        S.op("vector", lambda V: V.tensor_scalar(
            r1, z, 5.0, 2.0, ALU.is_gt, ALU.mult))
        S.op("vector", lambda V: V.tensor_add(cc, cc, r1))
        sg3 = sg.rearrange("p (b i) -> p b i", i=16)
        ss3 = ss.rearrange("p (b i) -> p b i", i=16)
        S.op("vector", lambda V: V.tensor_tensor(ss3, sg3, sc_b, ALU.mult))
        S.op("vector", lambda V: V.tensor_tensor(dst, cc, ss, ALU.mult))

    wq_ci = [0]
    wq_cons = [ {"scalar": 0, "vector": 0}, {"scalar": 0, "vector": 0} ]
    wq_tmark = [0, 0]

    def quant_transpose(dram_in, dram_out, R, Cc):
        nr = R // 128
        for r in range(nr):
            rsl = slice(r * 128, r * 128 + 128)
            subs = [(0, 1024), (1024, 2048), (2048, 2816)] if Cc == 2816 \
                else [(0, 1024)]
            for (f0, f1) in subs:
                F = f1 - f0
                ci = wq_ci[0]; wq_ci[0] += 1
                p = ci % 2
                s_wf = s_wf2[p]; s_wq = s_wq2[p]; s_wt = s_wt2[p]
                S.dma("sync", lambda E_, rsl=rsl, f0=f0, f1=f1, s_wf=s_wf:
                      E_.dma_start(s_wf[:][:, 0:f1 - f0], dram_in[rsl, f0:f1]),
                      after=dict(wq_cons[p]), sem="wload")
                emit_quant(s_wf[:][:, 0:F], s_wq[:][:, 0:F], F,
                           after={**S.now("wload"), "tensor": wq_tmark[p]})
                wq_cons[p] = {"scalar": S.cnt["scalar"],
                              "vector": S.cnt["vector"]}
                wst = S.now("wstore")
                for cb in range(F // 128):
                    bank = pA if cb % 2 == 0 else pB
                    S.op("tensor", lambda Tn, cb=cb, bank=bank, s_wq=s_wq:
                         Tn.matmul(bank[:, 0:128],
                                   s_wq[:][:, cb * 128:cb * 128 + 128],
                                   s_eye[:], start=True, stop=True),
                         after={**S.now("vector", "cst"),
                                "scalar": tr_mark[cb % 2]})
                    S.op("scalar", lambda Sc, cb=cb, bank=bank, s_wt=s_wt:
                         Sc.activation(s_wt[:][:, cb * 128:cb * 128 + 128],
                                       bank[:, 0:128], AF.Copy),
                         after={**S.now("tensor"), **wst})
                    tr_mark[cb % 2] = S.cnt["scalar"]
                wq_tmark[p] = S.cnt["tensor"]
                S.dma("sync", lambda E_, rsl=rsl, f0=f0, F=F, s_wt=s_wt:
                      E_.dma_start(dram_out[f0:f0 + F, rsl].rearrange(
                          "(b p) r -> p b r", p=128),
                          s_wt[:][:, 0:F].rearrange("p (b r) -> p b r", r=128)),
                      after=S.now("scalar"), sem="wstore")

    quant_transpose(wg_d, wgT, I, D)
    quant_transpose(wu_d, wuT, I, D)
    quant_transpose(wd_d, wdT, D, I)
    wq_done_store = S.now("wstore")

    ev_mark = {0: 0, 1: 0}
    dn_mark = {0: 0, 1: 0}
    m_wgu = {}
    m_wdn = {}

    def load_wgu(t, oc):
        j = t * NI + oc
        b = j % 2
        aft = dict(wq_done_store)
        if j >= 2:
            aft["tensor"] = m_wgu[j - 2][1]
        S.dma("gpsimd", lambda G, oc=oc, b=b: [
            G.dma_start(wgu[b][:, 0, :, :],
                        wgT.rearrange("(c p) i -> p c i", p=128)
                        [:, :, oc * 128:oc * 128 + 128]),
            G.dma_start(wgu[b][:, 1, :, :],
                        wuT.rearrange("(c p) i -> p c i", p=128)
                        [:, :, oc * 128:oc * 128 + 128]),
        ], after=aft, n=2, sem="lgu_%d" % b)
        m_wgu[j] = ({S._p("lgu_%d" % b): S.cnt[S._p("lgu_%d" % b)]}, None)

    def load_wdn(t, dc):
        j = t * NCH + dc
        b = j % 2
        aft = dict(wq_done_store)
        if j >= 2:
            aft["tensor"] = m_wdn[j - 2][1]
        S.dma("gpsimd", lambda G, dc=dc, b=b: G.dma_start(
            s_wdn[b][:].rearrange("p (c f) -> p c f", f=128),
            wdT.rearrange("(c p) d -> p c d", p=128)
            [:, :, dc * 128:dc * 128 + 128].rearrange("p c d -> p (c d)")
            if False else
            wdT.rearrange("(c p) d -> p c d", p=128)[:, :, dc * 128:dc * 128 + 128]),
            after=aft, sem="ldn_%d" % b)
        m_wdn[j] = ({S._p("ldn_%d" % b): S.cnt[S._p("ldn_%d" % b)]}, None)

    def ln_block4(src3, dst3, sub3=None):
        sub3 = sub3 if sub3 is not None else src3
        mu = s_st[:][:, 0:512]
        ex2 = s_st[:][:, 512:1024]
        var = s_st[:][:, 1024:1536]
        m2 = s_st[:][:, 1536:2048]
        mub = s_stb[:][:, 0:512]
        rsb = s_stb[:][:, 512:1024]
        v_src = S.now("vector")
        for c in range(NCH):
            S.op("scalar", lambda Sc, c=c: Sc.activation(
                s_sq[:][:, (c % 2) * 512:(c % 2) * 512 + 512],
                src3[:, c, :], AF.Square),
                after={"tensor": mm_mark[c % 2], **v_src})
            S.op("tensor", lambda Tn, c=c: [
                Tn.matmul(pS1[:], s_ones[:][:, 0:1], src3[:, c, :],
                          start=(c == 0), stop=(c == NCH - 1)),
                Tn.matmul(pS2[:], s_ones[:][:, 0:1],
                          s_sq[:][:, (c % 2) * 512:(c % 2) * 512 + 512],
                          start=(c == 0), stop=(c == NCH - 1)),
            ], after=S.now("scalar"), n=2)
            mm_mark[c % 2] = S.cnt["tensor"]
        S.op("vector", lambda V: [
            V.tensor_scalar_mul(mu, pS1[:], 1.0 / D),
            V.tensor_scalar_mul(ex2, pS2[:], 1.0 / D),
        ], after=S.now("tensor"), n=2)
        S.op("vector", lambda V: V.tensor_mul(m2, mu, mu))
        S.op("vector", lambda V: V.tensor_copy(mub, mu))
        S.op("vector", lambda V: V.tensor_sub(var, ex2, m2))
        S.op("scalar", lambda Sc: Sc.activation(var, var, AF.Sqrt,
                                                bias=s_eps[:]),
             after=S.now("vector"))
        S.op("vector", lambda V: V.reciprocal(var, var), after=S.now("scalar"))
        S.op("vector", lambda V: V.tensor_copy(rsb, var))
        S.op("tensor", lambda Tn: [
            Tn.matmul(pBC[:], s_ones[:][0:1, 0:128], mub, start=True, stop=True),
            Tn.matmul(pG[:], s_ones[:][0:1, 0:128], rsb, start=True, stop=True),
        ], after=S.now("vector"), n=2)
        for c in range(NCH):
            S.op("vector", lambda V, c=c: V.tensor_tensor(
                s_t1[:][:, 0:512], sub3[:, c, :], pBC[:], ALU.subtract),
                after=S.now("tensor"))
            S.op("vector", lambda V, c=c: V.tensor_tensor(
                dst3[:, c, :], s_t1[:][:, 0:512], pG[:], ALU.mult))

    load_wgu(0, 0)
    load_wgu(0, 1)
    load_wdn(0, 0)
    load_wdn(0, 1)

    for t in range(NT):
        S.dma("sync", lambda E_, t=t: E_.dma_start(x3, xap(x_in, t)),
              after=S.now("vector", "sxo"), sem="sx")
        m_x = S.now("sx")
        for c in range(NCH):
            S.op("scalar", lambda Sc, c=c: Sc.activation(
                xh3[:, c, :], x3[:, c, :], AF.Copy),
                after={**m_x, **S.now("vector")})
        x323 = s_x32[:].rearrange("p (c n) -> p c n", n=512)
        ln_block4(xh3, x323, sub3=x3)
        xh_done = S.now("vector")
        xtmq = s_xtmq[:].rearrange("p (g f) -> p g f", f=D)
        for tg in range(4):
            for half in range(2):
                bank = pA if half == 0 else pB
                S.op("tensor", lambda Tn, tg=tg, half=half, bank=bank: [
                    Tn.matmul(bank[:, ci * 128:ci * 128 + 128],
                              x323[:, half * 4 + ci, tg * 128:tg * 128 + 128],
                              s_eyf[:], start=(ci == 0), stop=(ci == 3))
                    for ci in range(4)],
                    after={**S.now("vector", "cst"),
                           "scalar": tr_mark[half]}, n=4)
                S.op("scalar", lambda Sc, tg=tg, half=half, bank=bank:
                     Sc.activation(xtm[:, tg, half * 512:half * 512 + 512],
                                   bank[:], AF.Copy), after=S.now("tensor"))
                tr_mark[half] = S.cnt["scalar"]
        for tg in range(4):
            emit_quant(xtm[:, tg, :], xtmq[:, tg, :], D,
                       after=S.now("scalar"))
        for c in range(NCH):
            half = c % 2
            bank = pA if half == 0 else pB
            S.op("tensor", lambda Tn, c=c, bank=bank: [
                Tn.matmul(bank[:, tg * 128:tg * 128 + 128],
                          xtmq[:, tg, c * 128:c * 128 + 128],
                          s_eye[:], start=(tg == 0), stop=(tg == 3))
                for tg in range(4)],
                after={**S.now("vector"), "scalar": tr_mark[half]}, n=4)
            S.op("scalar", lambda Sc, c=c, bank=bank: Sc.activation(
                xq3[:, c, :], bank[:], AF.Copy), after=S.now("tensor"))
            tr_mark[half] = S.cnt["scalar"]
        xq_done = S.now("scalar")

        pcd = [(pC, pD), (pBC, pG)]
        sil_mark = [S.cnt["scalar"]] * 2
        hmul_mark = [S.cnt["vector"]] * 2
        htm_mark = {}
        hq_done_m = {}

        def stageA1(oc):
            p = oc % 2
            j = t * NI + oc
            b = j % 2
            bC, bD = pcd[p]
            aft = {**m_wgu[j][0],
                   "scalar": max(xq_done["scalar"], sil_mark[p]),
                   "vector": hmul_mark[p]}
            S.op("tensor", lambda Tn, b=b, bC=bC, bD=bD: [
                Tn.matmul(bC[:], wgu[b][:, 0, c2, :], xq3[:, c2, :],
                          start=(c2 == 0), stop=(c2 == NCH - 1))
                for c2 in range(NCH)] + [
                Tn.matmul(bD[:], wgu[b][:, 1, c2, :], xq3[:, c2, :],
                          start=(c2 == 0), stop=(c2 == NCH - 1))
                for c2 in range(NCH)],
                after=aft, n=2 * NCH)
            m_wgu[j] = (m_wgu[j][0], S.cnt["tensor"])
            if j + 2 < NT * NI:
                load_wgu(*divmod(j + 2, NI))

        def stageA2(oc):
            p = oc % 2
            bC, bD = pcd[p]
            S.op("scalar", lambda Sc, p=p, bC=bC: Sc.activation(
                s_sg2[p][:], bC[:], AF.Silu),
                 after=S.now("tensor", "vector"))
            sil_mark[p] = S.cnt["scalar"]
            S.op("vector", lambda V, p=p, bD=bD: V.tensor_tensor(
                s_hb2[p][:], s_sg2[p][:], bD[:], ALU.mult),
                 after=S.now("scalar", "tensor"))
            hmul_mark[p] = S.cnt["vector"]
            S.op("tensor", lambda Tn, p=p: [
                Tn.matmul(pA[:, tg * 128:tg * 128 + 128],
                          s_hb2[p][:][:, tg * 128:tg * 128 + 128],
                          s_eye[:], start=(tg == 0), stop=(tg == 3))
                for tg in range(4)],
                after={**S.now("vector"), "scalar": tr_mark[0]}, n=4)
            S.op("scalar", lambda Sc, p=p: Sc.activation(
                s_htm2[p][:], pA[:], AF.Copy), after=S.now("tensor"))
            tr_mark[0] = S.cnt["scalar"]
            htm_mark[oc] = S.cnt["scalar"]

        def stageB(oc):
            p = oc % 2
            emit_quant(s_htm2[p][:], s_htm2[p][:], 512,
                       after={"scalar": htm_mark[oc]})
            S.op("tensor", lambda Tn, p=p: [
                Tn.matmul(pB[:, tg * 128:tg * 128 + 128],
                          s_htm2[p][:][:, tg * 128:tg * 128 + 128],
                          s_eye[:], start=(tg == 0), stop=(tg == 3))
                for tg in range(4)],
                after={**S.now("vector"), "scalar": tr_mark[1]}, n=4)
            S.op("scalar", lambda Sc, oc=oc: Sc.activation(
                hq3[:, oc, :], pB[:], AF.Copy), after=S.now("tensor"))
            tr_mark[1] = S.cnt["scalar"]

        stageA1(0)
        stageA2(0)
        for oc in range(1, NI):
            stageA1(oc)
            stageB(oc - 1)
            stageA2(oc)
        stageB(NI - 1)
        hq_done = S.now("scalar")

        v_hq = S.cnt["vector"]
        for dc in range(NCH):
            j = t * NCH + dc
            b = j % 2
            bank = pC if dc % 2 == 0 else pD
            aft = {**m_wdn[j][0], "scalar": hq_done["scalar"],
                   "vector": max(dn_mark[dc % 2], v_hq)}
            S.op("tensor", lambda Tn, dc=dc, b=b, bank=bank: [
                Tn.matmul(bank[:], s_wdn[b][:].rearrange(
                    "p (c f) -> p c f", f=128)[:, c2, :], hq3[:, c2, :],
                    start=(c2 == 0), stop=(c2 == NI - 1))
                for c2 in range(NI)],
                after=aft, n=NI)
            m_wdn[j] = (m_wdn[j][0], S.cnt["tensor"])
            if j + 2 < NT * NCH:
                load_wdn(*divmod(j + 2, NCH))
            S.op("vector", lambda V, dc=dc, bank=bank: V.tensor_add(
                x3[:, dc, :], x3[:, dc, :], bank[:]), after=S.now("tensor"))
            dn_mark[dc % 2] = S.cnt["vector"]
        S.dma("sync", lambda E_, t=t: E_.dma_start(xap(x_out, t), x3),
              after=S.now("vector"), sem="sxo")

    S.wait_only("sync", S.now("sxo"))


def build_phase4(Ttok=T):
    nc = bass.Bass("TRN2", target_bir_lowering=False, debug=False)
    S = Sched()
    banks = _mk_banks(nc)
    x_in = nc.dram_tensor("x_in", [D, Ttok], F32, kind="ExternalInput").ap()
    x_out = nc.dram_tensor("x_out", [D, Ttok], F32, kind="ExternalOutput").ap()
    gen_phase4(nc, S, nc.alloc_sbuf_tensor, _mk_in(nc, ""), banks,
               x_in, x_out, Ttok)
    S.emit(nc)
    return nc


def prep_phase4_inputs(xcore, inp, fm=False):
    return {
        "x_in": np.ascontiguousarray(np.asarray(xcore, np.float32)) if fm
        else np.ascontiguousarray(np.asarray(xcore, np.float32).T),
        "wg": np.ascontiguousarray(np.asarray(inp["L1_ffn_w_gate"], np.float32)),
        "wu": np.ascontiguousarray(np.asarray(inp["L1_ffn_w_up"], np.float32)),
        "wd": np.ascontiguousarray(np.asarray(inp["L1_ffn_w_down"], np.float32)),
        "eyeb": np.eye(128, dtype=np.float32).astype(BF),
        "eyef4": np.eye(128, dtype=np.float32),
    }


GAMMA = 0.99


def gen_phase3(nc, S, sb, din, banks, x_in, x_out, Ttok, upto=8, mixn=9999):
    NT = Ttok // 512
    HD = 512
    wq_d = din("wq", [D, HD], F32)
    wk_d = din("wk", [D, HD], F32)
    wv_d = din("wv", [D, HD], F32)
    wo_d = din("wo", [HD, D], F32)
    upi_d = din("upi", [128, 512], BF16)
    ups_d = din("ups", [128, 512], BF16)
    gp1_d = din("gp1", [128, 128], BF16)
    gm1_d = din("gm1", [128, 128], BF16)
    gp0_d = din("gp0", [128, 128], BF16)
    glc_d = din("glc", [128, 1], F32)
    e32_d = din("e32", [128, 32], BF16)
    geL_d = din("geL", [128, 32], BF16)
    geLF_d = din("geLF", [128, 128], BF16)
    s_x = sb("s_x", [128, NCH * 512], F32)
    s_xh = sb("s_xh", [128, NCH * 512], BF16)
    s_q = sb("s_q", [128, 4 * 512], BF16)
    s_k = sb("s_k", [128, 4 * 512], BF16)
    s_v = sb("s_v", [128, 4 * 512], BF16)
    s_o = sb("s_o", [128, 4 * 512], BF16)
    s_wq = sb("s_wq", [128, NCH * HD], BF16)
    s_wk = sb("s_wk", [128, NCH * HD], BF16)
    s_wv = sb("s_wv", [128, NCH * HD], BF16)
    s_wo = sb("s_wo", [128, 4 * D], BF16)
    s_ST = sb("s_ST", [128, 4 * 96], BF16)
    s_upi = sb("s_upi", [128, 512], BF16)
    s_ups = sb("s_ups", [128, 512], BF16)
    s_gp1 = sb("s_gp1", [128, 128], BF16)
    s_gm1 = sb("s_gm1", [128, 128], BF16)
    s_gp0 = sb("s_gp0", [128, 128], BF16)
    s_glc = sb("s_glc", [128, 1], F32)
    s_e32 = sb("s_e32", [128, 32], BF16)
    s_geL = sb("s_geL", [128, 32], BF16)
    s_geLF = sb("s_geLF", [128, 128], BF16)
    s_Qg = sb("s_Qg", [128, 128], BF16)
    s_Qm = sb("s_Qm", [128, 128], BF16)
    s_Km = sb("s_Km", [128, 128], BF16)
    s_Ks = sb("s_Ks", [128, 128], BF16)
    s_Wt = sb("s_Wt", [128, 512], BF16)
    s_T2 = sb("s_T2", [128, 512], BF16)
    s_MA = sb("s_MA", [128, 512], BF16)
    s_Ktm = sb("s_Ktm", [128, 128], BF16)
    s_Vtm = sb("s_Vtm", [128, 128], BF16)
    s_Qtm = sb("s_Qtm", [128, 128], BF16)
    s_KtmL = sb("s_KtmL", [128, 128], BF16)
    s_QtmL = sb("s_QtmL", [128, 128], BF16)
    s_U = sb("s_U", [128, 128], BF16)
    s_a = sb("s_a", [128, 128], BF16)
    s_ag = sb("s_ag", [128, 128], BF16)
    s_tst = sb("s_tst", [128, 96], F32)
    s_ones = sb("s_ones", [128, 512], BF16)
    s_eps = sb("s_epsP", [1, 1], F32)
    s_sq = sb("s_sq", [128, 2 * 512], BF16)
    s_st = sb("s_st", [1, 4 * 512], F32)
    s_stb = sb("s_stb", [1, 2 * 512], BF16)
    s_t1 = sb("s_t1", [128, 512], F32)

    (bk6, pS1, pS2) = banks
    pA, pB, pC, pD, pE, pF = bk6

    x3 = s_x[:].rearrange("p (c n) -> p c n", n=512)
    xh3 = s_xh[:].rearrange("p (c n) -> p c n", n=512)
    q3 = s_q[:].rearrange("p (g n) -> p g n", n=512)
    k3 = s_k[:].rearrange("p (g n) -> p g n", n=512)
    v3 = s_v[:].rearrange("p (g n) -> p g n", n=512)
    o3 = s_o[:].rearrange("p (g n) -> p g n", n=512)
    wq3 = s_wq[:].rearrange("p (c j) -> p c j", j=HD)
    wk3 = s_wk[:].rearrange("p (c j) -> p c j", j=HD)
    wv3 = s_wv[:].rearrange("p (c j) -> p c j", j=HD)
    wo3 = s_wo[:].rearrange("p (c j) -> p c j", j=D)
    st3 = s_ST[:].rearrange("p (g w) -> p g w", w=96)

    def xap(dram, t):
        return dram.rearrange("(c p) n -> p c n", p=128)[:, :, t * 512:(t + 1) * 512]

    S.dma("sync", lambda E_: [
        E_.dma_start(s_upi[:], upi_d),
        E_.dma_start(s_ups[:], ups_d),
        E_.dma_start(s_gp1[:], gp1_d),
        E_.dma_start(s_gm1[:], gm1_d),
        E_.dma_start(s_gp0[:], gp0_d),
        E_.dma_start(s_glc[:], glc_d),
        E_.dma_start(s_e32[:], e32_d),
        E_.dma_start(s_geL[:], geL_d),
        E_.dma_start(s_geLF[:], geLF_d),
    ], n=9, sem="cst")
    S.dma("gpsimd", lambda G: [
        G.dma_start(s_wq[:].rearrange("p (c j) -> p c j", j=HD),
                    wq_d.rearrange("(c p) j -> p c j", p=128)),
        G.dma_start(s_wk[:].rearrange("p (c j) -> p c j", j=HD),
                    wk_d.rearrange("(c p) j -> p c j", p=128)),
        G.dma_start(s_wv[:].rearrange("p (c j) -> p c j", j=HD),
                    wv_d.rearrange("(c p) j -> p c j", p=128)),
        G.dma_start(s_wo[:].rearrange("p (c j) -> p c j", j=D),
                    wo_d.rearrange("(c p) j -> p c j", p=128)),
    ], n=4, sem="wts")
    m_wts = S.now("wts")
    S.op("vector", lambda V: [
        V.memset(s_ones[:], 1.0),
        V.memset(s_eps[:], LN_EPS),
        V.memset(s_ST[:], 0.0),
    ], n=3)
    cst_now = S.now("cst")

    mm_mark = {0: 0, 1: 0}
    evac_mark = {0: 0, 1: 0}
    oproj_mark = {0: 0, 1: 0}
    st_read_mark = 0
    st_read_vec = 0

    def ln_block3(src3, sub3, dst3):
        mu = s_st[:][:, 0:512]
        ex2 = s_st[:][:, 512:1024]
        var = s_st[:][:, 1024:1536]
        m2 = s_st[:][:, 1536:2048]
        mub = s_stb[:][:, 0:512]
        rsb = s_stb[:][:, 512:1024]
        v_src = S.now("vector")
        for c in range(NCH):
            S.op("scalar", lambda Sc, c=c: Sc.activation(
                s_sq[:][:, (c % 2) * 512:(c % 2) * 512 + 512],
                src3[:, c, :], AF.Square),
                after={"tensor": mm_mark[c % 2], **v_src})
            S.op("tensor", lambda Tn, c=c: [
                Tn.matmul(pS1[:], s_ones[:][:, 0:1], src3[:, c, :],
                          start=(c == 0), stop=(c == NCH - 1)),
                Tn.matmul(pS2[:], s_ones[:][:, 0:1],
                          s_sq[:][:, (c % 2) * 512:(c % 2) * 512 + 512],
                          start=(c == 0), stop=(c == NCH - 1)),
            ], after=S.now("scalar"), n=2)
            mm_mark[c % 2] = S.cnt["tensor"]
        S.op("vector", lambda V: [
            V.tensor_scalar_mul(mu, pS1[:], 1.0 / D),
            V.tensor_scalar_mul(ex2, pS2[:], 1.0 / D),
        ], after=S.now("tensor"), n=2)
        S.op("vector", lambda V: V.tensor_mul(m2, mu, mu))
        S.op("vector", lambda V: V.tensor_copy(mub, mu))
        S.op("vector", lambda V: V.tensor_sub(var, ex2, m2))
        S.op("scalar", lambda Sc: Sc.activation(var, var, AF.Sqrt,
                                                bias=s_eps[:]),
             after=S.now("vector"))
        S.op("vector", lambda V: V.reciprocal(var, var), after=S.now("scalar"))
        S.op("vector", lambda V: V.tensor_copy(rsb, var))
        S.op("tensor", lambda Tn: [
            Tn.matmul(pC[:], s_ones[:][0:1, 0:128], mub, start=True, stop=True),
            Tn.matmul(pD[:], s_ones[:][0:1, 0:128], rsb, start=True, stop=True),
        ], after=S.now("vector"), n=2)
        for c in range(NCH):
            S.op("vector", lambda V, c=c: V.tensor_tensor(
                s_t1[:], sub3[:, c, :], pC[:], ALU.subtract),
                after=S.now("tensor"))
            S.op("vector", lambda V, c=c: V.tensor_tensor(
                dst3[:, c, :], s_t1[:], pD[:], ALU.mult))

    for t in range(NT):
        S.dma("sync", lambda E_, t=t: E_.dma_start(x3, xap(x_in, t)),
              after=S.now("vector", "sxo"), sem="sx")
        m_x = S.now("sx")
        for c in range(NCH):
            S.op("scalar", lambda Sc, c=c: Sc.activation(
                xh3[:, c, :], x3[:, c, :], AF.Copy),
                after={**m_x, **S.now("vector")})
        ln_block3(xh3, xh3, xh3)
        xh_done = S.now("vector")

        for (wsrc, dst3) in ((wq3, q3), (wk3, k3), (wv3, v3)):
            for oc in range(4):
                bank = pA if (oc % 2 == 0) else pB
                S.op("tensor", lambda Tn, oc=oc, bank=bank, wsrc=wsrc: [
                    Tn.matmul(bank[:], wsrc[:, c2, oc * 128:oc * 128 + 128],
                              xh3[:, c2, :], start=(c2 == 0),
                              stop=(c2 == NCH - 1))
                    for c2 in range(NCH)],
                    after={**m_wts, "vector": xh_done["vector"],
                           "scalar": evac_mark[oc % 2]}, n=NCH)
                S.op("scalar", lambda Sc, oc=oc, bank=bank, dst3=dst3:
                     Sc.activation(dst3[:, oc, :], bank[:], AF.Copy),
                     after=S.now("tensor"))
                evac_mark[oc % 2] = S.cnt["scalar"]
        proj_done = S.now("scalar")

        bk = [pA, pB, pC, pD]
        for cc in range(4):
            csl = slice(cc * 128, cc * 128 + 128)
            for g in range(4):
                Qf = q3[:, g, csl]
                Kf = k3[:, g, csl]
                Vf = v3[:, g, csl]
                A0 = [st3[32 * m:32 * m + 32, g, 0:32] for m in range(4)]
                C0 = [st3[32 * m:32 * m + 32, g, 32:64] for m in range(4)]
                G0 = [st3[32 * m:32 * m + 32, g, 64:96] for m in range(4)]
                if upto < 1:
                    continue
                S.op("vector", lambda V, Qf=Qf: V.tensor_mul(
                    s_Qg[:], Qf, s_gp1[:]),
                    after={**cst_now, "vector": st_read_vec,
                           "scalar": max(proj_done["scalar"], st_read_mark)})
                S.op("vector", lambda V, Qf=Qf: V.tensor_mul(
                    s_Qm[:], Qf, s_gm1[:]))
                S.op("vector", lambda V, Kf=Kf: V.tensor_mul(
                    s_Km[:], Kf, s_gm1[:]))
                S.op("vector", lambda V, Kf=Kf: V.tensor_mul(
                    s_Ks[:], Kf, s_gp0[:]))
                sc_done = S.now("vector")
                if upto == 13:
                    continue
                S.op("tensor", lambda Tn: [
                    Tn.matmul(bk[m][:, 0:128],
                              s_Km[:][32 * m:32 * m + 32, :],
                              s_Qg[:][32 * m:32 * m + 32, :],
                              start=True, stop=True,
                              tile_position=(32 * m, 0),
                              skip_group_check=True)
                    for m in range(4)], after=sc_done, n=4)
                S.op("tensor", lambda Tn: [
                    Tn.matmul(bk[m][:, 128:256],
                              s_Qm[:][32 * m:32 * m + 32, :],
                              s_Ks[:][32 * m:32 * m + 32, :],
                              start=True, stop=True,
                              tile_position=(32 * m, 0),
                              skip_group_check=True)
                    for m in range(4)], n=4)
                if upto == 12:
                    continue
                S.op("vector", lambda V: [
                    V.tensor_tensor(s_Wt[:][:, m * 128:m * 128 + 128],
                                    bk[m][:, 0:128], s_upi[:][:, 0:128],
                                    ALU.mult)
                    for m in range(4)], after=S.now("tensor"), n=4)
                S.op("vector", lambda V: [
                    V.tensor_tensor(s_T2[:][:, m * 128:m * 128 + 128],
                                    bk[m][:, 128:256], s_ups[:][:, 0:128],
                                    ALU.mult)
                    for m in range(4)], n=4)
                wt_done = S.now("vector")
                if upto < 2:
                    continue
                for (srcf, dsttm, f0) in ((Kf, s_Ktm, 384), (Vf, s_Vtm, 416),
                                          (Qf, s_Qtm, 448)):
                    S.op("tensor", lambda Tn, srcf=srcf, f0=f0: [
                        Tn.matmul(bk[m][:, f0:f0 + 32],
                                  srcf[32 * m:32 * m + 32, :],
                                  s_e32[:][32 * m:32 * m + 32, :],
                                  start=True, stop=True,
                                  tile_position=(32 * m, 0),
                                  skip_group_check=True)
                        for m in range(4)],
                        after=S.now("vector", "scalar"), n=4)
                    S.op("scalar", lambda Sc, dsttm=dsttm, f0=f0: [
                        Sc.activation(dsttm[:][:, m * 32:m * 32 + 32],
                                      bk[m][:, f0:f0 + 32], AF.Copy)
                        for m in range(4)], after=S.now("tensor"), n=4)
                if upto < 3:
                    continue
                S.op("vector", lambda V: V.tensor_scalar(
                    s_KtmL[:], s_Ktm[:], s_glc[:], None, ALU.mult),
                    after=S.now("scalar"))
                S.op("vector", lambda V: V.tensor_scalar(
                    s_QtmL[:], s_Qtm[:], s_glc[:], None, ALU.mult))
                tm_done = S.now("vector", "scalar")
                if upto < 4:
                    continue
                S.op("tensor", lambda Tn, C0=C0: [
                    Tn.matmul(bk[m][:, 480:512],
                              s_Ks[:][32 * m:32 * m + 32, :], C0[m],
                              start=True, stop=False,
                              tile_position=(32 * m, 0),
                              skip_group_check=True)
                    for m in range(4)],
                    after={**tm_done, "scalar": st_read_mark,
                           "vector": max(tm_done.get("vector", 0), st_read_vec)}, n=4)
                S.op("tensor", lambda Tn: [
                    Tn.matmul(bk[m][:, 480:512],
                              s_T2[:][:, m * 128:m * 128 + 128],
                              s_Vtm[:][:, m * 32:m * 32 + 32],
                              start=False, stop=True,
                              skip_group_check=True)
                    for m in range(4)], n=4)
                S.op("vector", lambda V: [
                    V.tensor_scalar_mul(s_U[:][:, m * 32:m * 32 + 32],
                                        bk[m][:, 480:512], -GAMMA)
                    for m in range(4)], after=S.now("tensor"), n=4)
                if upto < 5:
                    continue
                S.op("tensor", lambda Tn, A0=A0: [
                    mmi for m in range(4) for mmi in [
                        Tn.matmul(pE[32 * m:32 * m + 32, 0:128],
                                  s_Ktm[:][:, m * 32:m * 32 + 32],
                                  s_Wt[:][:, m * 128:m * 128 + 128],
                                  start=True, stop=False,
                                  tile_position=(0, 32 * m),
                                  skip_group_check=True),
                        Tn.matmul(pE[32 * m:32 * m + 32, 0:128], A0[m],
                                  s_Qg[:][32 * m:32 * m + 32, :],
                                  start=False, stop=True,
                                  tile_position=(32 * m, 32 * m),
                                  skip_group_check=True),
                    ]], after=S.now("vector"), n=8)
                S.op("scalar", lambda Sc: Sc.activation(
                    s_a[:], pE[:, 0:128], AF.Copy), after=S.now("tensor"))
                S.op("vector", lambda V: V.tensor_mul(
                    s_ag[:], s_a[:], s_gp1[:]), after=S.now("scalar"))
                if upto < 6:
                    continue
                S.op("tensor", lambda Tn: [
                    Tn.matmul(bk[m][:, 256:384],
                              s_Qm[:][32 * m:32 * m + 32, :],
                              s_ag[:][32 * m:32 * m + 32, :],
                              start=True, stop=True,
                              tile_position=(32 * m, 0),
                              skip_group_check=True)
                    for m in range(4)], after=S.now("vector"), n=4)
                S.op("vector", lambda V: [
                    V.tensor_tensor(s_MA[:][:, m * 128:m * 128 + 128],
                                    bk[m][:, 256:384], s_upi[:][:, 0:128],
                                    ALU.mult)
                    for m in range(4)], after=S.now("tensor"), n=4)
                ma_done = S.now("vector")
                if upto < 7:
                    continue
                S.op("tensor", lambda Tn, C0=C0, G0=G0: [
                    mmi for m in range(4) for mmi in [
                        Tn.matmul(pE[32 * m:32 * m + 32, 128:256],
                                  s_Vtm[:][:, m * 32:m * 32 + 32],
                                  s_MA[:][:, m * 128:m * 128 + 128],
                                  start=True, stop=False,
                                  tile_position=(0, 32 * m),
                                  skip_group_check=True),
                        Tn.matmul(pE[32 * m:32 * m + 32, 128:256], C0[m],
                                  s_ag[:][32 * m:32 * m + 32, :],
                                  start=False, stop=False,
                                  tile_position=(32 * m, 32 * m),
                                  skip_group_check=True),
                        Tn.matmul(pE[32 * m:32 * m + 32, 128:256], G0[m],
                                  s_Qg[:][32 * m:32 * m + 32, :],
                                  start=False, stop=False,
                                  tile_position=(32 * m, 32 * m),
                                  skip_group_check=True),
                        Tn.matmul(pE[32 * m:32 * m + 32, 128:256],
                                  s_U[:][:, m * 32:m * 32 + 32],
                                  s_Wt[:][:, m * 128:m * 128 + 128],
                                  start=False, stop=True,
                                  tile_position=(0, 32 * m),
                                  skip_group_check=True),
                    ]], after=ma_done, n=16)
                S.op("scalar", lambda Sc, g=g, csl=csl: Sc.activation(
                    o3[:, g, csl], pE[:, 128:256], AF.Copy),
                    after=S.now("tensor"))
                if upto == 9:
                    S.op("tensor", lambda Tn: [
                        mmi for m in range(4) for mmi in [
                            Tn.matmul(pE[32 * m:32 * m + 32, 256:288],
                                      s_KtmL[:][:, m * 32:m * 32 + 32],
                                      s_Ktm[:][:, m * 32:m * 32 + 32],
                                      start=True, stop=True,
                                      tile_position=(0, 32 * m),
                                      skip_group_check=True),
                            Tn.matmul(pE[32 * m:32 * m + 32, 288:320],
                                      s_QtmL[:][:, m * 32:m * 32 + 32],
                                      s_Vtm[:][:, m * 32:m * 32 + 32],
                                      start=True, stop=True,
                                      tile_position=(0, 32 * m),
                                      skip_group_check=True),
                            Tn.matmul(pE[32 * m:32 * m + 32, 320:352],
                                      s_KtmL[:][:, m * 32:m * 32 + 32],
                                      s_U[:][:, m * 32:m * 32 + 32],
                                      start=True, stop=True,
                                      tile_position=(0, 32 * m),
                                      skip_group_check=True),
                        ]], n=12)
                    S.op("scalar", lambda Sc, g=g: Sc.activation(
                        st3[:, g, :], pE[:, 256:352], AF.Copy),
                        after=S.now("tensor"))
                    st_read_mark = S.cnt["scalar"]
                    continue
                if upto == 10:
                    S.op("tensor", lambda Tn, A0=A0, C0=C0, G0=G0: [
                        mmi for m in range(4) for mmi in [
                            Tn.matmul(pE[32 * m:32 * m + 32, 256:288],
                                      s_geL[:][32 * m:32 * m + 32, :], A0[m],
                                      start=True, stop=True,
                                      tile_position=(32 * m, 32 * m),
                                      skip_group_check=True),
                            Tn.matmul(pE[32 * m:32 * m + 32, 288:320],
                                      s_geL[:][32 * m:32 * m + 32, :], C0[m],
                                      start=True, stop=True,
                                      tile_position=(32 * m, 32 * m),
                                      skip_group_check=True),
                            Tn.matmul(pE[32 * m:32 * m + 32, 320:352],
                                      s_geL[:][32 * m:32 * m + 32, :], G0[m],
                                      start=True, stop=True,
                                      tile_position=(32 * m, 32 * m),
                                      skip_group_check=True),
                        ]], n=12)
                    S.op("scalar", lambda Sc, g=g: Sc.activation(
                        st3[:, g, :], pE[:, 256:352], AF.Copy),
                        after=S.now("tensor"))
                    st_read_mark = S.cnt["scalar"]
                    continue
                if upto < 8:
                    continue
                S.op("tensor", lambda Tn: [
                    Tn.matmul(pE[32 * m:32 * m + 32, 256 + r * 32:288 + r * 32],
                              lhs_tm[:][:, m * 32:m * 32 + 32],
                              rhs_tm[:][:, m * 32:m * 32 + 32],
                              start=True, stop=True,
                              tile_position=(0, 32 * m),
                              skip_group_check=True)
                    for (r, lhs_tm, rhs_tm) in (
                        (0, s_KtmL, s_Ktm), (1, s_QtmL, s_Vtm),
                        (2, s_KtmL, s_U))
                    for m in range(4)], n=12)
                S.op("tensor", lambda Tn: [
                    Tn.matmul(pF[:, r * 32:r * 32 + 32], s_geLF[:],
                              st3[:, g, r * 32:r * 32 + 32],
                              start=True, stop=True, skip_group_check=True)
                    for r in range(3)], after={"vector": st_read_vec}, n=3)
                S.op("scalar", lambda Sc: Sc.activation(
                    s_tst[:], pE[:, 256:352], AF.Copy), after=S.now("tensor"))
                S.op("vector", lambda V, g=g: V.tensor_tensor(
                    st3[:, g, :], s_tst[:], pF[:, 0:96], ALU.add),
                    after=S.now("scalar"))
                st_read_mark = S.cnt["scalar"]
                st_read_vec = S.cnt["vector"]

        o_done = S.now("scalar")
        v_mix = S.cnt["vector"]
        for dc in range(NCH):
            bank = pA if (dc % 2 == 0) else pB
            S.op("tensor", lambda Tn, dc=dc, bank=bank: [
                Tn.matmul(bank[:], wo3[:, c2, dc * 128:dc * 128 + 128],
                          o3[:, c2, :], start=(c2 == 0), stop=(c2 == 3))
                for c2 in range(4)],
                after={**o_done, "vector": max(oproj_mark[dc % 2], v_mix)},
                n=4)
            S.op("vector", lambda V, dc=dc, bank=bank: V.tensor_add(
                x3[:, dc, :], x3[:, dc, :], bank[:]), after=S.now("tensor"))
            oproj_mark[dc % 2] = S.cnt["vector"]
        S.dma("sync", lambda E_, t=t: E_.dma_start(xap(x_out, t), x3),
              after=S.now("vector"), sem="sxo")

    S.wait_only("sync", S.now("sxo"))


def build_phase3(Ttok=T, upto=8, mixn=9999):
    nc = bass.Bass("TRN2", target_bir_lowering=False, debug=False)
    S = Sched()
    banks = _mk_banks(nc)
    x_in = nc.dram_tensor("x_in", [D, Ttok], F32, kind="ExternalInput").ap()
    x_out = nc.dram_tensor("x_out", [D, Ttok], F32, kind="ExternalOutput").ap()
    gen_phase3(nc, S, nc.alloc_sbuf_tensor, _mk_in(nc, ""), banks,
               x_in, x_out, Ttok, upto=upto, mixn=mixn)
    S.emit(nc)
    return nc


def prep_phase3_inputs(xcore, inp, fm=False):
    L = 128
    g = GAMMA
    p = np.arange(128, dtype=np.float64)
    upi = np.triu(np.ones((128, 128), np.float32))
    ups = np.triu(np.ones((128, 128), np.float32), 1)
    gp1 = np.broadcast_to((g ** (p + 1)).astype(np.float32), (128, 128))
    gm1 = np.broadcast_to((g ** -(p + 1)).astype(np.float32), (128, 128))
    gp0 = np.broadcast_to((g ** p).astype(np.float32), (128, 128))
    glc = (g ** (127 - p)).astype(np.float32)[:, None]
    e32 = np.zeros((128, 32), np.float32)
    for i in range(128):
        e32[i, i % 32] = 1.0
    geL = (e32 * (g ** 128)).astype(np.float32)
    return {
        "x_in": np.ascontiguousarray(np.asarray(xcore, np.float32)) if fm
        else np.ascontiguousarray(np.asarray(xcore, np.float32).T),
        "wq": np.ascontiguousarray(np.asarray(inp["L1_attn_Wq"], np.float32)),
        "wk": np.ascontiguousarray(np.asarray(inp["L1_attn_Wk"], np.float32)),
        "wv": np.ascontiguousarray(np.asarray(inp["L1_attn_Wv"], np.float32)),
        "wo": np.ascontiguousarray(np.asarray(inp["L1_attn_Wo"], np.float32)),
        "upi": np.ascontiguousarray(np.tile(upi, (1, 4)).astype(BF)),
        "ups": np.ascontiguousarray(np.tile(ups, (1, 4)).astype(BF)),
        "gp1": np.ascontiguousarray(gp1.astype(BF)),
        "gm1": np.ascontiguousarray(gm1.astype(BF)),
        "gp0": np.ascontiguousarray(gp0.astype(BF)),
        "glc": np.ascontiguousarray(glc),
        "e32": np.ascontiguousarray(e32.astype(BF)),
        "geL": np.ascontiguousarray(geL.astype(BF)),
        "geLF": np.ascontiguousarray((np.eye(128, dtype=np.float32) * (g ** 128)).astype(BF)),
    }


import contextlib


def build_fused(Ttok=T):
    nc = bass.Bass("TRN2", target_bir_lowering=False, debug=False)
    S = Sched()
    banks = _mk_banks(nc)
    x_in = nc.dram_tensor("x_in", [D, Ttok], F32, kind="ExternalInput").ap()
    xA = nc.dram_tensor("xA", [D, Ttok], F32, kind="Internal").ap()
    xB = nc.dram_tensor("xB", [D, Ttok], F32, kind="Internal").ap()
    xC = nc.dram_tensor("xC", [D, Ttok], F32, kind="Internal").ap()
    x_out = nc.dram_tensor("x_out", [D, Ttok], F32, kind="ExternalOutput").ap()
    chain = [x_in, xA, xB, xC, x_out]
    gens = [(gen_phase1, "p1_"), (gen_phase2, "p2_"),
            (gen_phase3, "p3_"), (gen_phase4, "p4_")]
    for i, (gen, pref) in enumerate(gens):
        S.pref = pref
        stack = contextlib.ExitStack()

        def sb(name, shape, dt_, stack=stack, pref=pref):
            return stack.enter_context(nc.sbuf_tensor(pref + name, shape, dt_))

        gen(nc, S, sb, _mk_in(nc, pref), banks, chain[i], chain[i + 1], Ttok)
        stack.close()
        if i < len(gens) - 1:
            S.barrier_all()
    S.pref = ""
    S.emit(nc)
    return nc


PREFS = ["p1_", "p2_", "p3_", "p4_"]
PREPS_ALL = None


def fused_in_map(xcore_fm, inputs):
    m = {"x_in": np.ascontiguousarray(xcore_fm)}
    preps = [prep_phase1_inputs, prep_phase2_inputs, prep_phase3_inputs,
             prep_phase4_inputs]
    for pref, prep in zip(PREFS, preps):
        d = prep(xcore_fm, inputs, fm=True)
        for k, v in d.items():
            if k == "x_in":
                continue
            m[pref + k] = v
    return m


def kernel(**inputs):
    x = np.asarray(inputs["L0_attn_x"], np.float32)
    nc = build_fused()
    in_maps = [fused_in_map(x[c].T, inputs) for c in range(8)]
    res = run_bass_kernel_spmd(nc, in_maps, core_ids=list(range(8)))
    return np.stack([np.asarray(r["x_out"], np.float32).T
                     for r in res.results])


def kernel_unfused(**inputs):
    x = np.asarray(inputs["L0_attn_x"], np.float32)
    ncs = [build_phase1(), build_phase2(), build_phase3(), build_phase4()]
    preps = [prep_phase1_inputs, prep_phase2_inputs, prep_phase3_inputs,
             prep_phase4_inputs]
    cur = [np.ascontiguousarray(x[c].T) for c in range(8)]
    for nc, prep in zip(ncs, preps):
        in_maps = [prep(cur[c], inputs, fm=True) for c in range(8)]
        res = run_bass_kernel_spmd(nc, in_maps, core_ids=list(range(8)))
        cur = [np.asarray(r["x_out"], np.float32) for r in res.results]
    return np.stack([c.T for c in cur])


# version 10
# speedup vs baseline: 1.0196x; 1.0196x over previous
import numpy as np
import ml_dtypes
import concourse.bass as bass
import concourse.mybir as mybir
from concourse.bass_utils import run_bass_kernel_spmd

F32 = mybir.dt.float32
BF16 = mybir.dt.bfloat16
AF = mybir.ActivationFunctionType
ALU = mybir.AluOpType
AX = mybir.AxisListType
BF = ml_dtypes.bfloat16

D = 1024
T = 4096
NCH = D // 128
LN_EPS = 1e-5
GEPS = float(np.finfo(np.float32).eps)

ENGS = ("sync", "gpsimd", "tensor", "vector", "scalar")


class Sched:

    def __init__(self, paranoid=False):
        self.items = {e: [] for e in ENGS}
        self.cnt = {e: 0 for e in ENGS}
        self.extra = set()
        self.paranoid = paranoid
        self.pref = ""

    def _p(self, name):
        return name if name in ENGS else self.pref + name

    def now(self, *engs):
        if not engs:
            engs = ENGS
        return {self._p(e): self.cnt.get(self._p(e), 0) for e in engs}

    def op(self, eng, fn, after=None, inc=1, n=1, sem=None):
        sem = self._p(sem) if sem else eng
        if sem not in ENGS:
            self.extra.add(sem)
            self.cnt.setdefault(sem, 0)
        waits = dict(after or {})
        if self.paranoid:
            waits = {**{k: v for k, v in self.cnt.items() if k != eng or eng in ("sync", "gpsimd")}, **waits}
        self.items[eng].append((waits, fn, inc, n, sem))
        self.cnt[sem] += inc * (n if inc == 16 else 1)
        return self.cnt[sem]

    def dma(self, eng, fn, after=None, n=1, sem=None):
        return self.op(eng, fn, after=after, inc=16, n=n, sem=sem)

    def wait_only(self, eng, after):
        self.items[eng].append((dict(after), None, 0, 0, eng))

    def barrier_all(self):
        snap = dict(self.cnt)
        for e in ENGS:
            w = {k: v for k, v in snap.items() if v > 0}
            self.items[e].append((w, None, 0, 0, e))

    def emit(self, nc):
        names = list(ENGS) + sorted(self.extra)
        sems = {e: nc.alloc_semaphore("sem_" + e) for e in names}
        blk = nc.Block()
        blk.__enter__()

        def build(eng_name):
            self_sync = eng_name in ("vector", "scalar")
            def body(E):
                waited = {k: 0 for k in names}
                own = 0
                for waits, fn, inc, ndecl, semn in self.items[eng_name]:
                    if self_sync and fn is not None and own > 0:
                        if waited[eng_name] < own:
                            E.wait_ge(sems[eng_name], own)
                            waited[eng_name] = own
                    for k in names:
                        v = waits.get(k, 0)
                        if v <= waited[k]:
                            continue
                        if k == eng_name and eng_name not in ("sync", "gpsimd") \
                                and not self_sync:
                            continue
                        if k == eng_name and self_sync and v <= own:
                            continue
                        E.wait_ge(sems[k], v)
                        waited[k] = v
                    if fn is None:
                        continue
                    out = fn(E)
                    instrs = out if isinstance(out, (list, tuple)) else [out]
                    assert len(instrs) == ndecl, (eng_name, len(instrs), ndecl)
                    if inc == 16:
                        for i in instrs:
                            i.then_inc(sems[semn], inc)
                    else:
                        instrs[-1].then_inc(sems[semn], inc)
                        if semn == eng_name:
                            own += inc
            return body

        for e in ENGS:
            if self.items[e]:
                getattr(blk, e)(build(e))
        blk.__exit__(None, None, None)
        return sems




def _mk_banks(nc):
    ps = nc.alloc_psum_tensor
    return ([ps(f"bk{i}", [128, 512], F32) for i in range(6)],
            ps("bs1", [1, 512], F32), ps("bs2", [1, 512], F32))


def _mk_in(nc, pref, kind="ExternalInput"):
    def f(name, shape, dtype):
        return nc.dram_tensor(pref + name, shape, dtype, kind=kind).ap()
    return f

def gen_phase2(nc, S, sb, din, banks, x_in, x_out, Ttok):
    E, H = 8, 2048
    QT = min(1024, Ttok)
    NQ = Ttok // QT
    NTT = QT // 512
    NJ = NQ * E

    gwT = din("gwT", [D, E], F32)
    gb = din("gb", [1, E], F32)
    rmsw = din("rmsw", [1, E], F32)
    w1d = din("w1", [E, D, H], F32)
    b1d = din("b1", [E, 1, H], F32)
    w2d = din("w2", [E, H, D], F32)
    b2d = din("b2", [E, 1, D], F32)
    eyef = din("eyef", [128, 128], BF16)
    sel4 = din("sel4", [4, 512], BF16)
    s_xq = sb("s_xq", [128, NCH * QT], F32)
    s_xh = sb("s_xh", [128, NCH * QT], BF16)
    s_h = sb("s_h", [128, 16 * 512], BF16)
    s_xf = sb("s_xf", [128, NCH * 512], F32)
    s_w1 = [sb(f"s_w1_{i}", [128, NCH * H], BF16) for i in range(2)]
    s_w2 = sb("s_w2", [128, 16 * D], BF16)
    s_b1 = [sb(f"s_b1_{i}", [1, H], BF16) for i in range(2)]
    s_b2 = sb("s_b2", [1, D], BF16)
    s_gwT = sb("s_gwT", [128, NCH * E], BF16)
    s_gbr = sb("s_gbr", [1, E], BF16)
    s_rmsb = sb("s_rmsb", [128, E], F32)
    s_rmsr = sb("s_rmsr", [1, E], F32)
    s_rmsbr = sb("s_rmsbr", [1, E], BF16)
    s_ones = sb("s_ones", [128, 512], BF16)
    s_onesf = sb("s_onesf", [1, 128], F32)
    s_eyef = sb("s_eyef", [128, 128], BF16)
    s_sel4 = sb("s_sel4", [4, 512], BF16)
    s_sq = sb("s_sq", [128, 2 * 512], BF16)
    s_st = sb("s_st", [1, 4 * 512], F32)
    s_stb = sb("s_stb", [1, 2 * 512], BF16)
    s_g = sb("s_g", [128, 48], F32)
    s_ng4 = sb("s_ng4", [128, 4], BF16)
    s_ngT = sb("s_ngT", [4, 128], BF16)
    s_ngb = sb("s_ngb", [128, 2 * 512], F32)
    s_zt = sb("s_zt", [128, 1], F32)
    s_tmp = sb("s_tmp", [128, 512], F32)
    s_eps = sb("s_eps", [1, 1], F32)
    s_geps = sb("s_geps", [128, 1], F32)

    (bk6, pS1, pS2) = banks
    pH = [bk6[0], bk6[1]]
    pY = [bk6[2], bk6[3]]
    pBC = bk6[4]
    pG = bk6[5]

    def xap(dram, q):
        return dram.rearrange("(c p) t -> p c t", p=128)[:, :, q * QT:(q + 1) * QT]

    xq3 = s_xq[:].rearrange("p (c t) -> p c t", t=QT)
    xh3 = s_xh[:].rearrange("p (c t) -> p c t", t=QT)
    xf3 = s_xf[:].rearrange("p (c t) -> p c t", t=512)
    h3 = s_h[:].rearrange("p (c t) -> p c t", t=512)
    w1v = [s_w1[i][:].rearrange("p (c h) -> p c h", h=H) for i in range(2)]
    w2v = s_w2[:].rearrange("p (c d) -> p c d", d=D)
    gw3 = s_gwT[:].rearrange("p (c e) -> p c e", e=E)
    ey = s_eyef[:]

    S.dma("sync", lambda E_: [
        E_.dma_start(s_rmsr[:], rmsw),
        E_.dma_start(s_eyef[:], eyef),
        E_.dma_start(s_sel4[:], sel4),
    ], n=3, sem="cst")
    S.dma("gpsimd", lambda G: [
        G.dma_start(s_gwT[:].rearrange("p (c e) -> p c e", e=E),
                    gwT.rearrange("(c p) e -> p c e", p=128)),
        G.dma_start(s_gbr[:], gb),
    ], n=2, sem="gwc")
    wait_gw = S.now("gwc")
    S.op("vector", lambda V: [
        V.memset(s_ones[:], 1.0),
        V.memset(s_onesf[:], 1.0),
        V.memset(s_eps[:], LN_EPS),
        V.memset(s_geps[:], GEPS),
        V.tensor_copy(s_rmsbr[:], s_rmsr[:]),
    ], after=S.now("cst"), n=5)
    S.op("tensor", lambda Tn: Tn.matmul(
        pG[:, 0:E], s_ones[:][0:1, 0:128], s_rmsbr[:], start=True, stop=True),
        after=S.now("cst", "vector"))
    S.op("vector", lambda V: V.tensor_copy(s_rmsb[:], pG[:, 0:E]),
         after=S.now("tensor"))
    rmsb_done = S.now("vector")

    pe_after_h = {}
    pe_after_y = {}
    m_w1 = {}
    m_w2 = {}

    def load_w1(j):
        q, e = divmod(j, E)
        b = e % 2
        aft = {}
        if j >= 2:
            aft["tensor"] = pe_after_h[j - 2]
        w1r = w1d[e].rearrange("(c p) h -> p c h", p=128)
        s1 = w1v[b]
        S.dma("gpsimd", lambda G: [
            G.dma_start(s1[:, :, 0:1024], w1r[:, :, 0:1024]),
            G.dma_start(s1[:, :, 1024:2048], w1r[:, :, 1024:2048]),
            G.dma_start(s_b1[b][:], b1d[e]),
        ], after=aft, n=3, sem="w1_%d" % b)
        m_w1[j] = {S._p("w1_%d" % b): S.cnt[S._p("w1_%d" % b)]}

    def load_w2(j):
        q, e = divmod(j, E)
        aft = {}
        if j >= 1:
            aft["tensor"] = pe_after_y[j - 1]
        w2r = w2d[e].rearrange("(c p) d -> p c d", p=128)
        S.dma("gpsimd", lambda G: [
            G.dma_start(w2v[:, 0:8, :], w2r[:, 0:8, :]),
            G.dma_start(s_b2[:], b2d[e]),
            G.dma_start(w2v[:, 8:16, :], w2r[:, 8:16, :]),
        ], after=aft, n=3, sem="w2s")
        m_w2[j] = {S._p("w2s"): S.cnt[S._p("w2s")]}

    load_w1(0)
    load_w2(0)
    load_w1(1)

    vec_xhat = {}
    vec_ngb = {}
    prev_vec_smalls = 0
    mm_mark = {0: 0, 1: 0}
    gelu_mark = {0: 0, 1: 0}
    yev_mark = {0: 0, 1: 0}

    for q in range(NQ):
        S.dma("sync", lambda E_, q=q: E_.dma_start(xq3, xap(x_in, q)),
              after=S.now("vector", "sxo"), sem="sx")
        m_x = S.now("sx")

        for tt in range(NTT):
            tsl = slice(tt * 512, tt * 512 + 512)
            for c in range(NCH):
                S.op("scalar", lambda Sc, c=c, tsl=tsl: [
                    Sc.activation(xh3[:, c, tsl], xq3[:, c, tsl], AF.Copy),
                    Sc.activation(s_sq[:][:, (c % 2) * 512:(c % 2) * 512 + 512],
                                  xq3[:, c, tsl], AF.Square),
                ], after={"tensor": mm_mark[c % 2], **m_x}, n=2)
                S.op("tensor", lambda Tn, c=c, tsl=tsl: [
                    Tn.matmul(pS1[:], s_ones[:][:, 0:1], xh3[:, c, tsl],
                              start=(c == 0), stop=(c == NCH - 1)),
                    Tn.matmul(pS2[:], s_ones[:][:, 0:1],
                              s_sq[:][:, (c % 2) * 512:(c % 2) * 512 + 512],
                              start=(c == 0), stop=(c == NCH - 1)),
                ], after={**S.now("scalar"), "vector": prev_vec_smalls}, n=2)
                mm_mark[c % 2] = S.cnt["tensor"]
            mu = s_st[:][:, 0:512]
            ex2 = s_st[:][:, 512:1024]
            var = s_st[:][:, 1024:1536]
            mub = s_stb[:][:, 0:512]
            rsb = s_stb[:][:, 512:1024]
            m2 = s_st[:][:, 1536:2048]
            S.op("vector", lambda V: [
                V.tensor_scalar_mul(mu, pS1[:], 1.0 / D),
                V.tensor_scalar_mul(ex2, pS2[:], 1.0 / D),
            ], after=S.now("tensor"), n=2)
            S.op("vector", lambda V: [
                V.tensor_mul(m2, mu, mu),
                V.tensor_copy(mub, mu),
            ], n=2)
            S.op("vector", lambda V: V.tensor_sub(var, ex2, m2))
            prev_vec_smalls = S.cnt["vector"]
            S.op("scalar", lambda Sc: Sc.activation(var, var, AF.Sqrt, bias=s_eps[:]),
                 after=S.now("vector"))
            S.op("vector", lambda V: V.reciprocal(var, var),
                 after=S.now("scalar"))
            S.op("vector", lambda V: V.tensor_copy(rsb, var))
            S.op("tensor", lambda Tn: [
                Tn.matmul(pBC[:], s_ones[:][0:1, 0:128], mub, start=True, stop=True),
                Tn.matmul(pG[:], s_ones[:][0:1, 0:128], rsb, start=True, stop=True),
            ], after=S.now("vector"), n=2)
            for c in range(NCH):
                S.op("vector", lambda V, c=c, tsl=tsl:
                     V.tensor_sub(s_tmp[:], xq3[:, c, tsl], pBC[:]),
                     after=S.now("tensor"))
                S.op("vector", lambda V, c=c, tsl=tsl:
                     V.tensor_tensor(xh3[:, c, tsl], s_tmp[:], pG[:], ALU.mult))
            vec_xhat[(q, tt)] = S.cnt["vector"]

            sqo = s_g[:][:, 0:8]
            ms = s_g[:][:, 8:9]
            rstg = s_g[:][:, 9:10]
            lg = s_g[:][:, 16:24]
            mx = s_g[:][:, 24:25]
            nmx = s_g[:][:, 25:26]
            zs = s_g[:][:, 26:27]
            zz = s_g[:][:, 27:28]
            exo = s_g[:][:, 32:40]
            for tg in range(4):
                gsl = slice(tt * 512 + tg * 128, tt * 512 + tg * 128 + 128)
                S.op("tensor", lambda Tn, gsl=gsl: [
                    Tn.matmul(pG[:, 0:E], xh3[:, c2, gsl], gw3[:, c2, :],
                              start=(c2 == 0), stop=False)
                    for c2 in range(NCH)] + [
                    Tn.matmul(pG[:, 0:E], s_ones[:][0:1, 0:128], s_gbr[:],
                              start=False, stop=True)],
                    after={**S.now("vector", "scalar"), **wait_gw}, n=NCH + 1)
                S.op("scalar", lambda Sc: Sc.activation(
                    sqo, pG[:, 0:E], AF.Square, accum_out=ms),
                    after=S.now("tensor"))
                S.op("scalar", lambda Sc: Sc.activation(
                    rstg, ms, AF.Sqrt, bias=s_geps[:], scale=1.0 / E))
                S.op("vector", lambda V: V.reciprocal(rstg, rstg),
                     after={**S.now("scalar"), **rmsb_done})
                S.op("vector", lambda V: V.scalar_tensor_tensor(
                    lg, pG[:, 0:E], rstg, s_rmsb[:], ALU.mult, ALU.mult))
                S.op("vector", lambda V: V.reduce_max(mx, lg, AX.X))
                S.op("vector", lambda V: V.tensor_scalar_mul(nmx, mx, -2.0))
                S.op("scalar", lambda Sc: Sc.activation(
                    exo, lg, AF.Exp, bias=nmx, scale=2.0, accum_out=zs),
                    after=S.now("vector"))
                S.op("vector", lambda V, tg=tg: [
                    V.tensor_scalar(zz, zs, 1e-6, 1.0, ALU.mult, ALU.add),
                    V.tensor_scalar_mul(s_zt[:], zs, 1e-6),
                ], after=S.now("scalar"), n=2)
                S.op("vector", lambda V: V.reciprocal(zz, zz))
                S.op("vector", lambda V, tg=tg: V.tensor_mul(
                    s_ng4[:][:, tg:tg + 1], s_zt[:], zz))
            S.op("tensor", lambda Tn: Tn.matmul(
                pBC[0:4, 0:128], s_ng4[:], ey, start=True, stop=True),
                after=S.now("vector"))
            S.op("vector", lambda V: V.tensor_copy(s_ngT[:], pBC[0:4, 0:128]),
                 after=S.now("tensor"))
            for tg in range(4):
                S.op("tensor", lambda Tn, tg=tg: Tn.matmul(
                    pBC[:, 0:128], s_sel4[:][:, tg * 128:tg * 128 + 128],
                    s_ngT[:], start=True, stop=True), after=S.now("vector"))
                S.op("vector", lambda V, tt=tt, tg=tg: V.tensor_copy(
                    s_ngb[:][:, tt * 512 + tg * 128:tt * 512 + tg * 128 + 128],
                    pBC[:, 0:128]), after=S.now("tensor"))
            vec_ngb[(q, tt)] = S.cnt["vector"]

        for e in range(E):
            j = q * E + e
            b = e % 2
            for tt in range(NTT):
                tsl = slice(tt * 512, tt * 512 + 512)
                for hc in range(16):
                    bank = pH[hc % 2]
                    aft = {"scalar": gelu_mark[hc % 2], **m_w1[j],
                           "vector": vec_xhat[(q, tt)]}
                    S.op("tensor", lambda Tn, hc=hc, bank=bank, tsl=tsl, b=b: [
                        Tn.matmul(bank[:], w1v[b][:, c2, hc * 128:hc * 128 + 128],
                                  xh3[:, c2, tsl], start=(c2 == 0), stop=False)
                        for c2 in range(NCH)] + [
                        Tn.matmul(bank[:], s_b1[b][:][:, hc * 128:hc * 128 + 128],
                                  s_ones[:][0:1, 0:512], start=False, stop=True)],
                        after=aft, n=NCH + 1)
                    S.op("scalar", lambda Sc, hc=hc, bank=bank: Sc.activation(
                        h3[:, hc, :], bank[:], AF.Gelu), after=S.now("tensor"))
                    gelu_mark[hc % 2] = S.cnt["scalar"]
                gelu_done = S.cnt["scalar"]
                pe_after_h[j] = S.cnt["tensor"]
                for dc in range(NCH):
                    bank = pY[dc % 2]
                    aft = {"vector": yev_mark[dc % 2], "scalar": gelu_done,
                           **m_w2[j]}
                    S.op("tensor", lambda Tn, dc=dc, bank=bank: [
                        Tn.matmul(bank[:], w2v[:, hc2, dc * 128:dc * 128 + 128],
                                  h3[:, hc2, :], start=(hc2 == 0), stop=False)
                        for hc2 in range(16)] + [
                        Tn.matmul(bank[:], s_b2[:][:, dc * 128:dc * 128 + 128],
                                  s_ones[:][0:1, 0:512], start=False, stop=True)],
                        after=aft, n=17)
                    if e == 0:
                        S.op("vector", lambda V, dc=dc, tsl=tsl, bank=bank:
                             V.tensor_copy(xq3[:, dc, tsl], bank[:]),
                             after=S.now("tensor"))
                    else:
                        S.op("vector", lambda V, dc=dc, tsl=tsl, bank=bank:
                             V.tensor_add(xq3[:, dc, tsl], xq3[:, dc, tsl],
                                          bank[:]),
                             after=S.now("tensor"))
                    yev_mark[dc % 2] = S.cnt["vector"]
            pe_after_y[j] = S.cnt["tensor"]
            if j + 1 < NJ:
                load_w2(j + 1)
            if j + 2 < NJ:
                load_w1(j + 2)

        for tt in range(NTT):
            tsl = slice(tt * 512, tt * 512 + 512)
            S.dma("sync", lambda E_, q=q, tsl=tsl: E_.dma_start(
                xf3, xap(x_in, q)[:, :, tsl]),
                after=S.now("vector"), sem="sxf")
            m_xf = S.now("sxf")
            for dc in range(NCH):
                S.op("vector", lambda V, dc=dc, tt=tt, tsl=tsl:
                     V.tensor_tensor(s_tmp[:], xq3[:, dc, tsl],
                                     s_ngb[:][:, tt * 512:tt * 512 + 512],
                                     ALU.mult),
                     after={**S.now("tensor"), **m_xf})
                S.op("vector", lambda V, dc=dc, tsl=tsl:
                     V.tensor_sub(s_tmp[:], xq3[:, dc, tsl], s_tmp[:]))
                S.op("vector", lambda V, dc=dc, tsl=tsl:
                     V.tensor_add(xq3[:, dc, tsl], s_tmp[:], xf3[:, dc, :]))
        S.dma("sync", lambda E_, q=q: E_.dma_start(xap(x_out, q), xq3),
              after=S.now("vector"), sem="sxo")

    S.wait_only("sync", S.now("sxo"))


def build_phase2(Ttok=T, paranoid=False):
    nc = bass.Bass("TRN2", target_bir_lowering=False, debug=False)
    S = Sched(paranoid)
    banks = _mk_banks(nc)
    x_in = nc.dram_tensor("x_in", [D, Ttok], F32, kind="ExternalInput").ap()
    x_out = nc.dram_tensor("x_out", [D, Ttok], F32, kind="ExternalOutput").ap()
    gen_phase2(nc, S, nc.alloc_sbuf_tensor, _mk_in(nc, ""), banks,
               x_in, x_out, Ttok)
    S.emit(nc)
    return nc


def prep_phase2_inputs(xcore, inp, fm=False):
    sel4 = np.zeros((4, 512), np.float32)
    for r in range(4):
        sel4[r, 128 * r:128 * r + 128] = 1.0
    return {
        "x_in": np.ascontiguousarray(np.asarray(xcore, np.float32)) if fm
        else np.ascontiguousarray(np.asarray(xcore, np.float32).T),
        "gwT": np.ascontiguousarray(np.asarray(inp["L0_ffn_gate_w"], np.float32).T),
        "gb": np.ascontiguousarray(np.asarray(inp["L0_ffn_gate_b"], np.float32)[None, :]),
        "rmsw": np.ascontiguousarray(np.asarray(inp["L0_ffn_rms_w"], np.float32)[None, :]),
        "w1": np.ascontiguousarray(np.asarray(inp["L0_ffn_w1"], np.float32)),
        "b1": np.ascontiguousarray(np.asarray(inp["L0_ffn_b1"], np.float32)[:, None, :]),
        "w2": np.ascontiguousarray(np.asarray(inp["L0_ffn_w2"], np.float32)),
        "b2": np.ascontiguousarray(np.asarray(inp["L0_ffn_b2"], np.float32)[:, None, :]),
        "eyef": np.eye(128, dtype=np.float32).astype(BF),
        "sel4": sel4.astype(BF),
    }


def gen_phase1(nc, S, sb, din, banks, x_in, x_out, Ttok):
    NT = Ttok // 512
    wkT = din("wkT", [D, D], F32)
    wvT = din("wvT", [D, D], F32)
    woT = din("woT", [D, D], F32)
    bkр = din("bk", [1, D], F32)
    bvр = din("bv", [1, D], F32)
    boр = din("bo", [1, D], F32)
    lngd = din("lng", [128, NCH], F32)
    lnbd = din("lnb", [128, NCH], F32)
    epspos = din("epspos", [1, Ttok], F32)
    s_x = sb("s_x", [128, NCH * 512], F32)
    s_xh = sb("s_xh", [128, NCH * 512], BF16)
    s_k = sb("s_k", [128, NCH * 512], BF16)
    s_v = sb("s_v", [128, NCH * 512], BF16)
    s_bnd = sb("s_bnd", [128, NCH * 512], BF16)
    s_mem = sb("s_mem", [128, NCH * 512], BF16)
    s_rs = sb("s_rs", [128, 2 * 512], F32)
    s_wk = sb("s_wk", [128, NCH * D], BF16)
    s_wv = sb("s_wv", [128, NCH * D], BF16)
    s_wo = sb("s_wo", [128, NCH * D], BF16)
    s_bk = sb("s_bk", [1, D], BF16)
    s_bv = sb("s_bv", [1, D], BF16)
    s_bo = sb("s_bo", [1, D], BF16)
    s_lng = sb("s_lng", [128, NCH], F32)
    s_lnb = sb("s_lnb", [128, NCH], F32)
    s_eps = sb("s_epsP", [1, 1], F32)
    s_epsp = sb("s_epsp", [1, Ttok], F32)
    s_ones = sb("s_ones", [128, 512], BF16)
    s_onesf = sb("s_onesf", [128, 512], F32)
    s_carry = sb("s_carry", [128, NCH], F32)
    s_sq = sb("s_sq", [128, 2 * 512], BF16)
    s_st = sb("s_st", [1, 4 * 512], F32)
    s_stb = sb("s_stb", [1, 2 * 512], BF16)
    s_t1 = sb("s_t1", [128, 512], F32)
    s_t2 = sb("s_t2", [128, 512], F32)

    (bk6, pS1, pS2) = banks
    pA, pB, pBC, pG = bk6[0], bk6[1], bk6[4], bk6[5]

    def xap(dram, t):
        return dram.rearrange("(c p) n -> p c n", p=128)[:, :, t * 512:(t + 1) * 512]

    x3 = s_x[:].rearrange("p (c n) -> p c n", n=512)
    xh3 = s_xh[:].rearrange("p (c n) -> p c n", n=512)
    k3 = s_k[:].rearrange("p (c n) -> p c n", n=512)
    v3 = s_v[:].rearrange("p (c n) -> p c n", n=512)
    b3 = s_bnd[:].rearrange("p (c n) -> p c n", n=512)
    m3 = s_mem[:].rearrange("p (c n) -> p c n", n=512)
    rs2 = s_rs[:].rearrange("p (g n) -> p g n", n=512)
    wk3 = s_wk[:].rearrange("p (c j) -> p c j", j=D)
    wv3 = s_wv[:].rearrange("p (c j) -> p c j", j=D)
    wo3 = s_wo[:].rearrange("p (c j) -> p c j", j=D)

    S.dma("sync", lambda E_: [
        E_.dma_start(s_epsp[:], epspos),
        E_.dma_start(s_lng[:], lngd),
        E_.dma_start(s_lnb[:], lnbd),
    ], n=3, sem="cst")
    S.dma("gpsimd", lambda G: [
        G.dma_start(s_wk[:].rearrange("p (c j) -> p c j", j=D),
                    wkT.rearrange("(c p) j -> p c j", p=128)),
        G.dma_start(s_wv[:].rearrange("p (c j) -> p c j", j=D),
                    wvT.rearrange("(c p) j -> p c j", p=128)),
        G.dma_start(s_wo[:].rearrange("p (c j) -> p c j", j=D),
                    woT.rearrange("(c p) j -> p c j", p=128)),
        G.dma_start(s_bk[:], bkр),
        G.dma_start(s_bv[:], bvр),
        G.dma_start(s_bo[:], boр),
    ], n=6, sem="wts")
    m_wts = S.now("wts")
    S.op("vector", lambda V: [
        V.memset(s_ones[:], 1.0),
        V.memset(s_onesf[:], 1.0),
        V.memset(s_eps[:], LN_EPS),
        V.memset(s_carry[:], 0.0),
    ], n=4)
    init_v = S.now("vector")

    mm_mark = {0: 0, 1: 0}
    evac_mark = {0: 0, 1: 0}
    oproj_mark = {0: 0, 1: 0}

    def ln_block(tsl_ap, dst3, epsrow, lng=None, lnb=None, src3=None):
        mu = s_st[:][:, 0:512]
        ex2 = s_st[:][:, 512:1024]
        var = s_st[:][:, 1024:1536]
        m2 = s_st[:][:, 1536:2048]
        mub = s_stb[:][:, 0:512]
        rsb = s_stb[:][:, 512:1024]
        v_src = S.now("vector")
        for c in range(NCH):
            S.op("scalar", lambda Sc, c=c: Sc.activation(
                s_sq[:][:, (c % 2) * 512:(c % 2) * 512 + 512],
                src3[:, c, :], AF.Square),
                after={"tensor": mm_mark[c % 2], **v_src})
            S.op("tensor", lambda Tn, c=c: [
                Tn.matmul(pS1[:], s_ones[:][:, 0:1], src3[:, c, :],
                          start=(c == 0), stop=(c == NCH - 1)),
                Tn.matmul(pS2[:], s_ones[:][:, 0:1],
                          s_sq[:][:, (c % 2) * 512:(c % 2) * 512 + 512],
                          start=(c == 0), stop=(c == NCH - 1)),
            ], after=S.now("scalar"), n=2)
            mm_mark[c % 2] = S.cnt["tensor"]
        S.op("vector", lambda V: [
            V.tensor_scalar_mul(mu, pS1[:], 1.0 / D),
            V.tensor_scalar_mul(ex2, pS2[:], 1.0 / D),
        ], after=S.now("tensor"), n=2)
        S.op("vector", lambda V: [
            V.tensor_mul(m2, mu, mu),
            V.tensor_copy(mub, mu),
        ], n=2)
        S.op("vector", lambda V: V.tensor_sub(var, ex2, m2))
        S.op("vector", lambda V: V.tensor_add(var, var, epsrow),
             after=S.now("cst"))
        S.op("scalar", lambda Sc: Sc.activation(var, var, AF.Sqrt),
             after=S.now("vector"))
        S.op("vector", lambda V: V.reciprocal(var, var), after=S.now("scalar"))
        S.op("vector", lambda V: V.tensor_copy(rsb, var))
        S.op("tensor", lambda Tn: [
            Tn.matmul(pBC[:], s_ones[:][0:1, 0:128], mub, start=True, stop=True),
            Tn.matmul(pG[:], s_ones[:][0:1, 0:128], rsb, start=True, stop=True),
        ], after=S.now("vector"), n=2)
        for c in range(NCH):
            S.op("vector", lambda V, c=c: V.tensor_tensor(
                s_t1[:], src3[:, c, :] if tsl_ap is None else tsl_ap[c],
                pBC[:], ALU.subtract), after=S.now("tensor"))
            if lng is None:
                S.op("vector", lambda V, c=c: V.tensor_tensor(
                    dst3[:, c, :], s_t1[:], pG[:], ALU.mult))
            else:
                S.op("vector", lambda V, c=c: V.tensor_tensor(
                    s_t2[:], s_t1[:], pG[:], ALU.mult))
                S.op("vector", lambda V, c=c: V.tensor_scalar(
                    dst3[:, c, :], s_t2[:], lng[:, c:c + 1], lnb[:, c:c + 1],
                    ALU.mult, ALU.add), after=S.now("cst"))

    QM_BOUND = [
        (0, [(0, 0, 1), (1, 1, -1), (2, 2, -1), (3, 3, -1)]),
        (1, [(0, 1, 1), (1, 0, 1), (2, 3, 1), (3, 2, -1)]),
        (2, [(0, 2, 1), (1, 3, -1), (2, 0, 1), (3, 1, 1)]),
        (3, [(0, 3, 1), (1, 2, 1), (2, 1, -1), (3, 0, 1)]),
    ]
    QM_RETR = [
        (0, [(0, 0, 1), (1, 1, 1), (2, 2, 1), (3, 3, 1)]),
        (1, [(0, 1, -1), (1, 0, 1), (2, 3, -1), (3, 2, 1)]),
        (2, [(0, 2, -1), (1, 3, 1), (2, 0, 1), (3, 1, -1)]),
        (3, [(0, 3, -1), (1, 2, -1), (2, 1, 1), (3, 0, 1)]),
    ]

    def quat_mul_emit(table, a3, bb3, out3, g):
        for oc_, terms in table:
            ot = out3[:, 2 * oc_ + g, :]
            terms = sorted(terms, key=lambda t_: -t_[2])
            first = True
            for (ac, bc, sgn) in terms:
                at = a3[:, 2 * ac + g, :]
                bt = bb3[:, 2 * bc + g, :]
                if first:
                    assert sgn == 1
                    S.op("vector", lambda V, ot=ot, at=at, bt=bt:
                         V.tensor_mul(ot, at, bt))
                    first = False
                else:
                    S.op("vector", lambda V, at=at, bt=bt:
                         V.tensor_mul(s_t1[:], at, bt))
                    op = ALU.add if sgn == 1 else ALU.subtract
                    S.op("vector", lambda V, ot=ot, op=op: V.tensor_tensor(
                        ot, ot, s_t1[:] if op == ALU.add else s_t1[:], op))

    for t in range(NT):
        S.dma("sync", lambda E_, t=t: E_.dma_start(x3, xap(x_in, t)),
              after=S.now("vector", "sxo"), sem="sx")
        m_x = S.now("sx")
        for c in range(NCH):
            S.op("scalar", lambda Sc, c=c: Sc.activation(
                xh3[:, c, :], x3[:, c, :], AF.Copy),
                after={**m_x, **S.now("vector")})
        cpy_done = S.now("scalar")
        S.wait_only("vector", cpy_done)
        ln_block(None, xh3, s_eps[:].broadcast_to([1, 512]), src3=xh3)
        xh_done = S.now("vector")

        for (wsrc, bsrc, dst3) in ((wk3, s_bk, k3), (wv3, s_bv, v3)):
            for oc in range(NCH):
                bank = pA if (oc % 2 == 0) else pB
                S.op("tensor", lambda Tn, oc=oc, bank=bank, wsrc=wsrc, bsrc=bsrc: [
                    Tn.matmul(bank[:], wsrc[:, c2, oc * 128:oc * 128 + 128],
                              xh3[:, c2, :], start=(c2 == 0), stop=False)
                    for c2 in range(NCH)] + [
                    Tn.matmul(bank[:], bsrc[:][:, oc * 128:oc * 128 + 128],
                              s_ones[:][0:1, 0:512], start=False, stop=True)],
                    after={**m_wts,
                           "vector": max(xh_done["vector"], oproj_mark[oc % 2]),
                           "scalar": evac_mark[oc % 2]}, n=NCH + 1)
                S.op("scalar", lambda Sc, oc=oc, bank=bank, dst3=dst3:
                     Sc.activation(dst3[:, oc, :], bank[:], AF.Copy),
                     after=S.now("tensor"))
                evac_mark[oc % 2] = S.cnt["scalar"]
        kv_done = S.now("scalar")

        for g in range(2):
            rs = rs2[:, g, :]
            S.op("scalar", lambda Sc, g=g: Sc.activation(
                s_t1[:], k3[:, 0 * 2 + g, :], AF.Square),
                after={**kv_done, **S.now("vector")})
            S.op("scalar", lambda Sc, g=g: Sc.activation(
                s_t2[:], k3[:, 1 * 2 + g, :], AF.Square))
            S.op("vector", lambda V: V.tensor_add(s_t1[:], s_t1[:], s_t2[:]),
                 after=S.now("scalar"))
            S.op("scalar", lambda Sc, g=g: Sc.activation(
                s_t2[:], k3[:, 2 * 2 + g, :], AF.Square), after=S.now("vector"))
            S.op("vector", lambda V: V.tensor_add(s_t1[:], s_t1[:], s_t2[:]),
                 after=S.now("scalar"))
            S.op("scalar", lambda Sc, g=g: Sc.activation(
                s_t2[:], k3[:, 3 * 2 + g, :], AF.Square), after=S.now("vector"))
            S.op("vector", lambda V: V.tensor_add(s_t1[:], s_t1[:], s_t2[:]),
                 after=S.now("scalar"))
            S.op("scalar", lambda Sc: Sc.activation(s_t1[:], s_t1[:], AF.Sqrt),
                 after=S.now("vector"))
            S.op("vector", lambda V: V.tensor_scalar_max(s_t1[:], s_t1[:], 1e-12),
                 after=S.now("scalar"))
            S.op("vector", lambda V, rs=rs: V.reciprocal(rs, s_t1[:]))
            for c in range(4):
                S.op("vector", lambda V, c=c, g=g, rs=rs: V.tensor_tensor(
                    k3[:, 2 * c + g, :], k3[:, 2 * c + g, :], rs, ALU.mult))
        for g in range(2):
            quat_mul_emit(QM_BOUND, v3, k3, b3, g)
        for oc in range(NCH):
            if t == 0:
                S.op("vector", lambda V, oc=oc: V.tensor_tensor_scan(
                    m3[:, oc, :], s_onesf[:], b3[:, oc, :], 0.0,
                    ALU.mult, ALU.add))
            else:
                S.op("vector", lambda V, oc=oc: V.tensor_tensor_scan(
                    m3[:, oc, :], s_onesf[:], b3[:, oc, :],
                    s_carry[:][:, oc:oc + 1], ALU.mult, ALU.add))
            S.op("vector", lambda V, oc=oc: V.tensor_copy(
                s_carry[:][:, oc:oc + 1], m3[:, oc, 511:512]))
        for g in range(2):
            quat_mul_emit(QM_RETR, m3, k3, v3, g)
        tsl = slice(t * 512, t * 512 + 512)
        ln_block(None, b3, s_epsp[:][:, tsl], lng=s_lng[:], lnb=s_lnb[:],
                 src3=v3)
        nrm_done = S.now("vector")
        for dc in range(NCH):
            bank = pA if (dc % 2 == 0) else pB
            S.op("tensor", lambda Tn, dc=dc, bank=bank: [
                Tn.matmul(bank[:], wo3[:, c2, dc * 128:dc * 128 + 128],
                          b3[:, c2, :], start=(c2 == 0), stop=False)
                for c2 in range(NCH)] + [
                Tn.matmul(bank[:], s_bo[:][:, dc * 128:dc * 128 + 128],
                          s_ones[:][0:1, 0:512], start=False, stop=True)],
                after={"vector": max(nrm_done["vector"], oproj_mark[dc % 2])},
                n=NCH + 1)
            S.op("vector", lambda V, dc=dc, bank=bank: V.tensor_add(
                x3[:, dc, :], x3[:, dc, :], bank[:]), after=S.now("tensor"))
            S.op("vector", lambda V, dc=dc: V.tensor_add(
                x3[:, dc, :], x3[:, dc, :], xh3[:, dc, :]))
            oproj_mark[dc % 2] = S.cnt["vector"]
        S.dma("sync", lambda E_, t=t: E_.dma_start(xap(x_out, t), x3),
              after=S.now("vector"), sem="sxo")

    S.wait_only("sync", S.now("sxo"))


def build_phase1(Ttok=T):
    nc = bass.Bass("TRN2", target_bir_lowering=False, debug=False)
    S = Sched()
    banks = _mk_banks(nc)
    x_in = nc.dram_tensor("x_in", [D, Ttok], F32, kind="ExternalInput").ap()
    x_out = nc.dram_tensor("x_out", [D, Ttok], F32, kind="ExternalOutput").ap()
    gen_phase1(nc, S, nc.alloc_sbuf_tensor, _mk_in(nc, ""), banks,
               x_in, x_out, Ttok)
    S.emit(nc)
    return nc


def prep_phase1_inputs(xcore, inp, fm=False):
    perm = np.array([4 * ((oc % 2) * 128 + p) + oc // 2
                     for oc in range(8) for p in range(128)])
    Wk = np.asarray(inp["L0_attn_Wk"], np.float32)
    Wv = np.asarray(inp["L0_attn_Wv"], np.float32)
    Wo = np.asarray(inp["L0_attn_Wo"], np.float32)
    Tt = xcore.shape[1] if fm else xcore.shape[0]
    return {
        "x_in": np.ascontiguousarray(np.asarray(xcore, np.float32)) if fm
        else np.ascontiguousarray(np.asarray(xcore, np.float32).T),
        "wkT": np.ascontiguousarray(Wk[perm, :].T),
        "wvT": np.ascontiguousarray(Wv[perm, :].T),
        "woT": np.ascontiguousarray(Wo[:, perm].T),
        "bk": np.ascontiguousarray(np.asarray(inp["L0_attn_bk"], np.float32)[perm][None, :]),
        "bv": np.ascontiguousarray(np.asarray(inp["L0_attn_bv"], np.float32)[perm][None, :]),
        "bo": np.ascontiguousarray(np.asarray(inp["L0_attn_bo"], np.float32)[None, :]),
        "lng": np.ascontiguousarray(np.asarray(inp["L0_attn_ln_g"], np.float32)[perm].reshape(8, 128).T),
        "lnb": np.ascontiguousarray(np.asarray(inp["L0_attn_ln_b"], np.float32)[perm].reshape(8, 128).T),
        "epspos": np.ascontiguousarray(
            (LN_EPS * np.arange(1, Tt + 1, dtype=np.float32))[None, :]),
    }


FP4_MID = [0.25, 0.75, 1.25, 1.75, 2.5, 3.5, 5.0]
FP4_STEP = [0.5, 0.5, 0.5, 0.5, 1.0, 1.0, 2.0]


def gen_phase4(nc, S, sb, din, banks, x_in, x_out, Ttok):
    NT = Ttok // 512
    I = 2816
    NI = I // 128
    wg_d = din("wg", [I, D], F32)
    wu_d = din("wu", [I, D], F32)
    wd_d = din("wd", [D, I], F32)
    eyeb = din("eyeb", [128, 128], BF16)
    eyefd = din("eyef4", [128, 128], F32)
    _int = _mk_in(nc, S.pref, kind="Internal")
    wgT = _int("wgT", [D, I], BF16)
    wuT = _int("wuT", [D, I], BF16)
    wdT = _int("wdT", [I, D], BF16)
    s_x = sb("s_x", [128, NCH * 512], F32)
    s_xh = sb("s_xh", [128, NCH * 512], BF16)
    s_x32 = sb("s_x32", [128, NCH * 512], F32)
    s_xtm = sb("s_xtm", [128, 4 * D], F32)
    s_xtmq = sb("s_xtmq", [128, 4 * D], BF16)
    s_xq = sb("s_xq", [128, NCH * 512], BF16)
    s_hq = sb("s_hq", [128, NI * 512], BF16)
    s_sg2 = [sb(f"s_sg{i}", [128, 512], BF16) for i in range(2)]
    s_hb2 = [sb(f"s_hb{i}", [128, 512], BF16) for i in range(2)]
    s_htm2 = [sb(f"s_htm{i}", [128, 512], BF16) for i in range(2)]
    s_eye = sb("s_eye", [128, 128], BF16)
    s_eyf = sb("s_eyf", [128, 128], F32)
    s_ones = sb("s_ones", [128, 512], BF16)
    s_eps = sb("s_epsP", [1, 1], F32)
    s_sq = sb("s_sq", [128, 2 * 512], BF16)
    s_st = sb("s_st", [1, 4 * 512], F32)
    s_stb = sb("s_stb", [1, 2 * 512], BF16)
    s_t1 = sb("s_t1", [128, 1024], F32)
    s_t2 = sb("s_t2", [128, 1024], F32)
    s_qmx = sb("s_qmx", [128, 64], F32)
    s_qsc = sb("s_qsc", [128, 64], F32)
    s_qrs = sb("s_qrs", [128, 64], F32)
    s_qsg = sb("s_qsg", [128, 1024], BF16)
    s_qz = sb("s_qz", [128, 1024], F32)
    s_qc = sb("s_qc", [128, 1024], BF16)
    s_qt = sb("s_qt", [128, 1024], BF16)
    s_qs = sb("s_qs", [128, 1024], F32)
    s_qmagic = sb("s_qmagic", [128, 1], F32)
    s_qmagicn = sb("s_qmagicn", [128, 1], F32)
    nc.const_aps.register(F32, 64.0, s_qmagic[:])
    nc.const_aps.register(F32, -64.0, s_qmagicn[:])
    s_wf2 = [sb(f"s_wf{i}", [128, 1024], F32) for i in range(2)]
    s_wq2 = [sb(f"s_wq{i}", [128, 1024], BF16) for i in range(2)]
    s_wt2 = [sb(f"s_wt{i}", [128, 1024], BF16) for i in range(2)]
    s_wgu = [sb(f"s_wgu{i}", [128, 2 * NCH * 128], BF16) for i in range(2)]
    s_wdn = [sb(f"s_wdn{i}", [128, 22 * 128], BF16) for i in range(2)]

    (bk6, pS1, pS2) = banks
    pA, pB, pC, pD, pBC, pG = bk6

    x3 = s_x[:].rearrange("p (c n) -> p c n", n=512)
    xh3 = s_xh[:].rearrange("p (c n) -> p c n", n=512)
    xtm = s_xtm[:].rearrange("p (g f) -> p g f", f=D)
    xq3 = s_xq[:].rearrange("p (c n) -> p c n", n=512)
    hq3 = s_hq[:].rearrange("p (c n) -> p c n", n=512)
    wgu = [s_wgu[i][:].rearrange("p (m c f) -> p m c f", m=2, f=128)
           for i in range(2)]

    def xap(dram, t):
        return dram.rearrange("(c p) n -> p c n", p=128)[:, :, t * 512:(t + 1) * 512]

    S.dma("sync", lambda E_: [
        E_.dma_start(s_eye[:], eyeb),
        E_.dma_start(s_eyf[:], eyefd),
    ], n=2, sem="cst")
    S.op("vector", lambda V: [
        V.memset(s_ones[:], 1.0),
        V.memset(s_eps[:], LN_EPS),
        V.memset(s_qmagic[:], 64.0),
        V.memset(s_qmagicn[:], -64.0),
    ], n=4)

    mm_mark = {0: 0, 1: 0}
    tr_mark = {0: 0, 1: 0}

    def emit_quant(src, dst, F, after=None):
        nb = F // 16
        sv = s_qmx[:][:, 0:nb]
        sc = s_qsc[:][:, 0:nb]
        rs = s_qrs[:][:, 0:nb]
        sg = s_qsg[:][:, 0:F]
        z = s_qz[:][:, 0:F]
        r1 = s_qt[:][:, 0:F]
        cc = s_qc[:][:, 0:F]
        ss = s_qs[:][:, 0:F]
        src3 = src.rearrange("p (b i) -> p b i", i=16)
        S.op("vector", lambda V: V.tensor_reduce(
            sv, src3, AX.X, ALU.max, apply_absolute_value=True), after=after)
        S.op("vector", lambda V: V.tensor_scalar(
            sc, sv, 1.0 / 6.0, 1e-12, ALU.mult, ALU.max))
        S.op("vector", lambda V: V.reciprocal(rs, sc))
        S.op("scalar", lambda Sc: Sc.activation(sg, src, AF.Sign),
             after=S.now("vector"))
        S.op("scalar", lambda Sc: Sc.activation(z, src, AF.Abs))
        rs_b = rs.unsqueeze(2).broadcast_to([128, nb, 16])
        sc_b = sc.unsqueeze(2).broadcast_to([128, nb, 16])
        z3 = z.rearrange("p (b i) -> p b i", i=16)
        S.op("vector", lambda V: V.tensor_tensor(z3, z3, rs_b, ALU.mult),
             after=S.now("scalar"))
        S.op("scalar", lambda Sc: Sc.activation(r1, z, AF.Copy, bias=64.0),
             after=S.now("vector"))
        S.op("scalar", lambda Sc: Sc.activation(r1, r1, AF.Copy, bias=-64.0))
        S.op("vector", lambda V: V.tensor_scalar_min(cc, r1, 2.0),
             after=S.now("scalar"))
        S.op("vector", lambda V: V.scalar_tensor_tensor(
            cc, z, 2.5, cc, ALU.is_gt, ALU.add))
        S.op("vector", lambda V: V.scalar_tensor_tensor(
            cc, z, 3.5, cc, ALU.is_gt, ALU.add))
        S.op("vector", lambda V: V.tensor_scalar(
            r1, z, 5.0, 2.0, ALU.is_gt, ALU.mult))
        S.op("vector", lambda V: V.tensor_add(cc, cc, r1))
        sg3 = sg.rearrange("p (b i) -> p b i", i=16)
        ss3 = ss.rearrange("p (b i) -> p b i", i=16)
        S.op("vector", lambda V: V.tensor_tensor(ss3, sg3, sc_b, ALU.mult))
        S.op("vector", lambda V: V.tensor_tensor(dst, cc, ss, ALU.mult))

    wq_ci = [0]
    wq_cons = [ {"scalar": 0, "vector": 0}, {"scalar": 0, "vector": 0} ]
    wq_tmark = [0, 0]

    def quant_transpose(dram_in, dram_out, R, Cc):
        nr = R // 128
        for r in range(nr):
            rsl = slice(r * 128, r * 128 + 128)
            subs = [(0, 1024), (1024, 2048), (2048, 2816)] if Cc == 2816 \
                else [(0, 1024)]
            for (f0, f1) in subs:
                F = f1 - f0
                ci = wq_ci[0]; wq_ci[0] += 1
                p = ci % 2
                s_wf = s_wf2[p]; s_wq = s_wq2[p]; s_wt = s_wt2[p]
                S.dma("sync", lambda E_, rsl=rsl, f0=f0, f1=f1, s_wf=s_wf:
                      E_.dma_start(s_wf[:][:, 0:f1 - f0], dram_in[rsl, f0:f1]),
                      after=dict(wq_cons[p]), sem="wload")
                emit_quant(s_wf[:][:, 0:F], s_wq[:][:, 0:F], F,
                           after={**S.now("wload"), "tensor": wq_tmark[p]})
                wq_cons[p] = {"scalar": S.cnt["scalar"],
                              "vector": S.cnt["vector"]}
                wst = S.now("wstore")
                for cb in range(F // 128):
                    bank = pA if cb % 2 == 0 else pB
                    S.op("tensor", lambda Tn, cb=cb, bank=bank, s_wq=s_wq:
                         Tn.matmul(bank[:, 0:128],
                                   s_wq[:][:, cb * 128:cb * 128 + 128],
                                   s_eye[:], start=True, stop=True),
                         after={**S.now("vector", "cst"),
                                "scalar": tr_mark[cb % 2]})
                    S.op("scalar", lambda Sc, cb=cb, bank=bank, s_wt=s_wt:
                         Sc.activation(s_wt[:][:, cb * 128:cb * 128 + 128],
                                       bank[:, 0:128], AF.Copy),
                         after={**S.now("tensor"), **wst})
                    tr_mark[cb % 2] = S.cnt["scalar"]
                wq_tmark[p] = S.cnt["tensor"]
                S.dma("sync", lambda E_, rsl=rsl, f0=f0, F=F, s_wt=s_wt:
                      E_.dma_start(dram_out[f0:f0 + F, rsl].rearrange(
                          "(b p) r -> p b r", p=128),
                          s_wt[:][:, 0:F].rearrange("p (b r) -> p b r", r=128)),
                      after=S.now("scalar"), sem="wstore")

    quant_transpose(wg_d, wgT, I, D)
    quant_transpose(wu_d, wuT, I, D)
    quant_transpose(wd_d, wdT, D, I)
    wq_done_store = S.now("wstore")

    ev_mark = {0: 0, 1: 0}
    dn_mark = {0: 0, 1: 0}
    m_wgu = {}
    m_wdn = {}

    def load_wgu(t, oc):
        j = t * NI + oc
        b = j % 2
        aft = dict(wq_done_store)
        if j >= 2:
            aft["tensor"] = m_wgu[j - 2][1]
        S.dma("gpsimd", lambda G, oc=oc, b=b: [
            G.dma_start(wgu[b][:, 0, :, :],
                        wgT.rearrange("(c p) i -> p c i", p=128)
                        [:, :, oc * 128:oc * 128 + 128]),
            G.dma_start(wgu[b][:, 1, :, :],
                        wuT.rearrange("(c p) i -> p c i", p=128)
                        [:, :, oc * 128:oc * 128 + 128]),
        ], after=aft, n=2, sem="lgu_%d" % b)
        m_wgu[j] = ({S._p("lgu_%d" % b): S.cnt[S._p("lgu_%d" % b)]}, None)

    def load_wdn(t, dc):
        j = t * NCH + dc
        b = j % 2
        aft = dict(wq_done_store)
        if j >= 2:
            aft["tensor"] = m_wdn[j - 2][1]
        S.dma("gpsimd", lambda G, dc=dc, b=b: G.dma_start(
            s_wdn[b][:].rearrange("p (c f) -> p c f", f=128),
            wdT.rearrange("(c p) d -> p c d", p=128)
            [:, :, dc * 128:dc * 128 + 128].rearrange("p c d -> p (c d)")
            if False else
            wdT.rearrange("(c p) d -> p c d", p=128)[:, :, dc * 128:dc * 128 + 128]),
            after=aft, sem="ldn_%d" % b)
        m_wdn[j] = ({S._p("ldn_%d" % b): S.cnt[S._p("ldn_%d" % b)]}, None)

    def ln_block4(src3, dst3, sub3=None):
        sub3 = sub3 if sub3 is not None else src3
        mu = s_st[:][:, 0:512]
        ex2 = s_st[:][:, 512:1024]
        var = s_st[:][:, 1024:1536]
        m2 = s_st[:][:, 1536:2048]
        mub = s_stb[:][:, 0:512]
        rsb = s_stb[:][:, 512:1024]
        v_src = S.now("vector")
        for c in range(NCH):
            S.op("scalar", lambda Sc, c=c: Sc.activation(
                s_sq[:][:, (c % 2) * 512:(c % 2) * 512 + 512],
                src3[:, c, :], AF.Square),
                after={"tensor": mm_mark[c % 2], **v_src})
            S.op("tensor", lambda Tn, c=c: [
                Tn.matmul(pS1[:], s_ones[:][:, 0:1], src3[:, c, :],
                          start=(c == 0), stop=(c == NCH - 1)),
                Tn.matmul(pS2[:], s_ones[:][:, 0:1],
                          s_sq[:][:, (c % 2) * 512:(c % 2) * 512 + 512],
                          start=(c == 0), stop=(c == NCH - 1)),
            ], after=S.now("scalar"), n=2)
            mm_mark[c % 2] = S.cnt["tensor"]
        S.op("vector", lambda V: [
            V.tensor_scalar_mul(mu, pS1[:], 1.0 / D),
            V.tensor_scalar_mul(ex2, pS2[:], 1.0 / D),
        ], after=S.now("tensor"), n=2)
        S.op("vector", lambda V: V.tensor_mul(m2, mu, mu))
        S.op("vector", lambda V: V.tensor_copy(mub, mu))
        S.op("vector", lambda V: V.tensor_sub(var, ex2, m2))
        S.op("scalar", lambda Sc: Sc.activation(var, var, AF.Sqrt,
                                                bias=s_eps[:]),
             after=S.now("vector"))
        S.op("vector", lambda V: V.reciprocal(var, var), after=S.now("scalar"))
        S.op("vector", lambda V: V.tensor_copy(rsb, var))
        S.op("tensor", lambda Tn: [
            Tn.matmul(pBC[:], s_ones[:][0:1, 0:128], mub, start=True, stop=True),
            Tn.matmul(pG[:], s_ones[:][0:1, 0:128], rsb, start=True, stop=True),
        ], after=S.now("vector"), n=2)
        for c in range(NCH):
            S.op("vector", lambda V, c=c: V.tensor_tensor(
                s_t1[:][:, 0:512], sub3[:, c, :], pBC[:], ALU.subtract),
                after=S.now("tensor"))
            S.op("vector", lambda V, c=c: V.tensor_tensor(
                dst3[:, c, :], s_t1[:][:, 0:512], pG[:], ALU.mult))

    load_wgu(0, 0)
    load_wgu(0, 1)
    load_wdn(0, 0)
    load_wdn(0, 1)

    for t in range(NT):
        S.dma("sync", lambda E_, t=t: E_.dma_start(x3, xap(x_in, t)),
              after=S.now("vector", "sxo"), sem="sx")
        m_x = S.now("sx")
        for c in range(NCH):
            S.op("scalar", lambda Sc, c=c: Sc.activation(
                xh3[:, c, :], x3[:, c, :], AF.Copy),
                after={**m_x, **S.now("vector")})
        x323 = s_x32[:].rearrange("p (c n) -> p c n", n=512)
        ln_block4(xh3, x323, sub3=x3)
        xh_done = S.now("vector")
        xtmq = s_xtmq[:].rearrange("p (g f) -> p g f", f=D)
        for tg in range(4):
            for half in range(2):
                bank = pA if half == 0 else pB
                S.op("tensor", lambda Tn, tg=tg, half=half, bank=bank: [
                    Tn.matmul(bank[:, ci * 128:ci * 128 + 128],
                              x323[:, half * 4 + ci, tg * 128:tg * 128 + 128],
                              s_eyf[:], start=(ci == 0), stop=(ci == 3))
                    for ci in range(4)],
                    after={**S.now("vector", "cst"),
                           "scalar": tr_mark[half]}, n=4)
                S.op("scalar", lambda Sc, tg=tg, half=half, bank=bank:
                     Sc.activation(xtm[:, tg, half * 512:half * 512 + 512],
                                   bank[:], AF.Copy), after=S.now("tensor"))
                tr_mark[half] = S.cnt["scalar"]
        for tg in range(4):
            emit_quant(xtm[:, tg, :], xtmq[:, tg, :], D,
                       after=S.now("scalar"))
        for c in range(NCH):
            half = c % 2
            bank = pA if half == 0 else pB
            S.op("tensor", lambda Tn, c=c, bank=bank: [
                Tn.matmul(bank[:, tg * 128:tg * 128 + 128],
                          xtmq[:, tg, c * 128:c * 128 + 128],
                          s_eye[:], start=(tg == 0), stop=(tg == 3))
                for tg in range(4)],
                after={**S.now("vector"), "scalar": tr_mark[half]}, n=4)
            S.op("scalar", lambda Sc, c=c, bank=bank: Sc.activation(
                xq3[:, c, :], bank[:], AF.Copy), after=S.now("tensor"))
            tr_mark[half] = S.cnt["scalar"]
        xq_done = S.now("scalar")

        pcd = [(pC, pD), (pBC, pG)]
        sil_mark = [S.cnt["scalar"]] * 2
        hmul_mark = [S.cnt["vector"]] * 2
        htm_mark = {}
        hq_done_m = {}

        def stageA1(oc):
            p = oc % 2
            j = t * NI + oc
            b = j % 2
            bC, bD = pcd[p]
            aft = {**m_wgu[j][0],
                   "scalar": max(xq_done["scalar"], sil_mark[p]),
                   "vector": hmul_mark[p]}
            S.op("tensor", lambda Tn, b=b, bC=bC, bD=bD: [
                Tn.matmul(bC[:], wgu[b][:, 0, c2, :], xq3[:, c2, :],
                          start=(c2 == 0), stop=(c2 == NCH - 1))
                for c2 in range(NCH)] + [
                Tn.matmul(bD[:], wgu[b][:, 1, c2, :], xq3[:, c2, :],
                          start=(c2 == 0), stop=(c2 == NCH - 1))
                for c2 in range(NCH)],
                after=aft, n=2 * NCH)
            m_wgu[j] = (m_wgu[j][0], S.cnt["tensor"])
            if j + 2 < NT * NI:
                load_wgu(*divmod(j + 2, NI))

        def stageA2(oc):
            p = oc % 2
            bC, bD = pcd[p]
            S.op("scalar", lambda Sc, p=p, bC=bC: Sc.activation(
                s_sg2[p][:], bC[:], AF.Silu),
                 after=S.now("tensor", "vector"))
            sil_mark[p] = S.cnt["scalar"]
            S.op("vector", lambda V, p=p, bD=bD: V.tensor_tensor(
                s_hb2[p][:], s_sg2[p][:], bD[:], ALU.mult),
                 after=S.now("scalar", "tensor"))
            hmul_mark[p] = S.cnt["vector"]
            S.op("tensor", lambda Tn, p=p: [
                Tn.matmul(pA[:, tg * 128:tg * 128 + 128],
                          s_hb2[p][:][:, tg * 128:tg * 128 + 128],
                          s_eye[:], start=(tg == 0), stop=(tg == 3))
                for tg in range(4)],
                after={**S.now("vector"), "scalar": tr_mark[0]}, n=4)
            S.op("scalar", lambda Sc, p=p: Sc.activation(
                s_htm2[p][:], pA[:], AF.Copy), after=S.now("tensor"))
            tr_mark[0] = S.cnt["scalar"]
            htm_mark[oc] = S.cnt["scalar"]

        btr_mark = [S.cnt["tensor"]] * 2

        def stageBq(oc):
            p = oc % 2
            emit_quant(s_htm2[p][:], s_htm2[p][:], 512,
                       after={"scalar": htm_mark[oc], "tensor": btr_mark[p]})
            hq_done_m[oc] = {"vector": S.cnt["vector"],
                             "scalar": S.cnt["scalar"]}

        def stageBt(oc):
            p = oc % 2
            S.op("tensor", lambda Tn, p=p: [
                Tn.matmul(pB[:, tg * 128:tg * 128 + 128],
                          s_htm2[p][:][:, tg * 128:tg * 128 + 128],
                          s_eye[:], start=(tg == 0), stop=(tg == 3))
                for tg in range(4)],
                after={"vector": hq_done_m[oc]["vector"],
                       "scalar": tr_mark[1]}, n=4)
            btr_mark[p] = S.cnt["tensor"]
            S.op("scalar", lambda Sc, oc=oc: Sc.activation(
                hq3[:, oc, :], pB[:], AF.Copy), after=S.now("tensor"))
            tr_mark[1] = S.cnt["scalar"]

        stageA1(0)
        stageA2(0)
        stageA1(1)
        stageBq(0)
        stageA2(1)
        for oc in range(2, NI):
            stageA1(oc)
            stageBq(oc - 1)
            stageBt(oc - 2)
            stageA2(oc)
        stageBq(NI - 1)
        stageBt(NI - 2)
        stageBt(NI - 1)
        hq_done = S.now("scalar")

        v_hq = S.cnt["vector"]
        for dc in range(NCH):
            j = t * NCH + dc
            b = j % 2
            bank = pC if dc % 2 == 0 else pD
            aft = {**m_wdn[j][0], "scalar": hq_done["scalar"],
                   "vector": max(dn_mark[dc % 2], v_hq)}
            S.op("tensor", lambda Tn, dc=dc, b=b, bank=bank: [
                Tn.matmul(bank[:], s_wdn[b][:].rearrange(
                    "p (c f) -> p c f", f=128)[:, c2, :], hq3[:, c2, :],
                    start=(c2 == 0), stop=(c2 == NI - 1))
                for c2 in range(NI)],
                after=aft, n=NI)
            m_wdn[j] = (m_wdn[j][0], S.cnt["tensor"])
            if j + 2 < NT * NCH:
                load_wdn(*divmod(j + 2, NCH))
            S.op("vector", lambda V, dc=dc, bank=bank: V.tensor_add(
                x3[:, dc, :], x3[:, dc, :], bank[:]), after=S.now("tensor"))
            dn_mark[dc % 2] = S.cnt["vector"]
        S.dma("sync", lambda E_, t=t: E_.dma_start(xap(x_out, t), x3),
              after=S.now("vector"), sem="sxo")

    S.wait_only("sync", S.now("sxo"))


def build_phase4(Ttok=T):
    nc = bass.Bass("TRN2", target_bir_lowering=False, debug=False)
    S = Sched()
    banks = _mk_banks(nc)
    x_in = nc.dram_tensor("x_in", [D, Ttok], F32, kind="ExternalInput").ap()
    x_out = nc.dram_tensor("x_out", [D, Ttok], F32, kind="ExternalOutput").ap()
    gen_phase4(nc, S, nc.alloc_sbuf_tensor, _mk_in(nc, ""), banks,
               x_in, x_out, Ttok)
    S.emit(nc)
    return nc


def prep_phase4_inputs(xcore, inp, fm=False):
    return {
        "x_in": np.ascontiguousarray(np.asarray(xcore, np.float32)) if fm
        else np.ascontiguousarray(np.asarray(xcore, np.float32).T),
        "wg": np.ascontiguousarray(np.asarray(inp["L1_ffn_w_gate"], np.float32)),
        "wu": np.ascontiguousarray(np.asarray(inp["L1_ffn_w_up"], np.float32)),
        "wd": np.ascontiguousarray(np.asarray(inp["L1_ffn_w_down"], np.float32)),
        "eyeb": np.eye(128, dtype=np.float32).astype(BF),
        "eyef4": np.eye(128, dtype=np.float32),
    }


GAMMA = 0.99


def gen_phase3(nc, S, sb, din, banks, x_in, x_out, Ttok, upto=8, mixn=9999):
    NT = Ttok // 512
    HD = 512
    wq_d = din("wq", [D, HD], F32)
    wk_d = din("wk", [D, HD], F32)
    wv_d = din("wv", [D, HD], F32)
    wo_d = din("wo", [HD, D], F32)
    upi_d = din("upi", [128, 512], BF16)
    ups_d = din("ups", [128, 512], BF16)
    gp1_d = din("gp1", [128, 128], BF16)
    gm1_d = din("gm1", [128, 128], BF16)
    gp0_d = din("gp0", [128, 128], BF16)
    glc_d = din("glc", [128, 1], F32)
    e32_d = din("e32", [128, 32], BF16)
    geL_d = din("geL", [128, 32], BF16)
    geLF_d = din("geLF", [128, 128], BF16)
    s_x = sb("s_x", [128, NCH * 512], F32)
    s_xh = sb("s_xh", [128, NCH * 512], BF16)
    s_q = sb("s_q", [128, 4 * 512], BF16)
    s_k = sb("s_k", [128, 4 * 512], BF16)
    s_v = sb("s_v", [128, 4 * 512], BF16)
    s_o = sb("s_o", [128, 4 * 512], BF16)
    s_wq = sb("s_wq", [128, NCH * HD], BF16)
    s_wk = sb("s_wk", [128, NCH * HD], BF16)
    s_wv = sb("s_wv", [128, NCH * HD], BF16)
    s_wo = sb("s_wo", [128, 4 * D], BF16)
    s_ST = sb("s_ST", [128, 4 * 96], BF16)
    s_upi = sb("s_upi", [128, 512], BF16)
    s_ups = sb("s_ups", [128, 512], BF16)
    s_gp1 = sb("s_gp1", [128, 128], BF16)
    s_gm1 = sb("s_gm1", [128, 128], BF16)
    s_gp0 = sb("s_gp0", [128, 128], BF16)
    s_glc = sb("s_glc", [128, 1], F32)
    s_e32 = sb("s_e32", [128, 32], BF16)
    s_geL = sb("s_geL", [128, 32], BF16)
    s_geLF = sb("s_geLF", [128, 128], BF16)
    s_Qg = sb("s_Qg", [128, 128], BF16)
    s_Qm = sb("s_Qm", [128, 128], BF16)
    s_Km = sb("s_Km", [128, 128], BF16)
    s_Ks = sb("s_Ks", [128, 128], BF16)
    s_Wt = sb("s_Wt", [128, 512], BF16)
    s_T2 = sb("s_T2", [128, 512], BF16)
    s_MA = sb("s_MA", [128, 512], BF16)
    s_Ktm = sb("s_Ktm", [128, 128], BF16)
    s_Vtm = sb("s_Vtm", [128, 128], BF16)
    s_Qtm = sb("s_Qtm", [128, 128], BF16)
    s_KtmL = sb("s_KtmL", [128, 128], BF16)
    s_QtmL = sb("s_QtmL", [128, 128], BF16)
    s_U = sb("s_U", [128, 128], BF16)
    s_a = sb("s_a", [128, 128], BF16)
    s_ag = sb("s_ag", [128, 128], BF16)
    s_tst = sb("s_tst", [128, 96], F32)
    s_ones = sb("s_ones", [128, 512], BF16)
    s_eps = sb("s_epsP", [1, 1], F32)
    s_sq = sb("s_sq", [128, 2 * 512], BF16)
    s_st = sb("s_st", [1, 4 * 512], F32)
    s_stb = sb("s_stb", [1, 2 * 512], BF16)
    s_t1 = sb("s_t1", [128, 512], F32)

    (bk6, pS1, pS2) = banks
    pA, pB, pC, pD, pE, pF = bk6

    x3 = s_x[:].rearrange("p (c n) -> p c n", n=512)
    xh3 = s_xh[:].rearrange("p (c n) -> p c n", n=512)
    q3 = s_q[:].rearrange("p (g n) -> p g n", n=512)
    k3 = s_k[:].rearrange("p (g n) -> p g n", n=512)
    v3 = s_v[:].rearrange("p (g n) -> p g n", n=512)
    o3 = s_o[:].rearrange("p (g n) -> p g n", n=512)
    wq3 = s_wq[:].rearrange("p (c j) -> p c j", j=HD)
    wk3 = s_wk[:].rearrange("p (c j) -> p c j", j=HD)
    wv3 = s_wv[:].rearrange("p (c j) -> p c j", j=HD)
    wo3 = s_wo[:].rearrange("p (c j) -> p c j", j=D)
    st3 = s_ST[:].rearrange("p (g w) -> p g w", w=96)

    def xap(dram, t):
        return dram.rearrange("(c p) n -> p c n", p=128)[:, :, t * 512:(t + 1) * 512]

    S.dma("sync", lambda E_: [
        E_.dma_start(s_upi[:], upi_d),
        E_.dma_start(s_ups[:], ups_d),
        E_.dma_start(s_gp1[:], gp1_d),
        E_.dma_start(s_gm1[:], gm1_d),
        E_.dma_start(s_gp0[:], gp0_d),
        E_.dma_start(s_glc[:], glc_d),
        E_.dma_start(s_e32[:], e32_d),
        E_.dma_start(s_geL[:], geL_d),
        E_.dma_start(s_geLF[:], geLF_d),
    ], n=9, sem="cst")
    S.dma("gpsimd", lambda G: [
        G.dma_start(s_wq[:].rearrange("p (c j) -> p c j", j=HD),
                    wq_d.rearrange("(c p) j -> p c j", p=128)),
        G.dma_start(s_wk[:].rearrange("p (c j) -> p c j", j=HD),
                    wk_d.rearrange("(c p) j -> p c j", p=128)),
        G.dma_start(s_wv[:].rearrange("p (c j) -> p c j", j=HD),
                    wv_d.rearrange("(c p) j -> p c j", p=128)),
        G.dma_start(s_wo[:].rearrange("p (c j) -> p c j", j=D),
                    wo_d.rearrange("(c p) j -> p c j", p=128)),
    ], n=4, sem="wts")
    m_wts = S.now("wts")
    S.op("vector", lambda V: [
        V.memset(s_ones[:], 1.0),
        V.memset(s_eps[:], LN_EPS),
        V.memset(s_ST[:], 0.0),
    ], n=3)
    cst_now = S.now("cst")

    mm_mark = {0: 0, 1: 0}
    evac_mark = {0: 0, 1: 0}
    oproj_mark = {0: 0, 1: 0}
    st_read_mark = 0
    st_read_vec = 0

    def ln_block3(src3, sub3, dst3):
        mu = s_st[:][:, 0:512]
        ex2 = s_st[:][:, 512:1024]
        var = s_st[:][:, 1024:1536]
        m2 = s_st[:][:, 1536:2048]
        mub = s_stb[:][:, 0:512]
        rsb = s_stb[:][:, 512:1024]
        v_src = S.now("vector")
        for c in range(NCH):
            S.op("scalar", lambda Sc, c=c: Sc.activation(
                s_sq[:][:, (c % 2) * 512:(c % 2) * 512 + 512],
                src3[:, c, :], AF.Square),
                after={"tensor": mm_mark[c % 2], **v_src})
            S.op("tensor", lambda Tn, c=c: [
                Tn.matmul(pS1[:], s_ones[:][:, 0:1], src3[:, c, :],
                          start=(c == 0), stop=(c == NCH - 1)),
                Tn.matmul(pS2[:], s_ones[:][:, 0:1],
                          s_sq[:][:, (c % 2) * 512:(c % 2) * 512 + 512],
                          start=(c == 0), stop=(c == NCH - 1)),
            ], after=S.now("scalar"), n=2)
            mm_mark[c % 2] = S.cnt["tensor"]
        S.op("vector", lambda V: [
            V.tensor_scalar_mul(mu, pS1[:], 1.0 / D),
            V.tensor_scalar_mul(ex2, pS2[:], 1.0 / D),
        ], after=S.now("tensor"), n=2)
        S.op("vector", lambda V: V.tensor_mul(m2, mu, mu))
        S.op("vector", lambda V: V.tensor_copy(mub, mu))
        S.op("vector", lambda V: V.tensor_sub(var, ex2, m2))
        S.op("scalar", lambda Sc: Sc.activation(var, var, AF.Sqrt,
                                                bias=s_eps[:]),
             after=S.now("vector"))
        S.op("vector", lambda V: V.reciprocal(var, var), after=S.now("scalar"))
        S.op("vector", lambda V: V.tensor_copy(rsb, var))
        S.op("tensor", lambda Tn: [
            Tn.matmul(pC[:], s_ones[:][0:1, 0:128], mub, start=True, stop=True),
            Tn.matmul(pD[:], s_ones[:][0:1, 0:128], rsb, start=True, stop=True),
        ], after=S.now("vector"), n=2)
        for c in range(NCH):
            S.op("vector", lambda V, c=c: V.tensor_tensor(
                s_t1[:], sub3[:, c, :], pC[:], ALU.subtract),
                after=S.now("tensor"))
            S.op("vector", lambda V, c=c: V.tensor_tensor(
                dst3[:, c, :], s_t1[:], pD[:], ALU.mult))

    for t in range(NT):
        S.dma("sync", lambda E_, t=t: E_.dma_start(x3, xap(x_in, t)),
              after=S.now("vector", "sxo"), sem="sx")
        m_x = S.now("sx")
        for c in range(NCH):
            S.op("scalar", lambda Sc, c=c: Sc.activation(
                xh3[:, c, :], x3[:, c, :], AF.Copy),
                after={**m_x, **S.now("vector")})
        ln_block3(xh3, xh3, xh3)
        xh_done = S.now("vector")

        for (wsrc, dst3) in ((wq3, q3), (wk3, k3), (wv3, v3)):
            for oc in range(4):
                bank = pA if (oc % 2 == 0) else pB
                S.op("tensor", lambda Tn, oc=oc, bank=bank, wsrc=wsrc: [
                    Tn.matmul(bank[:], wsrc[:, c2, oc * 128:oc * 128 + 128],
                              xh3[:, c2, :], start=(c2 == 0),
                              stop=(c2 == NCH - 1))
                    for c2 in range(NCH)],
                    after={**m_wts, "vector": xh_done["vector"],
                           "scalar": evac_mark[oc % 2]}, n=NCH)
                S.op("scalar", lambda Sc, oc=oc, bank=bank, dst3=dst3:
                     Sc.activation(dst3[:, oc, :], bank[:], AF.Copy),
                     after=S.now("tensor"))
                evac_mark[oc % 2] = S.cnt["scalar"]
        proj_done = S.now("scalar")

        bk = [pA, pB, pC, pD]
        for cc in range(4):
            csl = slice(cc * 128, cc * 128 + 128)
            for g in range(4):
                Qf = q3[:, g, csl]
                Kf = k3[:, g, csl]
                Vf = v3[:, g, csl]
                A0 = [st3[32 * m:32 * m + 32, g, 0:32] for m in range(4)]
                C0 = [st3[32 * m:32 * m + 32, g, 32:64] for m in range(4)]
                G0 = [st3[32 * m:32 * m + 32, g, 64:96] for m in range(4)]
                if upto < 1:
                    continue
                S.op("vector", lambda V, Qf=Qf: V.tensor_mul(
                    s_Qg[:], Qf, s_gp1[:]),
                    after={**cst_now, "vector": st_read_vec,
                           "scalar": max(proj_done["scalar"], st_read_mark)})
                S.op("vector", lambda V, Qf=Qf: V.tensor_mul(
                    s_Qm[:], Qf, s_gm1[:]))
                S.op("vector", lambda V, Kf=Kf: V.tensor_mul(
                    s_Km[:], Kf, s_gm1[:]))
                S.op("vector", lambda V, Kf=Kf: V.tensor_mul(
                    s_Ks[:], Kf, s_gp0[:]))
                sc_done = S.now("vector")
                if upto == 13:
                    continue
                S.op("tensor", lambda Tn: [
                    Tn.matmul(bk[m][:, 0:128],
                              s_Km[:][32 * m:32 * m + 32, :],
                              s_Qg[:][32 * m:32 * m + 32, :],
                              start=True, stop=True,
                              tile_position=(32 * m, 0),
                              skip_group_check=True)
                    for m in range(4)], after=sc_done, n=4)
                S.op("tensor", lambda Tn: [
                    Tn.matmul(bk[m][:, 128:256],
                              s_Qm[:][32 * m:32 * m + 32, :],
                              s_Ks[:][32 * m:32 * m + 32, :],
                              start=True, stop=True,
                              tile_position=(32 * m, 0),
                              skip_group_check=True)
                    for m in range(4)], n=4)
                if upto == 12:
                    continue
                S.op("vector", lambda V: [
                    V.tensor_tensor(s_Wt[:][:, m * 128:m * 128 + 128],
                                    bk[m][:, 0:128], s_upi[:][:, 0:128],
                                    ALU.mult)
                    for m in range(4)], after=S.now("tensor"), n=4)
                S.op("vector", lambda V: [
                    V.tensor_tensor(s_T2[:][:, m * 128:m * 128 + 128],
                                    bk[m][:, 128:256], s_ups[:][:, 0:128],
                                    ALU.mult)
                    for m in range(4)], n=4)
                wt_done = S.now("vector")
                if upto < 2:
                    continue
                for (srcf, dsttm, f0) in ((Kf, s_Ktm, 384), (Vf, s_Vtm, 416),
                                          (Qf, s_Qtm, 448)):
                    S.op("tensor", lambda Tn, srcf=srcf, f0=f0: [
                        Tn.matmul(bk[m][:, f0:f0 + 32],
                                  srcf[32 * m:32 * m + 32, :],
                                  s_e32[:][32 * m:32 * m + 32, :],
                                  start=True, stop=True,
                                  tile_position=(32 * m, 0),
                                  skip_group_check=True)
                        for m in range(4)],
                        after=S.now("vector", "scalar"), n=4)
                    S.op("scalar", lambda Sc, dsttm=dsttm, f0=f0: [
                        Sc.activation(dsttm[:][:, m * 32:m * 32 + 32],
                                      bk[m][:, f0:f0 + 32], AF.Copy)
                        for m in range(4)], after=S.now("tensor"), n=4)
                if upto < 3:
                    continue
                S.op("vector", lambda V: V.tensor_scalar(
                    s_KtmL[:], s_Ktm[:], s_glc[:], None, ALU.mult),
                    after=S.now("scalar"))
                S.op("vector", lambda V: V.tensor_scalar(
                    s_QtmL[:], s_Qtm[:], s_glc[:], None, ALU.mult))
                tm_done = S.now("vector", "scalar")
                if upto < 4:
                    continue
                S.op("tensor", lambda Tn, C0=C0: [
                    Tn.matmul(bk[m][:, 480:512],
                              s_Ks[:][32 * m:32 * m + 32, :], C0[m],
                              start=True, stop=False,
                              tile_position=(32 * m, 0),
                              skip_group_check=True)
                    for m in range(4)],
                    after={**tm_done, "scalar": st_read_mark,
                           "vector": max(tm_done.get("vector", 0), st_read_vec)}, n=4)
                S.op("tensor", lambda Tn: [
                    Tn.matmul(bk[m][:, 480:512],
                              s_T2[:][:, m * 128:m * 128 + 128],
                              s_Vtm[:][:, m * 32:m * 32 + 32],
                              start=False, stop=True,
                              skip_group_check=True)
                    for m in range(4)], n=4)
                S.op("vector", lambda V: [
                    V.tensor_scalar_mul(s_U[:][:, m * 32:m * 32 + 32],
                                        bk[m][:, 480:512], -GAMMA)
                    for m in range(4)], after=S.now("tensor"), n=4)
                if upto < 5:
                    continue
                S.op("tensor", lambda Tn, A0=A0: [
                    mmi for m in range(4) for mmi in [
                        Tn.matmul(pE[32 * m:32 * m + 32, 0:128],
                                  s_Ktm[:][:, m * 32:m * 32 + 32],
                                  s_Wt[:][:, m * 128:m * 128 + 128],
                                  start=True, stop=False,
                                  tile_position=(0, 32 * m),
                                  skip_group_check=True),
                        Tn.matmul(pE[32 * m:32 * m + 32, 0:128], A0[m],
                                  s_Qg[:][32 * m:32 * m + 32, :],
                                  start=False, stop=True,
                                  tile_position=(32 * m, 32 * m),
                                  skip_group_check=True),
                    ]], after=S.now("vector"), n=8)
                S.op("scalar", lambda Sc: Sc.activation(
                    s_a[:], pE[:, 0:128], AF.Copy), after=S.now("tensor"))
                S.op("vector", lambda V: V.tensor_mul(
                    s_ag[:], s_a[:], s_gp1[:]), after=S.now("scalar"))
                if upto < 6:
                    continue
                S.op("tensor", lambda Tn: [
                    Tn.matmul(bk[m][:, 256:384],
                              s_Qm[:][32 * m:32 * m + 32, :],
                              s_ag[:][32 * m:32 * m + 32, :],
                              start=True, stop=True,
                              tile_position=(32 * m, 0),
                              skip_group_check=True)
                    for m in range(4)], after=S.now("vector"), n=4)
                S.op("vector", lambda V: [
                    V.tensor_tensor(s_MA[:][:, m * 128:m * 128 + 128],
                                    bk[m][:, 256:384], s_upi[:][:, 0:128],
                                    ALU.mult)
                    for m in range(4)], after=S.now("tensor"), n=4)
                ma_done = S.now("vector")
                if upto < 7:
                    continue
                S.op("tensor", lambda Tn, C0=C0, G0=G0: [
                    mmi for m in range(4) for mmi in [
                        Tn.matmul(pE[32 * m:32 * m + 32, 128:256],
                                  s_Vtm[:][:, m * 32:m * 32 + 32],
                                  s_MA[:][:, m * 128:m * 128 + 128],
                                  start=True, stop=False,
                                  tile_position=(0, 32 * m),
                                  skip_group_check=True),
                        Tn.matmul(pE[32 * m:32 * m + 32, 128:256], C0[m],
                                  s_ag[:][32 * m:32 * m + 32, :],
                                  start=False, stop=False,
                                  tile_position=(32 * m, 32 * m),
                                  skip_group_check=True),
                        Tn.matmul(pE[32 * m:32 * m + 32, 128:256], G0[m],
                                  s_Qg[:][32 * m:32 * m + 32, :],
                                  start=False, stop=False,
                                  tile_position=(32 * m, 32 * m),
                                  skip_group_check=True),
                        Tn.matmul(pE[32 * m:32 * m + 32, 128:256],
                                  s_U[:][:, m * 32:m * 32 + 32],
                                  s_Wt[:][:, m * 128:m * 128 + 128],
                                  start=False, stop=True,
                                  tile_position=(0, 32 * m),
                                  skip_group_check=True),
                    ]], after=ma_done, n=16)
                S.op("scalar", lambda Sc, g=g, csl=csl: Sc.activation(
                    o3[:, g, csl], pE[:, 128:256], AF.Copy),
                    after=S.now("tensor"))
                if upto == 9:
                    S.op("tensor", lambda Tn: [
                        mmi for m in range(4) for mmi in [
                            Tn.matmul(pE[32 * m:32 * m + 32, 256:288],
                                      s_KtmL[:][:, m * 32:m * 32 + 32],
                                      s_Ktm[:][:, m * 32:m * 32 + 32],
                                      start=True, stop=True,
                                      tile_position=(0, 32 * m),
                                      skip_group_check=True),
                            Tn.matmul(pE[32 * m:32 * m + 32, 288:320],
                                      s_QtmL[:][:, m * 32:m * 32 + 32],
                                      s_Vtm[:][:, m * 32:m * 32 + 32],
                                      start=True, stop=True,
                                      tile_position=(0, 32 * m),
                                      skip_group_check=True),
                            Tn.matmul(pE[32 * m:32 * m + 32, 320:352],
                                      s_KtmL[:][:, m * 32:m * 32 + 32],
                                      s_U[:][:, m * 32:m * 32 + 32],
                                      start=True, stop=True,
                                      tile_position=(0, 32 * m),
                                      skip_group_check=True),
                        ]], n=12)
                    S.op("scalar", lambda Sc, g=g: Sc.activation(
                        st3[:, g, :], pE[:, 256:352], AF.Copy),
                        after=S.now("tensor"))
                    st_read_mark = S.cnt["scalar"]
                    continue
                if upto == 10:
                    S.op("tensor", lambda Tn, A0=A0, C0=C0, G0=G0: [
                        mmi for m in range(4) for mmi in [
                            Tn.matmul(pE[32 * m:32 * m + 32, 256:288],
                                      s_geL[:][32 * m:32 * m + 32, :], A0[m],
                                      start=True, stop=True,
                                      tile_position=(32 * m, 32 * m),
                                      skip_group_check=True),
                            Tn.matmul(pE[32 * m:32 * m + 32, 288:320],
                                      s_geL[:][32 * m:32 * m + 32, :], C0[m],
                                      start=True, stop=True,
                                      tile_position=(32 * m, 32 * m),
                                      skip_group_check=True),
                            Tn.matmul(pE[32 * m:32 * m + 32, 320:352],
                                      s_geL[:][32 * m:32 * m + 32, :], G0[m],
                                      start=True, stop=True,
                                      tile_position=(32 * m, 32 * m),
                                      skip_group_check=True),
                        ]], n=12)
                    S.op("scalar", lambda Sc, g=g: Sc.activation(
                        st3[:, g, :], pE[:, 256:352], AF.Copy),
                        after=S.now("tensor"))
                    st_read_mark = S.cnt["scalar"]
                    continue
                if upto < 8:
                    continue
                S.op("tensor", lambda Tn: [
                    Tn.matmul(pE[32 * m:32 * m + 32, 256 + r * 32:288 + r * 32],
                              lhs_tm[:][:, m * 32:m * 32 + 32],
                              rhs_tm[:][:, m * 32:m * 32 + 32],
                              start=True, stop=True,
                              tile_position=(0, 32 * m),
                              skip_group_check=True)
                    for (r, lhs_tm, rhs_tm) in (
                        (0, s_KtmL, s_Ktm), (1, s_QtmL, s_Vtm),
                        (2, s_KtmL, s_U))
                    for m in range(4)], n=12)
                S.op("tensor", lambda Tn: [
                    Tn.matmul(pF[:, r * 32:r * 32 + 32], s_geLF[:],
                              st3[:, g, r * 32:r * 32 + 32],
                              start=True, stop=True, skip_group_check=True)
                    for r in range(3)], after={"vector": st_read_vec}, n=3)
                S.op("scalar", lambda Sc: Sc.activation(
                    s_tst[:], pE[:, 256:352], AF.Copy), after=S.now("tensor"))
                S.op("vector", lambda V, g=g: V.tensor_tensor(
                    st3[:, g, :], s_tst[:], pF[:, 0:96], ALU.add),
                    after=S.now("scalar"))
                st_read_mark = S.cnt["scalar"]
                st_read_vec = S.cnt["vector"]

        o_done = S.now("scalar")
        v_mix = S.cnt["vector"]
        for dc in range(NCH):
            bank = pA if (dc % 2 == 0) else pB
            S.op("tensor", lambda Tn, dc=dc, bank=bank: [
                Tn.matmul(bank[:], wo3[:, c2, dc * 128:dc * 128 + 128],
                          o3[:, c2, :], start=(c2 == 0), stop=(c2 == 3))
                for c2 in range(4)],
                after={**o_done, "vector": max(oproj_mark[dc % 2], v_mix)},
                n=4)
            S.op("vector", lambda V, dc=dc, bank=bank: V.tensor_add(
                x3[:, dc, :], x3[:, dc, :], bank[:]), after=S.now("tensor"))
            oproj_mark[dc % 2] = S.cnt["vector"]
        S.dma("sync", lambda E_, t=t: E_.dma_start(xap(x_out, t), x3),
              after=S.now("vector"), sem="sxo")

    S.wait_only("sync", S.now("sxo"))


def build_phase3(Ttok=T, upto=8, mixn=9999):
    nc = bass.Bass("TRN2", target_bir_lowering=False, debug=False)
    S = Sched()
    banks = _mk_banks(nc)
    x_in = nc.dram_tensor("x_in", [D, Ttok], F32, kind="ExternalInput").ap()
    x_out = nc.dram_tensor("x_out", [D, Ttok], F32, kind="ExternalOutput").ap()
    gen_phase3(nc, S, nc.alloc_sbuf_tensor, _mk_in(nc, ""), banks,
               x_in, x_out, Ttok, upto=upto, mixn=mixn)
    S.emit(nc)
    return nc


def prep_phase3_inputs(xcore, inp, fm=False):
    L = 128
    g = GAMMA
    p = np.arange(128, dtype=np.float64)
    upi = np.triu(np.ones((128, 128), np.float32))
    ups = np.triu(np.ones((128, 128), np.float32), 1)
    gp1 = np.broadcast_to((g ** (p + 1)).astype(np.float32), (128, 128))
    gm1 = np.broadcast_to((g ** -(p + 1)).astype(np.float32), (128, 128))
    gp0 = np.broadcast_to((g ** p).astype(np.float32), (128, 128))
    glc = (g ** (127 - p)).astype(np.float32)[:, None]
    e32 = np.zeros((128, 32), np.float32)
    for i in range(128):
        e32[i, i % 32] = 1.0
    geL = (e32 * (g ** 128)).astype(np.float32)
    return {
        "x_in": np.ascontiguousarray(np.asarray(xcore, np.float32)) if fm
        else np.ascontiguousarray(np.asarray(xcore, np.float32).T),
        "wq": np.ascontiguousarray(np.asarray(inp["L1_attn_Wq"], np.float32)),
        "wk": np.ascontiguousarray(np.asarray(inp["L1_attn_Wk"], np.float32)),
        "wv": np.ascontiguousarray(np.asarray(inp["L1_attn_Wv"], np.float32)),
        "wo": np.ascontiguousarray(np.asarray(inp["L1_attn_Wo"], np.float32)),
        "upi": np.ascontiguousarray(np.tile(upi, (1, 4)).astype(BF)),
        "ups": np.ascontiguousarray(np.tile(ups, (1, 4)).astype(BF)),
        "gp1": np.ascontiguousarray(gp1.astype(BF)),
        "gm1": np.ascontiguousarray(gm1.astype(BF)),
        "gp0": np.ascontiguousarray(gp0.astype(BF)),
        "glc": np.ascontiguousarray(glc),
        "e32": np.ascontiguousarray(e32.astype(BF)),
        "geL": np.ascontiguousarray(geL.astype(BF)),
        "geLF": np.ascontiguousarray((np.eye(128, dtype=np.float32) * (g ** 128)).astype(BF)),
    }


import contextlib


def build_fused(Ttok=T):
    nc = bass.Bass("TRN2", target_bir_lowering=False, debug=False)
    S = Sched()
    banks = _mk_banks(nc)
    x_in = nc.dram_tensor("x_in", [D, Ttok], F32, kind="ExternalInput").ap()
    xA = nc.dram_tensor("xA", [D, Ttok], F32, kind="Internal").ap()
    xB = nc.dram_tensor("xB", [D, Ttok], F32, kind="Internal").ap()
    xC = nc.dram_tensor("xC", [D, Ttok], F32, kind="Internal").ap()
    x_out = nc.dram_tensor("x_out", [D, Ttok], F32, kind="ExternalOutput").ap()
    chain = [x_in, xA, xB, xC, x_out]
    gens = [(gen_phase1, "p1_"), (gen_phase2, "p2_"),
            (gen_phase3, "p3_"), (gen_phase4, "p4_")]
    for i, (gen, pref) in enumerate(gens):
        S.pref = pref
        stack = contextlib.ExitStack()

        def sb(name, shape, dt_, stack=stack, pref=pref):
            return stack.enter_context(nc.sbuf_tensor(pref + name, shape, dt_))

        gen(nc, S, sb, _mk_in(nc, pref), banks, chain[i], chain[i + 1], Ttok)
        stack.close()
        if i < len(gens) - 1:
            S.barrier_all()
    S.pref = ""
    S.emit(nc)
    return nc


PREFS = ["p1_", "p2_", "p3_", "p4_"]
PREPS_ALL = None


def fused_in_map(xcore_fm, inputs):
    m = {"x_in": np.ascontiguousarray(xcore_fm)}
    preps = [prep_phase1_inputs, prep_phase2_inputs, prep_phase3_inputs,
             prep_phase4_inputs]
    for pref, prep in zip(PREFS, preps):
        d = prep(xcore_fm, inputs, fm=True)
        for k, v in d.items():
            if k == "x_in":
                continue
            m[pref + k] = v
    return m


def kernel(**inputs):
    x = np.asarray(inputs["L0_attn_x"], np.float32)
    nc = build_fused()
    in_maps = [fused_in_map(x[c].T, inputs) for c in range(8)]
    res = run_bass_kernel_spmd(nc, in_maps, core_ids=list(range(8)))
    return np.stack([np.asarray(r["x_out"], np.float32).T
                     for r in res.results])


def kernel_unfused(**inputs):
    x = np.asarray(inputs["L0_attn_x"], np.float32)
    ncs = [build_phase1(), build_phase2(), build_phase3(), build_phase4()]
    preps = [prep_phase1_inputs, prep_phase2_inputs, prep_phase3_inputs,
             prep_phase4_inputs]
    cur = [np.ascontiguousarray(x[c].T) for c in range(8)]
    for nc, prep in zip(ncs, preps):
        in_maps = [prep(cur[c], inputs, fm=True) for c in range(8)]
        res = run_bass_kernel_spmd(nc, in_maps, core_ids=list(range(8)))
        cur = [np.asarray(r["x_out"], np.float32) for r in res.results]
    return np.stack([c.T for c in cur])


# version 11
# speedup vs baseline: 1.0854x; 1.0645x over previous
import numpy as np
import ml_dtypes
import concourse.bass as bass
import concourse.mybir as mybir
from concourse.bass_utils import run_bass_kernel_spmd

F32 = mybir.dt.float32
BF16 = mybir.dt.bfloat16
AF = mybir.ActivationFunctionType
ALU = mybir.AluOpType
AX = mybir.AxisListType
BF = ml_dtypes.bfloat16

D = 1024
T = 4096
NCH = D // 128
LN_EPS = 1e-5
GEPS = float(np.finfo(np.float32).eps)

ENGS = ("sync", "gpsimd", "tensor", "vector", "scalar")


class Sched:

    def __init__(self, paranoid=False):
        self.items = {e: [] for e in ENGS}
        self.cnt = {e: 0 for e in ENGS}
        self.extra = set()
        self.paranoid = paranoid
        self.pref = ""

    def _p(self, name):
        return name if name in ENGS else self.pref + name

    def now(self, *engs):
        if not engs:
            engs = ENGS
        return {self._p(e): self.cnt.get(self._p(e), 0) for e in engs}

    def op(self, eng, fn, after=None, inc=1, n=1, sem=None):
        sem = self._p(sem) if sem else eng
        if sem not in ENGS:
            self.extra.add(sem)
            self.cnt.setdefault(sem, 0)
        waits = dict(after or {})
        if self.paranoid:
            waits = {**{k: v for k, v in self.cnt.items() if k != eng or eng in ("sync", "gpsimd")}, **waits}
        self.items[eng].append((waits, fn, inc, n, sem))
        self.cnt[sem] += inc * (n if inc == 16 else 1)
        return self.cnt[sem]

    def dma(self, eng, fn, after=None, n=1, sem=None):
        return self.op(eng, fn, after=after, inc=16, n=n, sem=sem)

    def wait_only(self, eng, after):
        self.items[eng].append((dict(after), None, 0, 0, eng))

    def barrier_all(self):
        snap = dict(self.cnt)
        for e in ENGS:
            w = {k: v for k, v in snap.items() if v > 0}
            self.items[e].append((w, None, 0, 0, e))

    def emit(self, nc):
        names = list(ENGS) + sorted(self.extra)
        sems = {e: nc.alloc_semaphore("sem_" + e) for e in names}
        blk = nc.Block()
        blk.__enter__()

        def build(eng_name):
            self_sync = eng_name in ("vector", "scalar")
            def body(E):
                waited = {k: 0 for k in names}
                own = 0
                for waits, fn, inc, ndecl, semn in self.items[eng_name]:
                    if self_sync and fn is not None and own > 0:
                        if waited[eng_name] < own:
                            E.wait_ge(sems[eng_name], own)
                            waited[eng_name] = own
                    for k in names:
                        v = waits.get(k, 0)
                        if v <= waited[k]:
                            continue
                        if k == eng_name and eng_name not in ("sync", "gpsimd") \
                                and not self_sync:
                            continue
                        if k == eng_name and self_sync and v <= own:
                            continue
                        E.wait_ge(sems[k], v)
                        waited[k] = v
                    if fn is None:
                        continue
                    out = fn(E)
                    instrs = out if isinstance(out, (list, tuple)) else [out]
                    assert len(instrs) == ndecl, (eng_name, len(instrs), ndecl)
                    if inc == 16:
                        for i in instrs:
                            i.then_inc(sems[semn], inc)
                    else:
                        instrs[-1].then_inc(sems[semn], inc)
                        if semn == eng_name:
                            own += inc
            return body

        for e in ENGS:
            if self.items[e]:
                getattr(blk, e)(build(e))
        blk.__exit__(None, None, None)
        return sems




def _mk_banks(nc):
    ps = nc.alloc_psum_tensor
    return ([ps(f"bk{i}", [128, 512], F32) for i in range(6)],
            ps("bs1", [1, 512], F32), ps("bs2", [1, 512], F32))


def _mk_in(nc, pref, kind="ExternalInput"):
    def f(name, shape, dtype):
        return nc.dram_tensor(pref + name, shape, dtype, kind=kind).ap()
    return f

def gen_phase2(nc, S, sb, din, banks, x_in, x_out, Ttok):
    E, H = 8, 2048
    QT = min(1024, Ttok)
    NQ = Ttok // QT
    NTT = QT // 512
    NJ = NQ * E

    gwT = din("gwT", [D, E], F32)
    gb = din("gb", [1, E], F32)
    rmsw = din("rmsw", [1, E], F32)
    w1d = din("w1", [E, D, H], F32)
    b1d = din("b1", [E, 1, H], F32)
    w2d = din("w2", [E, H, D], F32)
    b2d = din("b2", [E, 1, D], F32)
    eyef = din("eyef", [128, 128], BF16)
    sel4 = din("sel4", [4, 512], BF16)
    s_xq = sb("s_xq", [128, NCH * QT], F32)
    s_xh = sb("s_xh", [128, NCH * QT], BF16)
    s_h = sb("s_h", [128, 16 * 512], BF16)
    s_xf = sb("s_xf", [128, NCH * 512], F32)
    s_w1 = [sb(f"s_w1_{i}", [128, NCH * H], BF16) for i in range(2)]
    s_w2 = sb("s_w2", [128, 16 * D], BF16)
    s_b1 = [sb(f"s_b1_{i}", [1, H], BF16) for i in range(2)]
    s_b2 = sb("s_b2", [1, D], BF16)
    s_gwT = sb("s_gwT", [128, NCH * E], BF16)
    s_gbr = sb("s_gbr", [1, E], BF16)
    s_rmsb = sb("s_rmsb", [128, E], F32)
    s_rmsr = sb("s_rmsr", [1, E], F32)
    s_rmsbr = sb("s_rmsbr", [1, E], BF16)
    s_ones = sb("s_ones", [128, 512], BF16)
    s_onesf = sb("s_onesf", [1, 128], F32)
    s_eyef = sb("s_eyef", [128, 128], BF16)
    s_sel4 = sb("s_sel4", [4, 512], BF16)
    s_sq = sb("s_sq", [128, 2 * 512], BF16)
    s_st = sb("s_st", [1, 4 * 512], F32)
    s_stb = sb("s_stb", [1, 2 * 512], BF16)
    s_g = sb("s_g", [128, 48], F32)
    s_ng4 = sb("s_ng4", [128, 4], BF16)
    s_ngT = sb("s_ngT", [4, 128], BF16)
    s_ngb = sb("s_ngb", [128, 2 * 512], F32)
    s_zt = sb("s_zt", [128, 1], F32)
    s_tmp = sb("s_tmp", [128, 512], F32)
    s_eps = sb("s_eps", [1, 1], F32)
    s_geps = sb("s_geps", [128, 1], F32)

    (bk6, pS1, pS2) = banks
    pH = [bk6[0], bk6[1]]
    pY = [bk6[2], bk6[3]]
    pBC = bk6[4]
    pG = bk6[5]

    def xap(dram, q):
        return dram.rearrange("(c p) t -> p c t", p=128)[:, :, q * QT:(q + 1) * QT]

    xq3 = s_xq[:].rearrange("p (c t) -> p c t", t=QT)
    xh3 = s_xh[:].rearrange("p (c t) -> p c t", t=QT)
    xf3 = s_xf[:].rearrange("p (c t) -> p c t", t=512)
    h3 = s_h[:].rearrange("p (c t) -> p c t", t=512)
    w1v = [s_w1[i][:].rearrange("p (c h) -> p c h", h=H) for i in range(2)]
    w2v = s_w2[:].rearrange("p (c d) -> p c d", d=D)
    gw3 = s_gwT[:].rearrange("p (c e) -> p c e", e=E)
    ey = s_eyef[:]

    S.dma("sync", lambda E_: [
        E_.dma_start(s_rmsr[:], rmsw),
        E_.dma_start(s_eyef[:], eyef),
        E_.dma_start(s_sel4[:], sel4),
    ], n=3, sem="cst")
    S.dma("gpsimd", lambda G: [
        G.dma_start(s_gwT[:].rearrange("p (c e) -> p c e", e=E),
                    gwT.rearrange("(c p) e -> p c e", p=128)),
        G.dma_start(s_gbr[:], gb),
    ], n=2, sem="gwc")
    wait_gw = S.now("gwc")
    S.op("vector", lambda V: [
        V.memset(s_ones[:], 1.0),
        V.memset(s_onesf[:], 1.0),
        V.memset(s_eps[:], LN_EPS),
        V.memset(s_geps[:], GEPS),
        V.tensor_copy(s_rmsbr[:], s_rmsr[:]),
    ], after=S.now("cst"), n=5)
    S.op("tensor", lambda Tn: Tn.matmul(
        pG[:, 0:E], s_ones[:][0:1, 0:128], s_rmsbr[:], start=True, stop=True),
        after=S.now("cst", "vector"))
    S.op("vector", lambda V: V.tensor_copy(s_rmsb[:], pG[:, 0:E]),
         after=S.now("tensor"))
    rmsb_done = S.now("vector")

    pe_after_h = {}
    pe_after_y = {}
    m_w1 = {}
    m_w2 = {}

    def load_w1(j):
        q, e = divmod(j, E)
        b = e % 2
        aft = {}
        if j >= 2:
            aft["tensor"] = pe_after_h[j - 2]
        w1r = w1d[e].rearrange("(c p) h -> p c h", p=128)
        s1 = w1v[b]
        S.dma("gpsimd", lambda G: [
            G.dma_start(s1[:, :, 0:1024], w1r[:, :, 0:1024]),
            G.dma_start(s1[:, :, 1024:2048], w1r[:, :, 1024:2048]),
            G.dma_start(s_b1[b][:], b1d[e]),
        ], after=aft, n=3, sem="w1_%d" % b)
        m_w1[j] = {S._p("w1_%d" % b): S.cnt[S._p("w1_%d" % b)]}

    def load_w2(j):
        q, e = divmod(j, E)
        aft = {}
        if j >= 1:
            aft["tensor"] = pe_after_y[j - 1]
        w2r = w2d[e].rearrange("(c p) d -> p c d", p=128)
        S.dma("gpsimd", lambda G: [
            G.dma_start(w2v[:, 0:8, :], w2r[:, 0:8, :]),
            G.dma_start(s_b2[:], b2d[e]),
            G.dma_start(w2v[:, 8:16, :], w2r[:, 8:16, :]),
        ], after=aft, n=3, sem="w2s")
        m_w2[j] = {S._p("w2s"): S.cnt[S._p("w2s")]}

    load_w1(0)
    load_w2(0)
    load_w1(1)

    vec_xhat = {}
    vec_ngb = {}
    prev_vec_smalls = 0
    mm_mark = {0: 0, 1: 0}
    gelu_mark = {0: 0, 1: 0}
    yev_mark = {0: 0, 1: 0}

    for q in range(NQ):
        S.dma("sync", lambda E_, q=q: E_.dma_start(xq3, xap(x_in, q)),
              after=S.now("vector", "sxo"), sem="sx")
        m_x = S.now("sx")

        for tt in range(NTT):
            tsl = slice(tt * 512, tt * 512 + 512)
            for c in range(NCH):
                S.op("scalar", lambda Sc, c=c, tsl=tsl: [
                    Sc.activation(xh3[:, c, tsl], xq3[:, c, tsl], AF.Copy),
                    Sc.activation(s_sq[:][:, (c % 2) * 512:(c % 2) * 512 + 512],
                                  xq3[:, c, tsl], AF.Square),
                ], after={"tensor": mm_mark[c % 2], **m_x}, n=2)
                S.op("tensor", lambda Tn, c=c, tsl=tsl: [
                    Tn.matmul(pS1[:], s_ones[:][:, 0:1], xh3[:, c, tsl],
                              start=(c == 0), stop=(c == NCH - 1)),
                    Tn.matmul(pS2[:], s_ones[:][:, 0:1],
                              s_sq[:][:, (c % 2) * 512:(c % 2) * 512 + 512],
                              start=(c == 0), stop=(c == NCH - 1)),
                ], after={**S.now("scalar"), "vector": prev_vec_smalls}, n=2)
                mm_mark[c % 2] = S.cnt["tensor"]
            mu = s_st[:][:, 0:512]
            ex2 = s_st[:][:, 512:1024]
            var = s_st[:][:, 1024:1536]
            mub = s_stb[:][:, 0:512]
            rsb = s_stb[:][:, 512:1024]
            m2 = s_st[:][:, 1536:2048]
            S.op("vector", lambda V: [
                V.tensor_scalar_mul(mu, pS1[:], 1.0 / D),
                V.tensor_scalar_mul(ex2, pS2[:], 1.0 / D),
            ], after=S.now("tensor"), n=2)
            S.op("vector", lambda V: [
                V.tensor_mul(m2, mu, mu),
                V.tensor_copy(mub, mu),
            ], n=2)
            S.op("vector", lambda V: V.tensor_sub(var, ex2, m2))
            prev_vec_smalls = S.cnt["vector"]
            S.op("scalar", lambda Sc: Sc.activation(var, var, AF.Sqrt, bias=s_eps[:]),
                 after=S.now("vector"))
            S.op("vector", lambda V: V.reciprocal(var, var),
                 after=S.now("scalar"))
            S.op("vector", lambda V: V.tensor_copy(rsb, var))
            S.op("tensor", lambda Tn: [
                Tn.matmul(pBC[:], s_ones[:][0:1, 0:128], mub, start=True, stop=True),
                Tn.matmul(pG[:], s_ones[:][0:1, 0:128], rsb, start=True, stop=True),
            ], after=S.now("vector"), n=2)
            for c in range(NCH):
                S.op("vector", lambda V, c=c, tsl=tsl:
                     V.tensor_sub(s_tmp[:], xq3[:, c, tsl], pBC[:]),
                     after=S.now("tensor"))
                S.op("vector", lambda V, c=c, tsl=tsl:
                     V.tensor_tensor(xh3[:, c, tsl], s_tmp[:], pG[:], ALU.mult))
            vec_xhat[(q, tt)] = S.cnt["vector"]

            sqo = s_g[:][:, 0:8]
            ms = s_g[:][:, 8:9]
            rstg = s_g[:][:, 9:10]
            lg = s_g[:][:, 16:24]
            mx = s_g[:][:, 24:25]
            nmx = s_g[:][:, 25:26]
            zs = s_g[:][:, 26:27]
            zz = s_g[:][:, 27:28]
            exo = s_g[:][:, 32:40]
            for tg in range(4):
                gsl = slice(tt * 512 + tg * 128, tt * 512 + tg * 128 + 128)
                S.op("tensor", lambda Tn, gsl=gsl: [
                    Tn.matmul(pG[:, 0:E], xh3[:, c2, gsl], gw3[:, c2, :],
                              start=(c2 == 0), stop=False)
                    for c2 in range(NCH)] + [
                    Tn.matmul(pG[:, 0:E], s_ones[:][0:1, 0:128], s_gbr[:],
                              start=False, stop=True)],
                    after={**S.now("vector", "scalar"), **wait_gw}, n=NCH + 1)
                S.op("scalar", lambda Sc: Sc.activation(
                    sqo, pG[:, 0:E], AF.Square, accum_out=ms),
                    after=S.now("tensor"))
                S.op("scalar", lambda Sc: Sc.activation(
                    rstg, ms, AF.Sqrt, bias=s_geps[:], scale=1.0 / E))
                S.op("vector", lambda V: V.reciprocal(rstg, rstg),
                     after={**S.now("scalar"), **rmsb_done})
                S.op("vector", lambda V: V.scalar_tensor_tensor(
                    lg, pG[:, 0:E], rstg, s_rmsb[:], ALU.mult, ALU.mult))
                S.op("vector", lambda V: V.reduce_max(mx, lg, AX.X))
                S.op("vector", lambda V: V.tensor_scalar_mul(nmx, mx, -2.0))
                S.op("scalar", lambda Sc: Sc.activation(
                    exo, lg, AF.Exp, bias=nmx, scale=2.0, accum_out=zs),
                    after=S.now("vector"))
                S.op("vector", lambda V, tg=tg: [
                    V.tensor_scalar(zz, zs, 1e-6, 1.0, ALU.mult, ALU.add),
                    V.tensor_scalar_mul(s_zt[:], zs, 1e-6),
                ], after=S.now("scalar"), n=2)
                S.op("vector", lambda V: V.reciprocal(zz, zz))
                S.op("vector", lambda V, tg=tg: V.tensor_mul(
                    s_ng4[:][:, tg:tg + 1], s_zt[:], zz))
            S.op("tensor", lambda Tn: Tn.matmul(
                pBC[0:4, 0:128], s_ng4[:], ey, start=True, stop=True),
                after=S.now("vector"))
            S.op("vector", lambda V: V.tensor_copy(s_ngT[:], pBC[0:4, 0:128]),
                 after=S.now("tensor"))
            for tg in range(4):
                S.op("tensor", lambda Tn, tg=tg: Tn.matmul(
                    pBC[:, 0:128], s_sel4[:][:, tg * 128:tg * 128 + 128],
                    s_ngT[:], start=True, stop=True), after=S.now("vector"))
                S.op("vector", lambda V, tt=tt, tg=tg: V.tensor_copy(
                    s_ngb[:][:, tt * 512 + tg * 128:tt * 512 + tg * 128 + 128],
                    pBC[:, 0:128]), after=S.now("tensor"))
            vec_ngb[(q, tt)] = S.cnt["vector"]

        for e in range(E):
            j = q * E + e
            b = e % 2
            for tt in range(NTT):
                tsl = slice(tt * 512, tt * 512 + 512)
                for hc in range(16):
                    bank = pH[hc % 2]
                    aft = {"scalar": gelu_mark[hc % 2], **m_w1[j],
                           "vector": vec_xhat[(q, tt)]}
                    S.op("tensor", lambda Tn, hc=hc, bank=bank, tsl=tsl, b=b: [
                        Tn.matmul(bank[:], w1v[b][:, c2, hc * 128:hc * 128 + 128],
                                  xh3[:, c2, tsl], start=(c2 == 0), stop=False)
                        for c2 in range(NCH)] + [
                        Tn.matmul(bank[:], s_b1[b][:][:, hc * 128:hc * 128 + 128],
                                  s_ones[:][0:1, 0:512], start=False, stop=True)],
                        after=aft, n=NCH + 1)
                    S.op("scalar", lambda Sc, hc=hc, bank=bank: Sc.activation(
                        h3[:, hc, :], bank[:], AF.Gelu), after=S.now("tensor"))
                    gelu_mark[hc % 2] = S.cnt["scalar"]
                gelu_done = S.cnt["scalar"]
                pe_after_h[j] = S.cnt["tensor"]
                for dc in range(NCH):
                    bank = pY[dc % 2]
                    aft = {"vector": yev_mark[dc % 2], "scalar": gelu_done,
                           **m_w2[j]}
                    S.op("tensor", lambda Tn, dc=dc, bank=bank: [
                        Tn.matmul(bank[:], w2v[:, hc2, dc * 128:dc * 128 + 128],
                                  h3[:, hc2, :], start=(hc2 == 0), stop=False)
                        for hc2 in range(16)] + [
                        Tn.matmul(bank[:], s_b2[:][:, dc * 128:dc * 128 + 128],
                                  s_ones[:][0:1, 0:512], start=False, stop=True)],
                        after=aft, n=17)
                    if e == 0:
                        S.op("vector", lambda V, dc=dc, tsl=tsl, bank=bank:
                             V.tensor_copy(xq3[:, dc, tsl], bank[:]),
                             after=S.now("tensor"))
                    else:
                        S.op("vector", lambda V, dc=dc, tsl=tsl, bank=bank:
                             V.tensor_add(xq3[:, dc, tsl], xq3[:, dc, tsl],
                                          bank[:]),
                             after=S.now("tensor"))
                    yev_mark[dc % 2] = S.cnt["vector"]
            pe_after_y[j] = S.cnt["tensor"]
            if j + 1 < NJ:
                load_w2(j + 1)
            if j + 2 < NJ:
                load_w1(j + 2)

        for tt in range(NTT):
            tsl = slice(tt * 512, tt * 512 + 512)
            S.dma("sync", lambda E_, q=q, tsl=tsl: E_.dma_start(
                xf3, xap(x_in, q)[:, :, tsl]),
                after=S.now("vector"), sem="sxf")
            m_xf = S.now("sxf")
            for dc in range(NCH):
                S.op("vector", lambda V, dc=dc, tt=tt, tsl=tsl:
                     V.tensor_tensor(s_tmp[:], xq3[:, dc, tsl],
                                     s_ngb[:][:, tt * 512:tt * 512 + 512],
                                     ALU.mult),
                     after={**S.now("tensor"), **m_xf})
                S.op("vector", lambda V, dc=dc, tsl=tsl:
                     V.tensor_sub(s_tmp[:], xq3[:, dc, tsl], s_tmp[:]))
                S.op("vector", lambda V, dc=dc, tsl=tsl:
                     V.tensor_add(xq3[:, dc, tsl], s_tmp[:], xf3[:, dc, :]))
        S.dma("sync", lambda E_, q=q: E_.dma_start(xap(x_out, q), xq3),
              after=S.now("vector"), sem="sxo")

    S.wait_only("sync", S.now("sxo"))


def build_phase2(Ttok=T, paranoid=False):
    nc = bass.Bass("TRN2", target_bir_lowering=False, debug=False)
    S = Sched(paranoid)
    banks = _mk_banks(nc)
    x_in = nc.dram_tensor("x_in", [D, Ttok], F32, kind="ExternalInput").ap()
    x_out = nc.dram_tensor("x_out", [D, Ttok], F32, kind="ExternalOutput").ap()
    gen_phase2(nc, S, nc.alloc_sbuf_tensor, _mk_in(nc, ""), banks,
               x_in, x_out, Ttok)
    S.emit(nc)
    return nc


def prep_phase2_inputs(xcore, inp, fm=False):
    sel4 = np.zeros((4, 512), np.float32)
    for r in range(4):
        sel4[r, 128 * r:128 * r + 128] = 1.0
    return {
        "x_in": np.ascontiguousarray(np.asarray(xcore, np.float32)) if fm
        else np.ascontiguousarray(np.asarray(xcore, np.float32).T),
        "gwT": np.ascontiguousarray(np.asarray(inp["L0_ffn_gate_w"], np.float32).T),
        "gb": np.ascontiguousarray(np.asarray(inp["L0_ffn_gate_b"], np.float32)[None, :]),
        "rmsw": np.ascontiguousarray(np.asarray(inp["L0_ffn_rms_w"], np.float32)[None, :]),
        "w1": np.ascontiguousarray(np.asarray(inp["L0_ffn_w1"], np.float32)),
        "b1": np.ascontiguousarray(np.asarray(inp["L0_ffn_b1"], np.float32)[:, None, :]),
        "w2": np.ascontiguousarray(np.asarray(inp["L0_ffn_w2"], np.float32)),
        "b2": np.ascontiguousarray(np.asarray(inp["L0_ffn_b2"], np.float32)[:, None, :]),
        "eyef": np.eye(128, dtype=np.float32).astype(BF),
        "sel4": sel4.astype(BF),
    }


def gen_phase1(nc, S, sb, din, banks, x_in, x_out, Ttok):
    NT = Ttok // 512
    wkT = din("wkT", [D, D], F32)
    wvT = din("wvT", [D, D], F32)
    woT = din("woT", [D, D], F32)
    bkр = din("bk", [1, D], F32)
    bvр = din("bv", [1, D], F32)
    boр = din("bo", [1, D], F32)
    lngd = din("lng", [128, NCH], F32)
    lnbd = din("lnb", [128, NCH], F32)
    epspos = din("epspos", [1, Ttok], F32)
    s_x = sb("s_x", [128, NCH * 512], F32)
    s_xh = sb("s_xh", [128, NCH * 512], BF16)
    s_k = sb("s_k", [128, NCH * 512], BF16)
    s_v = sb("s_v", [128, NCH * 512], BF16)
    s_bnd = sb("s_bnd", [128, NCH * 512], BF16)
    s_mem = sb("s_mem", [128, NCH * 512], BF16)
    s_rs = sb("s_rs", [128, 2 * 512], F32)
    s_wk = sb("s_wk", [128, NCH * D], BF16)
    s_wv = sb("s_wv", [128, NCH * D], BF16)
    s_wo = sb("s_wo", [128, NCH * D], BF16)
    s_bk = sb("s_bk", [1, D], BF16)
    s_bv = sb("s_bv", [1, D], BF16)
    s_bo = sb("s_bo", [1, D], BF16)
    s_lng = sb("s_lng", [128, NCH], F32)
    s_lnb = sb("s_lnb", [128, NCH], F32)
    s_eps = sb("s_epsP", [1, 1], F32)
    s_epsp = sb("s_epsp", [1, Ttok], F32)
    s_ones = sb("s_ones", [128, 512], BF16)
    s_onesf = sb("s_onesf", [128, 512], F32)
    s_carry = sb("s_carry", [128, NCH], F32)
    s_sq = sb("s_sq", [128, 2 * 512], BF16)
    s_st = sb("s_st", [1, 4 * 512], F32)
    s_stb = sb("s_stb", [1, 2 * 512], BF16)
    s_t1 = sb("s_t1", [128, 512], F32)
    s_t2 = sb("s_t2", [128, 512], F32)

    (bk6, pS1, pS2) = banks
    pA, pB, pBC, pG = bk6[0], bk6[1], bk6[4], bk6[5]

    def xap(dram, t):
        return dram.rearrange("(c p) n -> p c n", p=128)[:, :, t * 512:(t + 1) * 512]

    x3 = s_x[:].rearrange("p (c n) -> p c n", n=512)
    xh3 = s_xh[:].rearrange("p (c n) -> p c n", n=512)
    k3 = s_k[:].rearrange("p (c n) -> p c n", n=512)
    v3 = s_v[:].rearrange("p (c n) -> p c n", n=512)
    b3 = s_bnd[:].rearrange("p (c n) -> p c n", n=512)
    m3 = s_mem[:].rearrange("p (c n) -> p c n", n=512)
    rs2 = s_rs[:].rearrange("p (g n) -> p g n", n=512)
    wk3 = s_wk[:].rearrange("p (c j) -> p c j", j=D)
    wv3 = s_wv[:].rearrange("p (c j) -> p c j", j=D)
    wo3 = s_wo[:].rearrange("p (c j) -> p c j", j=D)

    S.dma("sync", lambda E_: [
        E_.dma_start(s_epsp[:], epspos),
        E_.dma_start(s_lng[:], lngd),
        E_.dma_start(s_lnb[:], lnbd),
    ], n=3, sem="cst")
    S.dma("gpsimd", lambda G: [
        G.dma_start(s_wk[:].rearrange("p (c j) -> p c j", j=D),
                    wkT.rearrange("(c p) j -> p c j", p=128)),
        G.dma_start(s_wv[:].rearrange("p (c j) -> p c j", j=D),
                    wvT.rearrange("(c p) j -> p c j", p=128)),
        G.dma_start(s_wo[:].rearrange("p (c j) -> p c j", j=D),
                    woT.rearrange("(c p) j -> p c j", p=128)),
        G.dma_start(s_bk[:], bkр),
        G.dma_start(s_bv[:], bvр),
        G.dma_start(s_bo[:], boр),
    ], n=6, sem="wts")
    m_wts = S.now("wts")
    S.op("vector", lambda V: [
        V.memset(s_ones[:], 1.0),
        V.memset(s_onesf[:], 1.0),
        V.memset(s_eps[:], LN_EPS),
        V.memset(s_carry[:], 0.0),
    ], n=4)
    init_v = S.now("vector")

    mm_mark = {0: 0, 1: 0}
    evac_mark = {0: 0, 1: 0}
    oproj_mark = {0: 0, 1: 0}

    def ln_block(tsl_ap, dst3, epsrow, lng=None, lnb=None, src3=None):
        mu = s_st[:][:, 0:512]
        ex2 = s_st[:][:, 512:1024]
        var = s_st[:][:, 1024:1536]
        m2 = s_st[:][:, 1536:2048]
        mub = s_stb[:][:, 0:512]
        rsb = s_stb[:][:, 512:1024]
        v_src = S.now("vector")
        for c in range(NCH):
            S.op("scalar", lambda Sc, c=c: Sc.activation(
                s_sq[:][:, (c % 2) * 512:(c % 2) * 512 + 512],
                src3[:, c, :], AF.Square),
                after={"tensor": mm_mark[c % 2], **v_src})
            S.op("tensor", lambda Tn, c=c: [
                Tn.matmul(pS1[:], s_ones[:][:, 0:1], src3[:, c, :],
                          start=(c == 0), stop=(c == NCH - 1)),
                Tn.matmul(pS2[:], s_ones[:][:, 0:1],
                          s_sq[:][:, (c % 2) * 512:(c % 2) * 512 + 512],
                          start=(c == 0), stop=(c == NCH - 1)),
            ], after=S.now("scalar"), n=2)
            mm_mark[c % 2] = S.cnt["tensor"]
        S.op("vector", lambda V: [
            V.tensor_scalar_mul(mu, pS1[:], 1.0 / D),
            V.tensor_scalar_mul(ex2, pS2[:], 1.0 / D),
        ], after=S.now("tensor"), n=2)
        S.op("vector", lambda V: [
            V.tensor_mul(m2, mu, mu),
            V.tensor_copy(mub, mu),
        ], n=2)
        S.op("vector", lambda V: V.tensor_sub(var, ex2, m2))
        S.op("vector", lambda V: V.tensor_add(var, var, epsrow),
             after=S.now("cst"))
        S.op("scalar", lambda Sc: Sc.activation(var, var, AF.Sqrt),
             after=S.now("vector"))
        S.op("vector", lambda V: V.reciprocal(var, var), after=S.now("scalar"))
        S.op("vector", lambda V: V.tensor_copy(rsb, var))
        S.op("tensor", lambda Tn: [
            Tn.matmul(pBC[:], s_ones[:][0:1, 0:128], mub, start=True, stop=True),
            Tn.matmul(pG[:], s_ones[:][0:1, 0:128], rsb, start=True, stop=True),
        ], after=S.now("vector"), n=2)
        for c in range(NCH):
            S.op("vector", lambda V, c=c: V.tensor_tensor(
                s_t1[:], src3[:, c, :] if tsl_ap is None else tsl_ap[c],
                pBC[:], ALU.subtract), after=S.now("tensor"))
            if lng is None:
                S.op("vector", lambda V, c=c: V.tensor_tensor(
                    dst3[:, c, :], s_t1[:], pG[:], ALU.mult))
            else:
                S.op("vector", lambda V, c=c: V.tensor_tensor(
                    s_t2[:], s_t1[:], pG[:], ALU.mult))
                S.op("vector", lambda V, c=c: V.tensor_scalar(
                    dst3[:, c, :], s_t2[:], lng[:, c:c + 1], lnb[:, c:c + 1],
                    ALU.mult, ALU.add), after=S.now("cst"))

    QM_BOUND = [
        (0, [(0, 0, 1), (1, 1, -1), (2, 2, -1), (3, 3, -1)]),
        (1, [(0, 1, 1), (1, 0, 1), (2, 3, 1), (3, 2, -1)]),
        (2, [(0, 2, 1), (1, 3, -1), (2, 0, 1), (3, 1, 1)]),
        (3, [(0, 3, 1), (1, 2, 1), (2, 1, -1), (3, 0, 1)]),
    ]
    QM_RETR = [
        (0, [(0, 0, 1), (1, 1, 1), (2, 2, 1), (3, 3, 1)]),
        (1, [(0, 1, -1), (1, 0, 1), (2, 3, -1), (3, 2, 1)]),
        (2, [(0, 2, -1), (1, 3, 1), (2, 0, 1), (3, 1, -1)]),
        (3, [(0, 3, -1), (1, 2, -1), (2, 1, 1), (3, 0, 1)]),
    ]

    def quat_mul_emit(table, a3, bb3, out3, g):
        for oc_, terms in table:
            ot = out3[:, 2 * oc_ + g, :]
            terms = sorted(terms, key=lambda t_: -t_[2])
            first = True
            for (ac, bc, sgn) in terms:
                at = a3[:, 2 * ac + g, :]
                bt = bb3[:, 2 * bc + g, :]
                if first:
                    assert sgn == 1
                    S.op("vector", lambda V, ot=ot, at=at, bt=bt:
                         V.tensor_mul(ot, at, bt))
                    first = False
                else:
                    S.op("vector", lambda V, at=at, bt=bt:
                         V.tensor_mul(s_t1[:], at, bt))
                    op = ALU.add if sgn == 1 else ALU.subtract
                    S.op("vector", lambda V, ot=ot, op=op: V.tensor_tensor(
                        ot, ot, s_t1[:] if op == ALU.add else s_t1[:], op))

    for t in range(NT):
        S.dma("sync", lambda E_, t=t: E_.dma_start(x3, xap(x_in, t)),
              after=S.now("vector", "sxo"), sem="sx")
        m_x = S.now("sx")
        for c in range(NCH):
            S.op("scalar", lambda Sc, c=c: Sc.activation(
                xh3[:, c, :], x3[:, c, :], AF.Copy),
                after={**m_x, **S.now("vector")})
        cpy_done = S.now("scalar")
        S.wait_only("vector", cpy_done)
        ln_block(None, xh3, s_eps[:].broadcast_to([1, 512]), src3=xh3)
        xh_done = S.now("vector")

        for (wsrc, bsrc, dst3) in ((wk3, s_bk, k3), (wv3, s_bv, v3)):
            for oc in range(NCH):
                bank = pA if (oc % 2 == 0) else pB
                S.op("tensor", lambda Tn, oc=oc, bank=bank, wsrc=wsrc, bsrc=bsrc: [
                    Tn.matmul(bank[:], wsrc[:, c2, oc * 128:oc * 128 + 128],
                              xh3[:, c2, :], start=(c2 == 0), stop=False)
                    for c2 in range(NCH)] + [
                    Tn.matmul(bank[:], bsrc[:][:, oc * 128:oc * 128 + 128],
                              s_ones[:][0:1, 0:512], start=False, stop=True)],
                    after={**m_wts,
                           "vector": max(xh_done["vector"], oproj_mark[oc % 2]),
                           "scalar": evac_mark[oc % 2]}, n=NCH + 1)
                S.op("scalar", lambda Sc, oc=oc, bank=bank, dst3=dst3:
                     Sc.activation(dst3[:, oc, :], bank[:], AF.Copy),
                     after=S.now("tensor"))
                evac_mark[oc % 2] = S.cnt["scalar"]
        kv_done = S.now("scalar")

        for g in range(2):
            rs = rs2[:, g, :]
            S.op("scalar", lambda Sc, g=g: Sc.activation(
                s_t1[:], k3[:, 0 * 2 + g, :], AF.Square),
                after={**kv_done, **S.now("vector")})
            S.op("scalar", lambda Sc, g=g: Sc.activation(
                s_t2[:], k3[:, 1 * 2 + g, :], AF.Square))
            S.op("vector", lambda V: V.tensor_add(s_t1[:], s_t1[:], s_t2[:]),
                 after=S.now("scalar"))
            S.op("scalar", lambda Sc, g=g: Sc.activation(
                s_t2[:], k3[:, 2 * 2 + g, :], AF.Square), after=S.now("vector"))
            S.op("vector", lambda V: V.tensor_add(s_t1[:], s_t1[:], s_t2[:]),
                 after=S.now("scalar"))
            S.op("scalar", lambda Sc, g=g: Sc.activation(
                s_t2[:], k3[:, 3 * 2 + g, :], AF.Square), after=S.now("vector"))
            S.op("vector", lambda V: V.tensor_add(s_t1[:], s_t1[:], s_t2[:]),
                 after=S.now("scalar"))
            S.op("scalar", lambda Sc: Sc.activation(s_t1[:], s_t1[:], AF.Sqrt),
                 after=S.now("vector"))
            S.op("vector", lambda V: V.tensor_scalar_max(s_t1[:], s_t1[:], 1e-12),
                 after=S.now("scalar"))
            S.op("vector", lambda V, rs=rs: V.reciprocal(rs, s_t1[:]))
            for c in range(4):
                S.op("vector", lambda V, c=c, g=g, rs=rs: V.tensor_tensor(
                    k3[:, 2 * c + g, :], k3[:, 2 * c + g, :], rs, ALU.mult))
        for g in range(2):
            quat_mul_emit(QM_BOUND, v3, k3, b3, g)
        for oc in range(NCH):
            if t == 0:
                S.op("vector", lambda V, oc=oc: V.tensor_tensor_scan(
                    m3[:, oc, :], s_onesf[:], b3[:, oc, :], 0.0,
                    ALU.mult, ALU.add))
            else:
                S.op("vector", lambda V, oc=oc: V.tensor_tensor_scan(
                    m3[:, oc, :], s_onesf[:], b3[:, oc, :],
                    s_carry[:][:, oc:oc + 1], ALU.mult, ALU.add))
            S.op("vector", lambda V, oc=oc: V.tensor_copy(
                s_carry[:][:, oc:oc + 1], m3[:, oc, 511:512]))
        for g in range(2):
            quat_mul_emit(QM_RETR, m3, k3, v3, g)
        tsl = slice(t * 512, t * 512 + 512)
        ln_block(None, b3, s_epsp[:][:, tsl], lng=s_lng[:], lnb=s_lnb[:],
                 src3=v3)
        nrm_done = S.now("vector")
        for dc in range(NCH):
            bank = pA if (dc % 2 == 0) else pB
            S.op("tensor", lambda Tn, dc=dc, bank=bank: [
                Tn.matmul(bank[:], wo3[:, c2, dc * 128:dc * 128 + 128],
                          b3[:, c2, :], start=(c2 == 0), stop=False)
                for c2 in range(NCH)] + [
                Tn.matmul(bank[:], s_bo[:][:, dc * 128:dc * 128 + 128],
                          s_ones[:][0:1, 0:512], start=False, stop=True)],
                after={"vector": max(nrm_done["vector"], oproj_mark[dc % 2])},
                n=NCH + 1)
            S.op("vector", lambda V, dc=dc, bank=bank: V.tensor_add(
                x3[:, dc, :], x3[:, dc, :], bank[:]), after=S.now("tensor"))
            S.op("vector", lambda V, dc=dc: V.tensor_add(
                x3[:, dc, :], x3[:, dc, :], xh3[:, dc, :]))
            oproj_mark[dc % 2] = S.cnt["vector"]
        S.dma("sync", lambda E_, t=t: E_.dma_start(xap(x_out, t), x3),
              after=S.now("vector"), sem="sxo")

    S.wait_only("sync", S.now("sxo"))


def build_phase1(Ttok=T):
    nc = bass.Bass("TRN2", target_bir_lowering=False, debug=False)
    S = Sched()
    banks = _mk_banks(nc)
    x_in = nc.dram_tensor("x_in", [D, Ttok], F32, kind="ExternalInput").ap()
    x_out = nc.dram_tensor("x_out", [D, Ttok], F32, kind="ExternalOutput").ap()
    gen_phase1(nc, S, nc.alloc_sbuf_tensor, _mk_in(nc, ""), banks,
               x_in, x_out, Ttok)
    S.emit(nc)
    return nc


def prep_phase1_inputs(xcore, inp, fm=False):
    perm = np.array([4 * ((oc % 2) * 128 + p) + oc // 2
                     for oc in range(8) for p in range(128)])
    Wk = np.asarray(inp["L0_attn_Wk"], np.float32)
    Wv = np.asarray(inp["L0_attn_Wv"], np.float32)
    Wo = np.asarray(inp["L0_attn_Wo"], np.float32)
    Tt = xcore.shape[1] if fm else xcore.shape[0]
    return {
        "x_in": np.ascontiguousarray(np.asarray(xcore, np.float32)) if fm
        else np.ascontiguousarray(np.asarray(xcore, np.float32).T),
        "wkT": np.ascontiguousarray(Wk[perm, :].T),
        "wvT": np.ascontiguousarray(Wv[perm, :].T),
        "woT": np.ascontiguousarray(Wo[:, perm].T),
        "bk": np.ascontiguousarray(np.asarray(inp["L0_attn_bk"], np.float32)[perm][None, :]),
        "bv": np.ascontiguousarray(np.asarray(inp["L0_attn_bv"], np.float32)[perm][None, :]),
        "bo": np.ascontiguousarray(np.asarray(inp["L0_attn_bo"], np.float32)[None, :]),
        "lng": np.ascontiguousarray(np.asarray(inp["L0_attn_ln_g"], np.float32)[perm].reshape(8, 128).T),
        "lnb": np.ascontiguousarray(np.asarray(inp["L0_attn_ln_b"], np.float32)[perm].reshape(8, 128).T),
        "epspos": np.ascontiguousarray(
            (LN_EPS * np.arange(1, Tt + 1, dtype=np.float32))[None, :]),
    }


FP4_MID = [0.25, 0.75, 1.25, 1.75, 2.5, 3.5, 5.0]
FP4_STEP = [0.5, 0.5, 0.5, 0.5, 1.0, 1.0, 2.0]


def gen_phase4(nc, S, sb, din, banks, x_in, x_out, Ttok):
    NT = Ttok // 512
    I = 2816
    NI = I // 128
    wg_d = din("wg", [I, D], F32)
    wu_d = din("wu", [I, D], F32)
    wd_d = din("wd", [D, I], F32)
    eyeb = din("eyeb", [128, 128], BF16)
    eyefd = din("eyef4", [128, 128], F32)
    _int = _mk_in(nc, S.pref, kind="Internal")
    wgT = _int("wgT", [D, I], BF16)
    wuT = _int("wuT", [D, I], BF16)
    wdT = _int("wdT", [I, D], BF16)
    s_x = sb("s_x", [128, NCH * 512], F32)
    s_xh = sb("s_xh", [128, NCH * 512], BF16)
    s_x32 = sb("s_x32", [128, NCH * 512], F32)
    s_xtm = sb("s_xtm", [128, 4 * D], F32)
    s_xtmq = sb("s_xtmq", [128, 4 * D], BF16)
    s_xq = sb("s_xq", [128, NCH * 512], BF16)
    s_hq = sb("s_hq", [128, NI * 512], BF16)
    s_sg2 = [sb(f"s_sg{i}", [128, 512], BF16) for i in range(2)]
    s_hb2 = [sb(f"s_hb{i}", [128, 512], BF16) for i in range(2)]
    s_htm2 = [sb(f"s_htm{i}", [128, 512], BF16) for i in range(2)]
    s_eye = sb("s_eye", [128, 128], BF16)
    s_eyf = sb("s_eyf", [128, 128], F32)
    s_ones = sb("s_ones", [128, 512], BF16)
    s_eps = sb("s_epsP", [1, 1], F32)
    s_sq = sb("s_sq", [128, 2 * 512], BF16)
    s_st = sb("s_st", [1, 4 * 512], F32)
    s_stb = sb("s_stb", [1, 2 * 512], BF16)
    s_t1 = sb("s_t1", [128, 1024], F32)
    s_t2 = sb("s_t2", [128, 1024], F32)
    QS = [dict(
        mx=sb(f"s_qmx{i}", [128, 64], F32),
        sc=sb(f"s_qsc{i}", [128, 64], F32),
        rs=sb(f"s_qrs{i}", [128, 64], F32),
        sg=sb(f"s_qsg{i}", [128, 1024], BF16),
        z=sb(f"s_qz{i}", [128, 1024], F32),
        c=sb(f"s_qc{i}", [128, 1024], BF16),
        t=sb(f"s_qt{i}", [128, 1024], BF16),
        s=sb(f"s_qs{i}", [128, 1024], F32),
    ) for i in range(2)]
    s_qmagic = sb("s_qmagic", [128, 1], F32)
    s_qmagicn = sb("s_qmagicn", [128, 1], F32)
    nc.const_aps.register(F32, 64.0, s_qmagic[:])
    nc.const_aps.register(F32, -64.0, s_qmagicn[:])
    s_wf2 = [sb(f"s_wf{i}", [128, 1024], F32) for i in range(2)]
    s_wq2 = [sb(f"s_wq{i}", [128, 1024], BF16) for i in range(2)]
    s_wt2 = [sb(f"s_wt{i}", [128, 1024], BF16) for i in range(2)]
    s_wgu = [sb(f"s_wgu{i}", [128, 2 * NCH * 128], BF16) for i in range(2)]
    s_wdn = [sb(f"s_wdn{i}", [128, 22 * 128], BF16) for i in range(2)]

    (bk6, pS1, pS2) = banks
    pA, pB, pC, pD, pBC, pG = bk6

    x3 = s_x[:].rearrange("p (c n) -> p c n", n=512)
    xh3 = s_xh[:].rearrange("p (c n) -> p c n", n=512)
    xtm = s_xtm[:].rearrange("p (g f) -> p g f", f=D)
    xq3 = s_xq[:].rearrange("p (c n) -> p c n", n=512)
    hq3 = s_hq[:].rearrange("p (c n) -> p c n", n=512)
    wgu = [s_wgu[i][:].rearrange("p (m c f) -> p m c f", m=2, f=128)
           for i in range(2)]

    def xap(dram, t):
        return dram.rearrange("(c p) n -> p c n", p=128)[:, :, t * 512:(t + 1) * 512]

    S.dma("sync", lambda E_: [
        E_.dma_start(s_eye[:], eyeb),
        E_.dma_start(s_eyf[:], eyefd),
    ], n=2, sem="cst")
    S.op("vector", lambda V: [
        V.memset(s_ones[:], 1.0),
        V.memset(s_eps[:], LN_EPS),
        V.memset(s_qmagic[:], 64.0),
        V.memset(s_qmagicn[:], -64.0),
    ], n=4)

    mm_mark = {0: 0, 1: 0}
    tr_mark = {0: 0, 1: 0}

    eq_ci = [0]
    eq_vdone = [0, 0]

    def q_front(qsrc, F, after=None):
        k = eq_ci[0]; eq_ci[0] += 1
        p = k % 2; Q = QS[p]
        nb = F // 16
        src3 = qsrc.rearrange("p (b i) -> p b i", i=16)
        S.op("vector", lambda V: V.tensor_reduce(
            Q["mx"][:][:, 0:nb], src3, AX.X, ALU.max,
            apply_absolute_value=True), after=after)
        S.op("vector", lambda V: V.tensor_scalar(
            Q["sc"][:][:, 0:nb], Q["mx"][:][:, 0:nb], 1.0 / 6.0, 1e-12,
            ALU.mult, ALU.max))
        S.op("vector", lambda V: V.reciprocal(
            Q["rs"][:][:, 0:nb], Q["sc"][:][:, 0:nb]))
        saft = dict(after or {})
        saft["vector"] = max(saft.get("vector", 0), eq_vdone[p])
        S.op("scalar", lambda Sc: Sc.activation(
            Q["sg"][:][:, 0:F], qsrc, AF.Sign), after=saft)
        S.op("scalar", lambda Sc: Sc.activation(
            Q["z"][:][:, 0:F], qsrc, AF.Abs))
        return k

    def q_mid(k, F):
        p = k % 2; Q = QS[p]
        nb = F // 16
        z = Q["z"][:][:, 0:F]
        z3 = z.rearrange("p (b i) -> p b i", i=16)
        rs_b = Q["rs"][:][:, 0:nb].unsqueeze(2).broadcast_to([128, nb, 16])
        r1 = Q["t"][:][:, 0:F]
        S.op("vector", lambda V: V.tensor_tensor(z3, z3, rs_b, ALU.mult),
             after=S.now("scalar"))
        S.op("scalar", lambda Sc: Sc.activation(r1, z, AF.Copy, bias=64.0),
             after=S.now("vector"))
        S.op("scalar", lambda Sc: Sc.activation(r1, r1, AF.Copy, bias=-64.0))
        return S.cnt["scalar"]

    def q_back(k, dst, F, magic_mark, war=0):
        p = k % 2; Q = QS[p]
        nb = F // 16
        z = Q["z"][:][:, 0:F]
        r1 = Q["t"][:][:, 0:F]
        cc = Q["c"][:][:, 0:F]
        ss = Q["s"][:][:, 0:F]
        sc_b = Q["sc"][:][:, 0:nb].unsqueeze(2).broadcast_to([128, nb, 16])
        S.op("vector", lambda V: V.tensor_scalar_min(cc, r1, 2.0),
             after={"scalar": magic_mark})
        S.op("vector", lambda V: V.scalar_tensor_tensor(
            cc, z, 2.5, cc, ALU.is_gt, ALU.add))
        S.op("vector", lambda V: V.scalar_tensor_tensor(
            cc, z, 3.5, cc, ALU.is_gt, ALU.add))
        S.op("vector", lambda V: V.tensor_scalar(
            r1, z, 5.0, 2.0, ALU.is_gt, ALU.mult))
        S.op("vector", lambda V: V.tensor_add(cc, cc, r1))
        sg3 = Q["sg"][:][:, 0:F].rearrange("p (b i) -> p b i", i=16)
        ss3 = ss.rearrange("p (b i) -> p b i", i=16)
        S.op("vector", lambda V: V.tensor_tensor(ss3, sg3, sc_b, ALU.mult))
        S.op("vector", lambda V: V.tensor_tensor(dst, cc, ss, ALU.mult),
             after={"tensor": war} if war else None)
        eq_vdone[p] = S.cnt["vector"]
        return S.cnt["vector"]

    def emit_quant(qsrc, dst, F, after=None, war=0):
        k = q_front(qsrc, F, after=after)
        mm = q_mid(k, F)
        return q_back(k, dst, F, mm, war=war)

    def q_pipe(jobs):
        pend = []
        for (qsrc, dst, F, aft, war) in jobs:
            k = q_front(qsrc, F, after=aft)
            mm = q_mid(k, F)
            pend.append((k, dst, F, mm, war))
            if len(pend) == 2:
                kk, dd, FF, m2, w2 = pend.pop(0)
                q_back(kk, dd, FF, m2, war=w2)
        for (kk, dd, FF, m2, w2) in pend:
            q_back(kk, dd, FF, m2, war=w2)

    wq_cons = [{"scalar": 0, "vector": 0}, {"scalar": 0, "vector": 0}]
    wq_tmark = [0, 0]

    def w_chunks(dram_in, dram_out, R, Cc):
        for r in range(R // 128):
            rsl = slice(r * 128, r * 128 + 128)
            subs = [(0, 1024), (1024, 2048), (2048, 2816)] if Cc == 2816 \
                else [(0, 1024)]
            for (f0, f1) in subs:
                yield (dram_in, dram_out, rsl, f0, f1)

    def w_flush(job):
        k, dram_out, rsl, f0, F, mm = job
        p = k % 2
        s_wq = s_wq2[p]; s_wt = s_wt2[p]
        vb = q_back(k, s_wq[:][:, 0:F], F, mm, war=wq_tmark[p])
        wst = S.now("ws%d" % p)
        for cb in range(F // 128):
            bank = pA if cb % 2 == 0 else pB
            S.op("tensor", lambda Tn, cb=cb, bank=bank, s_wq=s_wq:
                 Tn.matmul(bank[:, 0:128],
                           s_wq[:][:, cb * 128:cb * 128 + 128],
                           s_eye[:], start=True, stop=True),
                 after={"vector": vb, **S.now("cst"),
                        "scalar": tr_mark[cb % 2]})
            S.op("scalar", lambda Sc, cb=cb, bank=bank, s_wt=s_wt:
                 Sc.activation(s_wt[:][:, cb * 128:cb * 128 + 128],
                               bank[:, 0:128], AF.Copy),
                 after={**S.now("tensor"), **wst})
            tr_mark[cb % 2] = S.cnt["scalar"]
        wq_tmark[p] = S.cnt["tensor"]
        S.dma("sync", lambda E_, dram_out=dram_out, rsl=rsl, f0=f0, F=F,
              s_wt=s_wt: E_.dma_start(
                  dram_out[f0:f0 + F, rsl].rearrange("(b p) r -> p b r", p=128),
                  s_wt[:][:, 0:F].rearrange("p (b r) -> p b r", r=128)),
              after=S.now("scalar"), sem="ws%d" % p)

    pend_w = None
    for (dram_in, dram_out, rsl, f0, f1) in list(w_chunks(wg_d, wgT, I, D)) \
            + list(w_chunks(wu_d, wuT, I, D)) + list(w_chunks(wd_d, wdT, D, I)):
        F = f1 - f0
        p = eq_ci[0] % 2
        s_wf = s_wf2[p]
        S.dma("sync", lambda E_, dram_in=dram_in, rsl=rsl, f0=f0, f1=f1,
              s_wf=s_wf: E_.dma_start(s_wf[:][:, 0:f1 - f0],
                                      dram_in[rsl, f0:f1]),
              after=dict(wq_cons[p]), sem="wl%d" % p)
        k = q_front(s_wf[:][:, 0:F], F, after=S.now("wl%d" % p))
        mm = q_mid(k, F)
        wq_cons[p] = {"scalar": S.cnt["scalar"], "vector": S.cnt["vector"]}
        if pend_w is not None:
            w_flush(pend_w)
        pend_w = (k, dram_out, rsl, f0, F, mm)
    w_flush(pend_w)
    wq_done_store = {**S.now("ws0"), **S.now("ws1")}

    ev_mark = {0: 0, 1: 0}
    dn_mark = {0: 0, 1: 0}
    m_wgu = {}
    m_wdn = {}

    def load_wgu(t, oc):
        j = t * NI + oc
        b = j % 2
        aft = dict(wq_done_store)
        if j >= 2:
            aft["tensor"] = m_wgu[j - 2][1]
        S.dma("gpsimd", lambda G, oc=oc, b=b: [
            G.dma_start(wgu[b][:, 0, :, :],
                        wgT.rearrange("(c p) i -> p c i", p=128)
                        [:, :, oc * 128:oc * 128 + 128]),
            G.dma_start(wgu[b][:, 1, :, :],
                        wuT.rearrange("(c p) i -> p c i", p=128)
                        [:, :, oc * 128:oc * 128 + 128]),
        ], after=aft, n=2, sem="lgu_%d" % b)
        m_wgu[j] = ({S._p("lgu_%d" % b): S.cnt[S._p("lgu_%d" % b)]}, None)

    def load_wdn(t, dc):
        j = t * NCH + dc
        b = j % 2
        aft = dict(wq_done_store)
        if j >= 2:
            aft["tensor"] = m_wdn[j - 2][1]
        S.dma("gpsimd", lambda G, dc=dc, b=b: G.dma_start(
            s_wdn[b][:].rearrange("p (c f) -> p c f", f=128),
            wdT.rearrange("(c p) d -> p c d", p=128)
            [:, :, dc * 128:dc * 128 + 128].rearrange("p c d -> p (c d)")
            if False else
            wdT.rearrange("(c p) d -> p c d", p=128)[:, :, dc * 128:dc * 128 + 128]),
            after=aft, sem="ldn_%d" % b)
        m_wdn[j] = ({S._p("ldn_%d" % b): S.cnt[S._p("ldn_%d" % b)]}, None)

    def ln_block4(src3, dst3, sub3=None):
        sub3 = sub3 if sub3 is not None else src3
        mu = s_st[:][:, 0:512]
        ex2 = s_st[:][:, 512:1024]
        var = s_st[:][:, 1024:1536]
        m2 = s_st[:][:, 1536:2048]
        mub = s_stb[:][:, 0:512]
        rsb = s_stb[:][:, 512:1024]
        v_src = S.now("vector")
        for c in range(NCH):
            S.op("scalar", lambda Sc, c=c: Sc.activation(
                s_sq[:][:, (c % 2) * 512:(c % 2) * 512 + 512],
                src3[:, c, :], AF.Square),
                after={"tensor": mm_mark[c % 2], **v_src})
            S.op("tensor", lambda Tn, c=c: [
                Tn.matmul(pS1[:], s_ones[:][:, 0:1], src3[:, c, :],
                          start=(c == 0), stop=(c == NCH - 1)),
                Tn.matmul(pS2[:], s_ones[:][:, 0:1],
                          s_sq[:][:, (c % 2) * 512:(c % 2) * 512 + 512],
                          start=(c == 0), stop=(c == NCH - 1)),
            ], after=S.now("scalar"), n=2)
            mm_mark[c % 2] = S.cnt["tensor"]
        S.op("vector", lambda V: [
            V.tensor_scalar_mul(mu, pS1[:], 1.0 / D),
            V.tensor_scalar_mul(ex2, pS2[:], 1.0 / D),
        ], after=S.now("tensor"), n=2)
        S.op("vector", lambda V: V.tensor_mul(m2, mu, mu))
        S.op("vector", lambda V: V.tensor_copy(mub, mu))
        S.op("vector", lambda V: V.tensor_sub(var, ex2, m2))
        S.op("scalar", lambda Sc: Sc.activation(var, var, AF.Sqrt,
                                                bias=s_eps[:]),
             after=S.now("vector"))
        S.op("vector", lambda V: V.reciprocal(var, var), after=S.now("scalar"))
        S.op("vector", lambda V: V.tensor_copy(rsb, var))
        S.op("tensor", lambda Tn: [
            Tn.matmul(pBC[:], s_ones[:][0:1, 0:128], mub, start=True, stop=True),
            Tn.matmul(pG[:], s_ones[:][0:1, 0:128], rsb, start=True, stop=True),
        ], after=S.now("vector"), n=2)
        for c in range(NCH):
            S.op("vector", lambda V, c=c: V.tensor_tensor(
                s_t1[:][:, 0:512], sub3[:, c, :], pBC[:], ALU.subtract),
                after=S.now("tensor"))
            S.op("vector", lambda V, c=c: V.tensor_tensor(
                dst3[:, c, :], s_t1[:][:, 0:512], pG[:], ALU.mult))

    load_wgu(0, 0)
    load_wgu(0, 1)
    load_wdn(0, 0)
    load_wdn(0, 1)

    for t in range(NT):
        S.dma("sync", lambda E_, t=t: E_.dma_start(x3, xap(x_in, t)),
              after=S.now("vector", "sxo"), sem="sx")
        m_x = S.now("sx")
        for c in range(NCH):
            S.op("scalar", lambda Sc, c=c: Sc.activation(
                xh3[:, c, :], x3[:, c, :], AF.Copy),
                after={**m_x, **S.now("vector")})
        x323 = s_x32[:].rearrange("p (c n) -> p c n", n=512)
        ln_block4(xh3, x323, sub3=x3)
        xh_done = S.now("vector")
        xtmq = s_xtmq[:].rearrange("p (g f) -> p g f", f=D)
        for tg in range(4):
            for half in range(2):
                bank = pA if half == 0 else pB
                S.op("tensor", lambda Tn, tg=tg, half=half, bank=bank: [
                    Tn.matmul(bank[:, ci * 128:ci * 128 + 128],
                              x323[:, half * 4 + ci, tg * 128:tg * 128 + 128],
                              s_eyf[:], start=(ci == 0), stop=(ci == 3))
                    for ci in range(4)],
                    after={**S.now("vector", "cst"),
                           "scalar": tr_mark[half]}, n=4)
                S.op("scalar", lambda Sc, tg=tg, half=half, bank=bank:
                     Sc.activation(xtm[:, tg, half * 512:half * 512 + 512],
                                   bank[:], AF.Copy), after=S.now("tensor"))
                tr_mark[half] = S.cnt["scalar"]
        xq_src_rdy = S.now("scalar")
        q_pipe([(xtm[:, tg, :], xtmq[:, tg, :], D, xq_src_rdy, 0)
                for tg in range(4)])
        for c in range(NCH):
            half = c % 2
            bank = pA if half == 0 else pB
            S.op("tensor", lambda Tn, c=c, bank=bank: [
                Tn.matmul(bank[:, tg * 128:tg * 128 + 128],
                          xtmq[:, tg, c * 128:c * 128 + 128],
                          s_eye[:], start=(tg == 0), stop=(tg == 3))
                for tg in range(4)],
                after={**S.now("vector"), "scalar": tr_mark[half]}, n=4)
            S.op("scalar", lambda Sc, c=c, bank=bank: Sc.activation(
                xq3[:, c, :], bank[:], AF.Copy), after=S.now("tensor"))
            tr_mark[half] = S.cnt["scalar"]
        xq_done = S.now("scalar")

        pcd = [(pC, pD), (pBC, pG)]
        sil_mark = [S.cnt["scalar"]] * 2
        hmul_mark = [S.cnt["vector"]] * 2
        htm_mark = {}
        hq_done_m = {}

        def stageA1(oc):
            p = oc % 2
            j = t * NI + oc
            b = j % 2
            bC, bD = pcd[p]
            aft = {**m_wgu[j][0],
                   "scalar": max(xq_done["scalar"], sil_mark[p]),
                   "vector": hmul_mark[p]}
            S.op("tensor", lambda Tn, b=b, bC=bC, bD=bD: [
                Tn.matmul(bC[:], wgu[b][:, 0, c2, :], xq3[:, c2, :],
                          start=(c2 == 0), stop=(c2 == NCH - 1))
                for c2 in range(NCH)] + [
                Tn.matmul(bD[:], wgu[b][:, 1, c2, :], xq3[:, c2, :],
                          start=(c2 == 0), stop=(c2 == NCH - 1))
                for c2 in range(NCH)],
                after=aft, n=2 * NCH)
            m_wgu[j] = (m_wgu[j][0], S.cnt["tensor"])
            if j + 2 < NT * NI:
                load_wgu(*divmod(j + 2, NI))

        def stageA2(oc):
            p = oc % 2
            bC, bD = pcd[p]
            S.op("scalar", lambda Sc, p=p, bC=bC: Sc.activation(
                s_sg2[p][:], bC[:], AF.Silu),
                 after=S.now("tensor", "vector"))
            sil_mark[p] = S.cnt["scalar"]
            S.op("vector", lambda V, p=p, bD=bD: V.tensor_tensor(
                s_hb2[p][:], s_sg2[p][:], bD[:], ALU.mult),
                 after=S.now("scalar", "tensor"))
            hmul_mark[p] = S.cnt["vector"]
            S.op("tensor", lambda Tn, p=p: [
                Tn.matmul(pA[:, tg * 128:tg * 128 + 128],
                          s_hb2[p][:][:, tg * 128:tg * 128 + 128],
                          s_eye[:], start=(tg == 0), stop=(tg == 3))
                for tg in range(4)],
                after={**S.now("vector"), "scalar": tr_mark[0]}, n=4)
            S.op("scalar", lambda Sc, p=p: Sc.activation(
                s_htm2[p][:], pA[:], AF.Copy), after=S.now("tensor"))
            tr_mark[0] = S.cnt["scalar"]
            htm_mark[oc] = S.cnt["scalar"]

        btr_mark = [S.cnt["tensor"]] * 2

        def stageBq(oc):
            p = oc % 2
            emit_quant(s_htm2[p][:], s_htm2[p][:], 512,
                       after={"scalar": htm_mark[oc]}, war=btr_mark[p])
            hq_done_m[oc] = {"vector": S.cnt["vector"],
                             "scalar": S.cnt["scalar"]}

        def stageBt(oc):
            p = oc % 2
            S.op("tensor", lambda Tn, p=p: [
                Tn.matmul(pB[:, tg * 128:tg * 128 + 128],
                          s_htm2[p][:][:, tg * 128:tg * 128 + 128],
                          s_eye[:], start=(tg == 0), stop=(tg == 3))
                for tg in range(4)],
                after={"vector": hq_done_m[oc]["vector"],
                       "scalar": tr_mark[1]}, n=4)
            btr_mark[p] = S.cnt["tensor"]
            S.op("scalar", lambda Sc, oc=oc: Sc.activation(
                hq3[:, oc, :], pB[:], AF.Copy), after=S.now("tensor"))
            tr_mark[1] = S.cnt["scalar"]

        stageA1(0)
        stageA2(0)
        stageA1(1)
        stageBq(0)
        stageA2(1)
        for oc in range(2, NI):
            stageA1(oc)
            stageBq(oc - 1)
            stageBt(oc - 2)
            stageA2(oc)
        stageBq(NI - 1)
        stageBt(NI - 2)
        stageBt(NI - 1)
        hq_done = S.now("scalar")

        v_hq = S.cnt["vector"]
        for dc in range(NCH):
            j = t * NCH + dc
            b = j % 2
            bank = pC if dc % 2 == 0 else pD
            aft = {**m_wdn[j][0], "scalar": hq_done["scalar"],
                   "vector": max(dn_mark[dc % 2], v_hq)}
            S.op("tensor", lambda Tn, dc=dc, b=b, bank=bank: [
                Tn.matmul(bank[:], s_wdn[b][:].rearrange(
                    "p (c f) -> p c f", f=128)[:, c2, :], hq3[:, c2, :],
                    start=(c2 == 0), stop=(c2 == NI - 1))
                for c2 in range(NI)],
                after=aft, n=NI)
            m_wdn[j] = (m_wdn[j][0], S.cnt["tensor"])
            if j + 2 < NT * NCH:
                load_wdn(*divmod(j + 2, NCH))
            S.op("vector", lambda V, dc=dc, bank=bank: V.tensor_add(
                x3[:, dc, :], x3[:, dc, :], bank[:]), after=S.now("tensor"))
            dn_mark[dc % 2] = S.cnt["vector"]
        S.dma("sync", lambda E_, t=t: E_.dma_start(xap(x_out, t), x3),
              after=S.now("vector"), sem="sxo")

    S.wait_only("sync", S.now("sxo"))


def build_phase4(Ttok=T):
    nc = bass.Bass("TRN2", target_bir_lowering=False, debug=False)
    S = Sched()
    banks = _mk_banks(nc)
    x_in = nc.dram_tensor("x_in", [D, Ttok], F32, kind="ExternalInput").ap()
    x_out = nc.dram_tensor("x_out", [D, Ttok], F32, kind="ExternalOutput").ap()
    gen_phase4(nc, S, nc.alloc_sbuf_tensor, _mk_in(nc, ""), banks,
               x_in, x_out, Ttok)
    S.emit(nc)
    return nc


def prep_phase4_inputs(xcore, inp, fm=False):
    return {
        "x_in": np.ascontiguousarray(np.asarray(xcore, np.float32)) if fm
        else np.ascontiguousarray(np.asarray(xcore, np.float32).T),
        "wg": np.ascontiguousarray(np.asarray(inp["L1_ffn_w_gate"], np.float32)),
        "wu": np.ascontiguousarray(np.asarray(inp["L1_ffn_w_up"], np.float32)),
        "wd": np.ascontiguousarray(np.asarray(inp["L1_ffn_w_down"], np.float32)),
        "eyeb": np.eye(128, dtype=np.float32).astype(BF),
        "eyef4": np.eye(128, dtype=np.float32),
    }


GAMMA = 0.99


def gen_phase3(nc, S, sb, din, banks, x_in, x_out, Ttok, upto=8, mixn=9999):
    NT = Ttok // 512
    HD = 512
    wq_d = din("wq", [D, HD], F32)
    wk_d = din("wk", [D, HD], F32)
    wv_d = din("wv", [D, HD], F32)
    wo_d = din("wo", [HD, D], F32)
    upi_d = din("upi", [128, 512], BF16)
    ups_d = din("ups", [128, 512], BF16)
    gp1_d = din("gp1", [128, 128], BF16)
    gm1_d = din("gm1", [128, 128], BF16)
    gp0_d = din("gp0", [128, 128], BF16)
    glc_d = din("glc", [128, 1], F32)
    e32_d = din("e32", [128, 32], BF16)
    geL_d = din("geL", [128, 32], BF16)
    geLF_d = din("geLF", [128, 128], BF16)
    s_x = sb("s_x", [128, NCH * 512], F32)
    s_xh = sb("s_xh", [128, NCH * 512], BF16)
    s_q = sb("s_q", [128, 4 * 512], BF16)
    s_k = sb("s_k", [128, 4 * 512], BF16)
    s_v = sb("s_v", [128, 4 * 512], BF16)
    s_o = sb("s_o", [128, 4 * 512], BF16)
    s_wq = sb("s_wq", [128, NCH * HD], BF16)
    s_wk = sb("s_wk", [128, NCH * HD], BF16)
    s_wv = sb("s_wv", [128, NCH * HD], BF16)
    s_wo = sb("s_wo", [128, 4 * D], BF16)
    s_ST = sb("s_ST", [128, 4 * 96], BF16)
    s_upi = sb("s_upi", [128, 512], BF16)
    s_ups = sb("s_ups", [128, 512], BF16)
    s_gp1 = sb("s_gp1", [128, 128], BF16)
    s_gm1 = sb("s_gm1", [128, 128], BF16)
    s_gp0 = sb("s_gp0", [128, 128], BF16)
    s_glc = sb("s_glc", [128, 1], F32)
    s_e32 = sb("s_e32", [128, 32], BF16)
    s_geL = sb("s_geL", [128, 32], BF16)
    s_geLF = sb("s_geLF", [128, 128], BF16)
    s_Qg = sb("s_Qg", [128, 128], BF16)
    s_Qm = sb("s_Qm", [128, 128], BF16)
    s_Km = sb("s_Km", [128, 128], BF16)
    s_Ks = sb("s_Ks", [128, 128], BF16)
    s_Wt = sb("s_Wt", [128, 512], BF16)
    s_T2 = sb("s_T2", [128, 512], BF16)
    s_MA = sb("s_MA", [128, 512], BF16)
    s_Ktm = sb("s_Ktm", [128, 128], BF16)
    s_Vtm = sb("s_Vtm", [128, 128], BF16)
    s_Qtm = sb("s_Qtm", [128, 128], BF16)
    s_KtmL = sb("s_KtmL", [128, 128], BF16)
    s_QtmL = sb("s_QtmL", [128, 128], BF16)
    s_U = sb("s_U", [128, 128], BF16)
    s_a = sb("s_a", [128, 128], BF16)
    s_ag = sb("s_ag", [128, 128], BF16)
    s_tst = sb("s_tst", [128, 96], F32)
    s_ones = sb("s_ones", [128, 512], BF16)
    s_eps = sb("s_epsP", [1, 1], F32)
    s_sq = sb("s_sq", [128, 2 * 512], BF16)
    s_st = sb("s_st", [1, 4 * 512], F32)
    s_stb = sb("s_stb", [1, 2 * 512], BF16)
    s_t1 = sb("s_t1", [128, 512], F32)

    (bk6, pS1, pS2) = banks
    pA, pB, pC, pD, pE, pF = bk6

    x3 = s_x[:].rearrange("p (c n) -> p c n", n=512)
    xh3 = s_xh[:].rearrange("p (c n) -> p c n", n=512)
    q3 = s_q[:].rearrange("p (g n) -> p g n", n=512)
    k3 = s_k[:].rearrange("p (g n) -> p g n", n=512)
    v3 = s_v[:].rearrange("p (g n) -> p g n", n=512)
    o3 = s_o[:].rearrange("p (g n) -> p g n", n=512)
    wq3 = s_wq[:].rearrange("p (c j) -> p c j", j=HD)
    wk3 = s_wk[:].rearrange("p (c j) -> p c j", j=HD)
    wv3 = s_wv[:].rearrange("p (c j) -> p c j", j=HD)
    wo3 = s_wo[:].rearrange("p (c j) -> p c j", j=D)
    st3 = s_ST[:].rearrange("p (g w) -> p g w", w=96)

    def xap(dram, t):
        return dram.rearrange("(c p) n -> p c n", p=128)[:, :, t * 512:(t + 1) * 512]

    S.dma("sync", lambda E_: [
        E_.dma_start(s_upi[:], upi_d),
        E_.dma_start(s_ups[:], ups_d),
        E_.dma_start(s_gp1[:], gp1_d),
        E_.dma_start(s_gm1[:], gm1_d),
        E_.dma_start(s_gp0[:], gp0_d),
        E_.dma_start(s_glc[:], glc_d),
        E_.dma_start(s_e32[:], e32_d),
        E_.dma_start(s_geL[:], geL_d),
        E_.dma_start(s_geLF[:], geLF_d),
    ], n=9, sem="cst")
    S.dma("gpsimd", lambda G: [
        G.dma_start(s_wq[:].rearrange("p (c j) -> p c j", j=HD),
                    wq_d.rearrange("(c p) j -> p c j", p=128)),
        G.dma_start(s_wk[:].rearrange("p (c j) -> p c j", j=HD),
                    wk_d.rearrange("(c p) j -> p c j", p=128)),
        G.dma_start(s_wv[:].rearrange("p (c j) -> p c j", j=HD),
                    wv_d.rearrange("(c p) j -> p c j", p=128)),
        G.dma_start(s_wo[:].rearrange("p (c j) -> p c j", j=D),
                    wo_d.rearrange("(c p) j -> p c j", p=128)),
    ], n=4, sem="wts")
    m_wts = S.now("wts")
    S.op("vector", lambda V: [
        V.memset(s_ones[:], 1.0),
        V.memset(s_eps[:], LN_EPS),
        V.memset(s_ST[:], 0.0),
    ], n=3)
    cst_now = S.now("cst")

    mm_mark = {0: 0, 1: 0}
    evac_mark = {0: 0, 1: 0}
    oproj_mark = {0: 0, 1: 0}
    st_read_mark = 0
    st_read_vec = 0

    def ln_block3(src3, sub3, dst3):
        mu = s_st[:][:, 0:512]
        ex2 = s_st[:][:, 512:1024]
        var = s_st[:][:, 1024:1536]
        m2 = s_st[:][:, 1536:2048]
        mub = s_stb[:][:, 0:512]
        rsb = s_stb[:][:, 512:1024]
        v_src = S.now("vector")
        for c in range(NCH):
            S.op("scalar", lambda Sc, c=c: Sc.activation(
                s_sq[:][:, (c % 2) * 512:(c % 2) * 512 + 512],
                src3[:, c, :], AF.Square),
                after={"tensor": mm_mark[c % 2], **v_src})
            S.op("tensor", lambda Tn, c=c: [
                Tn.matmul(pS1[:], s_ones[:][:, 0:1], src3[:, c, :],
                          start=(c == 0), stop=(c == NCH - 1)),
                Tn.matmul(pS2[:], s_ones[:][:, 0:1],
                          s_sq[:][:, (c % 2) * 512:(c % 2) * 512 + 512],
                          start=(c == 0), stop=(c == NCH - 1)),
            ], after=S.now("scalar"), n=2)
            mm_mark[c % 2] = S.cnt["tensor"]
        S.op("vector", lambda V: [
            V.tensor_scalar_mul(mu, pS1[:], 1.0 / D),
            V.tensor_scalar_mul(ex2, pS2[:], 1.0 / D),
        ], after=S.now("tensor"), n=2)
        S.op("vector", lambda V: V.tensor_mul(m2, mu, mu))
        S.op("vector", lambda V: V.tensor_copy(mub, mu))
        S.op("vector", lambda V: V.tensor_sub(var, ex2, m2))
        S.op("scalar", lambda Sc: Sc.activation(var, var, AF.Sqrt,
                                                bias=s_eps[:]),
             after=S.now("vector"))
        S.op("vector", lambda V: V.reciprocal(var, var), after=S.now("scalar"))
        S.op("vector", lambda V: V.tensor_copy(rsb, var))
        S.op("tensor", lambda Tn: [
            Tn.matmul(pC[:], s_ones[:][0:1, 0:128], mub, start=True, stop=True),
            Tn.matmul(pD[:], s_ones[:][0:1, 0:128], rsb, start=True, stop=True),
        ], after=S.now("vector"), n=2)
        for c in range(NCH):
            S.op("vector", lambda V, c=c: V.tensor_tensor(
                s_t1[:], sub3[:, c, :], pC[:], ALU.subtract),
                after=S.now("tensor"))
            S.op("vector", lambda V, c=c: V.tensor_tensor(
                dst3[:, c, :], s_t1[:], pD[:], ALU.mult))

    for t in range(NT):
        S.dma("sync", lambda E_, t=t: E_.dma_start(x3, xap(x_in, t)),
              after=S.now("vector", "sxo"), sem="sx")
        m_x = S.now("sx")
        for c in range(NCH):
            S.op("scalar", lambda Sc, c=c: Sc.activation(
                xh3[:, c, :], x3[:, c, :], AF.Copy),
                after={**m_x, **S.now("vector")})
        ln_block3(xh3, xh3, xh3)
        xh_done = S.now("vector")

        for (wsrc, dst3) in ((wq3, q3), (wk3, k3), (wv3, v3)):
            for oc in range(4):
                bank = pA if (oc % 2 == 0) else pB
                S.op("tensor", lambda Tn, oc=oc, bank=bank, wsrc=wsrc: [
                    Tn.matmul(bank[:], wsrc[:, c2, oc * 128:oc * 128 + 128],
                              xh3[:, c2, :], start=(c2 == 0),
                              stop=(c2 == NCH - 1))
                    for c2 in range(NCH)],
                    after={**m_wts, "vector": xh_done["vector"],
                           "scalar": evac_mark[oc % 2]}, n=NCH)
                S.op("scalar", lambda Sc, oc=oc, bank=bank, dst3=dst3:
                     Sc.activation(dst3[:, oc, :], bank[:], AF.Copy),
                     after=S.now("tensor"))
                evac_mark[oc % 2] = S.cnt["scalar"]
        proj_done = S.now("scalar")

        bk = [pA, pB, pC, pD]
        for cc in range(4):
            csl = slice(cc * 128, cc * 128 + 128)
            for g in range(4):
                Qf = q3[:, g, csl]
                Kf = k3[:, g, csl]
                Vf = v3[:, g, csl]
                A0 = [st3[32 * m:32 * m + 32, g, 0:32] for m in range(4)]
                C0 = [st3[32 * m:32 * m + 32, g, 32:64] for m in range(4)]
                G0 = [st3[32 * m:32 * m + 32, g, 64:96] for m in range(4)]
                if upto < 1:
                    continue
                S.op("vector", lambda V, Qf=Qf: V.tensor_mul(
                    s_Qg[:], Qf, s_gp1[:]),
                    after={**cst_now, "vector": st_read_vec,
                           "scalar": max(proj_done["scalar"], st_read_mark)})
                S.op("vector", lambda V, Qf=Qf: V.tensor_mul(
                    s_Qm[:], Qf, s_gm1[:]))
                S.op("vector", lambda V, Kf=Kf: V.tensor_mul(
                    s_Km[:], Kf, s_gm1[:]))
                S.op("vector", lambda V, Kf=Kf: V.tensor_mul(
                    s_Ks[:], Kf, s_gp0[:]))
                sc_done = S.now("vector")
                if upto == 13:
                    continue
                S.op("tensor", lambda Tn: [
                    Tn.matmul(bk[m][:, 0:128],
                              s_Km[:][32 * m:32 * m + 32, :],
                              s_Qg[:][32 * m:32 * m + 32, :],
                              start=True, stop=True,
                              tile_position=(32 * m, 0),
                              skip_group_check=True)
                    for m in range(4)], after=sc_done, n=4)
                S.op("tensor", lambda Tn: [
                    Tn.matmul(bk[m][:, 128:256],
                              s_Qm[:][32 * m:32 * m + 32, :],
                              s_Ks[:][32 * m:32 * m + 32, :],
                              start=True, stop=True,
                              tile_position=(32 * m, 0),
                              skip_group_check=True)
                    for m in range(4)], n=4)
                if upto == 12:
                    continue
                S.op("vector", lambda V: [
                    V.tensor_tensor(s_Wt[:][:, m * 128:m * 128 + 128],
                                    bk[m][:, 0:128], s_upi[:][:, 0:128],
                                    ALU.mult)
                    for m in range(4)], after=S.now("tensor"), n=4)
                S.op("vector", lambda V: [
                    V.tensor_tensor(s_T2[:][:, m * 128:m * 128 + 128],
                                    bk[m][:, 128:256], s_ups[:][:, 0:128],
                                    ALU.mult)
                    for m in range(4)], n=4)
                wt_done = S.now("vector")
                if upto < 2:
                    continue
                for (srcf, dsttm, f0) in ((Kf, s_Ktm, 384), (Vf, s_Vtm, 416),
                                          (Qf, s_Qtm, 448)):
                    S.op("tensor", lambda Tn, srcf=srcf, f0=f0: [
                        Tn.matmul(bk[m][:, f0:f0 + 32],
                                  srcf[32 * m:32 * m + 32, :],
                                  s_e32[:][32 * m:32 * m + 32, :],
                                  start=True, stop=True,
                                  tile_position=(32 * m, 0),
                                  skip_group_check=True)
                        for m in range(4)],
                        after=S.now("vector", "scalar"), n=4)
                    S.op("scalar", lambda Sc, dsttm=dsttm, f0=f0: [
                        Sc.activation(dsttm[:][:, m * 32:m * 32 + 32],
                                      bk[m][:, f0:f0 + 32], AF.Copy)
                        for m in range(4)], after=S.now("tensor"), n=4)
                if upto < 3:
                    continue
                S.op("vector", lambda V: V.tensor_scalar(
                    s_KtmL[:], s_Ktm[:], s_glc[:], None, ALU.mult),
                    after=S.now("scalar"))
                S.op("vector", lambda V: V.tensor_scalar(
                    s_QtmL[:], s_Qtm[:], s_glc[:], None, ALU.mult))
                tm_done = S.now("vector", "scalar")
                if upto < 4:
                    continue
                S.op("tensor", lambda Tn, C0=C0: [
                    Tn.matmul(bk[m][:, 480:512],
                              s_Ks[:][32 * m:32 * m + 32, :], C0[m],
                              start=True, stop=False,
                              tile_position=(32 * m, 0),
                              skip_group_check=True)
                    for m in range(4)],
                    after={**tm_done, "scalar": st_read_mark,
                           "vector": max(tm_done.get("vector", 0), st_read_vec)}, n=4)
                S.op("tensor", lambda Tn: [
                    Tn.matmul(bk[m][:, 480:512],
                              s_T2[:][:, m * 128:m * 128 + 128],
                              s_Vtm[:][:, m * 32:m * 32 + 32],
                              start=False, stop=True,
                              skip_group_check=True)
                    for m in range(4)], n=4)
                S.op("vector", lambda V: [
                    V.tensor_scalar_mul(s_U[:][:, m * 32:m * 32 + 32],
                                        bk[m][:, 480:512], -GAMMA)
                    for m in range(4)], after=S.now("tensor"), n=4)
                if upto < 5:
                    continue
                S.op("tensor", lambda Tn, A0=A0: [
                    mmi for m in range(4) for mmi in [
                        Tn.matmul(pE[32 * m:32 * m + 32, 0:128],
                                  s_Ktm[:][:, m * 32:m * 32 + 32],
                                  s_Wt[:][:, m * 128:m * 128 + 128],
                                  start=True, stop=False,
                                  tile_position=(0, 32 * m),
                                  skip_group_check=True),
                        Tn.matmul(pE[32 * m:32 * m + 32, 0:128], A0[m],
                                  s_Qg[:][32 * m:32 * m + 32, :],
                                  start=False, stop=True,
                                  tile_position=(32 * m, 32 * m),
                                  skip_group_check=True),
                    ]], after=S.now("vector"), n=8)
                S.op("scalar", lambda Sc: Sc.activation(
                    s_a[:], pE[:, 0:128], AF.Copy), after=S.now("tensor"))
                S.op("vector", lambda V: V.tensor_mul(
                    s_ag[:], s_a[:], s_gp1[:]), after=S.now("scalar"))
                if upto < 6:
                    continue
                S.op("tensor", lambda Tn: [
                    Tn.matmul(bk[m][:, 256:384],
                              s_Qm[:][32 * m:32 * m + 32, :],
                              s_ag[:][32 * m:32 * m + 32, :],
                              start=True, stop=True,
                              tile_position=(32 * m, 0),
                              skip_group_check=True)
                    for m in range(4)], after=S.now("vector"), n=4)
                S.op("vector", lambda V: [
                    V.tensor_tensor(s_MA[:][:, m * 128:m * 128 + 128],
                                    bk[m][:, 256:384], s_upi[:][:, 0:128],
                                    ALU.mult)
                    for m in range(4)], after=S.now("tensor"), n=4)
                ma_done = S.now("vector")
                if upto < 7:
                    continue
                S.op("tensor", lambda Tn, C0=C0, G0=G0: [
                    mmi for m in range(4) for mmi in [
                        Tn.matmul(pE[32 * m:32 * m + 32, 128:256],
                                  s_Vtm[:][:, m * 32:m * 32 + 32],
                                  s_MA[:][:, m * 128:m * 128 + 128],
                                  start=True, stop=False,
                                  tile_position=(0, 32 * m),
                                  skip_group_check=True),
                        Tn.matmul(pE[32 * m:32 * m + 32, 128:256], C0[m],
                                  s_ag[:][32 * m:32 * m + 32, :],
                                  start=False, stop=False,
                                  tile_position=(32 * m, 32 * m),
                                  skip_group_check=True),
                        Tn.matmul(pE[32 * m:32 * m + 32, 128:256], G0[m],
                                  s_Qg[:][32 * m:32 * m + 32, :],
                                  start=False, stop=False,
                                  tile_position=(32 * m, 32 * m),
                                  skip_group_check=True),
                        Tn.matmul(pE[32 * m:32 * m + 32, 128:256],
                                  s_U[:][:, m * 32:m * 32 + 32],
                                  s_Wt[:][:, m * 128:m * 128 + 128],
                                  start=False, stop=True,
                                  tile_position=(0, 32 * m),
                                  skip_group_check=True),
                    ]], after=ma_done, n=16)
                S.op("scalar", lambda Sc, g=g, csl=csl: Sc.activation(
                    o3[:, g, csl], pE[:, 128:256], AF.Copy),
                    after=S.now("tensor"))
                if upto == 9:
                    S.op("tensor", lambda Tn: [
                        mmi for m in range(4) for mmi in [
                            Tn.matmul(pE[32 * m:32 * m + 32, 256:288],
                                      s_KtmL[:][:, m * 32:m * 32 + 32],
                                      s_Ktm[:][:, m * 32:m * 32 + 32],
                                      start=True, stop=True,
                                      tile_position=(0, 32 * m),
                                      skip_group_check=True),
                            Tn.matmul(pE[32 * m:32 * m + 32, 288:320],
                                      s_QtmL[:][:, m * 32:m * 32 + 32],
                                      s_Vtm[:][:, m * 32:m * 32 + 32],
                                      start=True, stop=True,
                                      tile_position=(0, 32 * m),
                                      skip_group_check=True),
                            Tn.matmul(pE[32 * m:32 * m + 32, 320:352],
                                      s_KtmL[:][:, m * 32:m * 32 + 32],
                                      s_U[:][:, m * 32:m * 32 + 32],
                                      start=True, stop=True,
                                      tile_position=(0, 32 * m),
                                      skip_group_check=True),
                        ]], n=12)
                    S.op("scalar", lambda Sc, g=g: Sc.activation(
                        st3[:, g, :], pE[:, 256:352], AF.Copy),
                        after=S.now("tensor"))
                    st_read_mark = S.cnt["scalar"]
                    continue
                if upto == 10:
                    S.op("tensor", lambda Tn, A0=A0, C0=C0, G0=G0: [
                        mmi for m in range(4) for mmi in [
                            Tn.matmul(pE[32 * m:32 * m + 32, 256:288],
                                      s_geL[:][32 * m:32 * m + 32, :], A0[m],
                                      start=True, stop=True,
                                      tile_position=(32 * m, 32 * m),
                                      skip_group_check=True),
                            Tn.matmul(pE[32 * m:32 * m + 32, 288:320],
                                      s_geL[:][32 * m:32 * m + 32, :], C0[m],
                                      start=True, stop=True,
                                      tile_position=(32 * m, 32 * m),
                                      skip_group_check=True),
                            Tn.matmul(pE[32 * m:32 * m + 32, 320:352],
                                      s_geL[:][32 * m:32 * m + 32, :], G0[m],
                                      start=True, stop=True,
                                      tile_position=(32 * m, 32 * m),
                                      skip_group_check=True),
                        ]], n=12)
                    S.op("scalar", lambda Sc, g=g: Sc.activation(
                        st3[:, g, :], pE[:, 256:352], AF.Copy),
                        after=S.now("tensor"))
                    st_read_mark = S.cnt["scalar"]
                    continue
                if upto < 8:
                    continue
                S.op("tensor", lambda Tn: [
                    Tn.matmul(pE[32 * m:32 * m + 32, 256 + r * 32:288 + r * 32],
                              lhs_tm[:][:, m * 32:m * 32 + 32],
                              rhs_tm[:][:, m * 32:m * 32 + 32],
                              start=True, stop=True,
                              tile_position=(0, 32 * m),
                              skip_group_check=True)
                    for (r, lhs_tm, rhs_tm) in (
                        (0, s_KtmL, s_Ktm), (1, s_QtmL, s_Vtm),
                        (2, s_KtmL, s_U))
                    for m in range(4)], n=12)
                S.op("tensor", lambda Tn: [
                    Tn.matmul(pF[:, r * 32:r * 32 + 32], s_geLF[:],
                              st3[:, g, r * 32:r * 32 + 32],
                              start=True, stop=True, skip_group_check=True)
                    for r in range(3)], after={"vector": st_read_vec}, n=3)
                S.op("scalar", lambda Sc: Sc.activation(
                    s_tst[:], pE[:, 256:352], AF.Copy), after=S.now("tensor"))
                S.op("vector", lambda V, g=g: V.tensor_tensor(
                    st3[:, g, :], s_tst[:], pF[:, 0:96], ALU.add),
                    after=S.now("scalar"))
                st_read_mark = S.cnt["scalar"]
                st_read_vec = S.cnt["vector"]

        o_done = S.now("scalar")
        v_mix = S.cnt["vector"]
        for dc in range(NCH):
            bank = pA if (dc % 2 == 0) else pB
            S.op("tensor", lambda Tn, dc=dc, bank=bank: [
                Tn.matmul(bank[:], wo3[:, c2, dc * 128:dc * 128 + 128],
                          o3[:, c2, :], start=(c2 == 0), stop=(c2 == 3))
                for c2 in range(4)],
                after={**o_done, "vector": max(oproj_mark[dc % 2], v_mix)},
                n=4)
            S.op("vector", lambda V, dc=dc, bank=bank: V.tensor_add(
                x3[:, dc, :], x3[:, dc, :], bank[:]), after=S.now("tensor"))
            oproj_mark[dc % 2] = S.cnt["vector"]
        S.dma("sync", lambda E_, t=t: E_.dma_start(xap(x_out, t), x3),
              after=S.now("vector"), sem="sxo")

    S.wait_only("sync", S.now("sxo"))


def build_phase3(Ttok=T, upto=8, mixn=9999):
    nc = bass.Bass("TRN2", target_bir_lowering=False, debug=False)
    S = Sched()
    banks = _mk_banks(nc)
    x_in = nc.dram_tensor("x_in", [D, Ttok], F32, kind="ExternalInput").ap()
    x_out = nc.dram_tensor("x_out", [D, Ttok], F32, kind="ExternalOutput").ap()
    gen_phase3(nc, S, nc.alloc_sbuf_tensor, _mk_in(nc, ""), banks,
               x_in, x_out, Ttok, upto=upto, mixn=mixn)
    S.emit(nc)
    return nc


def prep_phase3_inputs(xcore, inp, fm=False):
    L = 128
    g = GAMMA
    p = np.arange(128, dtype=np.float64)
    upi = np.triu(np.ones((128, 128), np.float32))
    ups = np.triu(np.ones((128, 128), np.float32), 1)
    gp1 = np.broadcast_to((g ** (p + 1)).astype(np.float32), (128, 128))
    gm1 = np.broadcast_to((g ** -(p + 1)).astype(np.float32), (128, 128))
    gp0 = np.broadcast_to((g ** p).astype(np.float32), (128, 128))
    glc = (g ** (127 - p)).astype(np.float32)[:, None]
    e32 = np.zeros((128, 32), np.float32)
    for i in range(128):
        e32[i, i % 32] = 1.0
    geL = (e32 * (g ** 128)).astype(np.float32)
    return {
        "x_in": np.ascontiguousarray(np.asarray(xcore, np.float32)) if fm
        else np.ascontiguousarray(np.asarray(xcore, np.float32).T),
        "wq": np.ascontiguousarray(np.asarray(inp["L1_attn_Wq"], np.float32)),
        "wk": np.ascontiguousarray(np.asarray(inp["L1_attn_Wk"], np.float32)),
        "wv": np.ascontiguousarray(np.asarray(inp["L1_attn_Wv"], np.float32)),
        "wo": np.ascontiguousarray(np.asarray(inp["L1_attn_Wo"], np.float32)),
        "upi": np.ascontiguousarray(np.tile(upi, (1, 4)).astype(BF)),
        "ups": np.ascontiguousarray(np.tile(ups, (1, 4)).astype(BF)),
        "gp1": np.ascontiguousarray(gp1.astype(BF)),
        "gm1": np.ascontiguousarray(gm1.astype(BF)),
        "gp0": np.ascontiguousarray(gp0.astype(BF)),
        "glc": np.ascontiguousarray(glc),
        "e32": np.ascontiguousarray(e32.astype(BF)),
        "geL": np.ascontiguousarray(geL.astype(BF)),
        "geLF": np.ascontiguousarray((np.eye(128, dtype=np.float32) * (g ** 128)).astype(BF)),
    }


import contextlib


def build_fused(Ttok=T):
    nc = bass.Bass("TRN2", target_bir_lowering=False, debug=False)
    S = Sched()
    banks = _mk_banks(nc)
    x_in = nc.dram_tensor("x_in", [D, Ttok], F32, kind="ExternalInput").ap()
    xA = nc.dram_tensor("xA", [D, Ttok], F32, kind="Internal").ap()
    xB = nc.dram_tensor("xB", [D, Ttok], F32, kind="Internal").ap()
    xC = nc.dram_tensor("xC", [D, Ttok], F32, kind="Internal").ap()
    x_out = nc.dram_tensor("x_out", [D, Ttok], F32, kind="ExternalOutput").ap()
    chain = [x_in, xA, xB, xC, x_out]
    gens = [(gen_phase1, "p1_"), (gen_phase2, "p2_"),
            (gen_phase3, "p3_"), (gen_phase4, "p4_")]
    for i, (gen, pref) in enumerate(gens):
        S.pref = pref
        stack = contextlib.ExitStack()

        def sb(name, shape, dt_, stack=stack, pref=pref):
            return stack.enter_context(nc.sbuf_tensor(pref + name, shape, dt_))

        gen(nc, S, sb, _mk_in(nc, pref), banks, chain[i], chain[i + 1], Ttok)
        stack.close()
        if i < len(gens) - 1:
            S.barrier_all()
    S.pref = ""
    S.emit(nc)
    return nc


PREFS = ["p1_", "p2_", "p3_", "p4_"]
PREPS_ALL = None


def fused_in_map(xcore_fm, inputs):
    m = {"x_in": np.ascontiguousarray(xcore_fm)}
    preps = [prep_phase1_inputs, prep_phase2_inputs, prep_phase3_inputs,
             prep_phase4_inputs]
    for pref, prep in zip(PREFS, preps):
        d = prep(xcore_fm, inputs, fm=True)
        for k, v in d.items():
            if k == "x_in":
                continue
            m[pref + k] = v
    return m


def kernel(**inputs):
    x = np.asarray(inputs["L0_attn_x"], np.float32)
    nc = build_fused()
    in_maps = [fused_in_map(x[c].T, inputs) for c in range(8)]
    res = run_bass_kernel_spmd(nc, in_maps, core_ids=list(range(8)))
    return np.stack([np.asarray(r["x_out"], np.float32).T
                     for r in res.results])


def kernel_unfused(**inputs):
    x = np.asarray(inputs["L0_attn_x"], np.float32)
    ncs = [build_phase1(), build_phase2(), build_phase3(), build_phase4()]
    preps = [prep_phase1_inputs, prep_phase2_inputs, prep_phase3_inputs,
             prep_phase4_inputs]
    cur = [np.ascontiguousarray(x[c].T) for c in range(8)]
    for nc, prep in zip(ncs, preps):
        in_maps = [prep(cur[c], inputs, fm=True) for c in range(8)]
        res = run_bass_kernel_spmd(nc, in_maps, core_ids=list(range(8)))
        cur = [np.asarray(r["x_out"], np.float32) for r in res.results]
    return np.stack([c.T for c in cur])
